# version 11
# baseline (speedup 1.0000x reference)
import contextlib
import numpy as np
import concourse.bass as bass
import concourse.mybir as mybir
from concourse.bass_utils import run_bass_kernel_spmd

F32 = mybir.dt.float32
BF16 = mybir.dt.bfloat16
FP8 = mybir.dt.float8e4
AF = mybir.ActivationFunctionType
ALU = mybir.AluOpType
AX = mybir.AxisListType

B_TOT, T, D = 16, 4096, 1024
E = 2048
S = 16
R = 64
Tc = 512
EB = E // 128
DBL = D // 128
SEQ_PER_CORE = 2
CH = T // Tc
NCH = SEQ_PER_CORE * CH
FP8_MAX = 448.0
LN_EPS = 1e-5


class Ch:
    def __init__(self, sem):
        self.sem = sem
        self.n = 0


class KB:

    ENGS = ("sync", "scalar", "vector", "tensor", "gpsimd")

    def __init__(self):
        self.ops = {e: [] for e in self.ENGS}

    def dma(self, eng, out, in_, ch, w=()):
        ch.n += 16
        t = ch.n
        waits = tuple((c.sem, v) for (c, v) in w if c is not None)
        sem = ch.sem

        def f(e, out=out, in_=in_, sem=sem, waits=waits):
            for (s, v) in waits:
                e.wait_ge(s, v)
            e.dma_start(out=out, in_=in_).then_inc(sem, 16)
        self.ops[eng].append(f)
        return (ch, t)

    def op(self, eng, fn, ch=None, w=()):
        t = None
        sem = None
        if ch is not None:
            ch.n += 1
            t = ch.n
            sem = ch.sem
        waits = tuple((c.sem, v) for (c, v) in w if c is not None)

        def f(e, fn=fn, sem=sem, waits=waits):
            for (s, v) in waits:
                e.wait_ge(s, v)
            i = fn(e)
            if sem is not None:
                i.then_inc(sem, 1)
        self.ops[eng].append(f)
        return (ch, t)


NONE_W = (None, 0)


def build_nc():
    nc = bass.Bass("TRN2", target_bir_lowering=False)

    x_fm = nc.dram_tensor("x_fm", [SEQ_PER_CORE, D, T], F32, kind="ExternalInput").ap()
    w1t = nc.dram_tensor("w1t", [D, E], F32, kind="ExternalInput").ap()
    wzt = nc.dram_tensor("wzt", [D, E], F32, kind="ExternalInput").ap()
    cwrows = nc.dram_tensor("cwrows", [4, E], F32, kind="ExternalInput").ap()
    cb = nc.dram_tensor("cb", [128, EB], F32, kind="ExternalInput").ap()
    wxt = nc.dram_tensor("wxt", [E, 96], F32, kind="ExternalInput").ap()
    wdtt = nc.dram_tensor("wdtt", [R, E], F32, kind="ExternalInput").ap()
    bdt = nc.dram_tensor("bdt", [128, EB], F32, kind="ExternalInput").ap()
    alog = nc.dram_tensor("alog", [128, EB * S], F32, kind="ExternalInput").ap()
    dskip = nc.dram_tensor("dskip", [128, EB], F32, kind="ExternalInput").ap()
    woutt = nc.dram_tensor("woutt", [E, D], F32, kind="ExternalInput").ap()
    fwt = nc.dram_tensor("fwt", [D, D], F32, kind="ExternalInput").ap()
    fb = nc.dram_tensor("fb", [128, DBL], F32, kind="ExternalInput").ap()
    out_fm = nc.dram_tensor("out_fm", [SEQ_PER_CORE, D, T], F32, kind="ExternalOutput").ap()

    wk_s = nc.dram_tensor("wk_s", [EB, 2, 128, DBL // 2, 4, 128], BF16, kind="Internal").ap()
    wz_s = nc.dram_tensor("wz_s", [EB, 128, DBL, 128], BF16, kind="Internal").ap()
    wo_s = nc.dram_tensor("wo_s", [EB, 128, D], BF16, kind="Internal").ap()
    u32_s = nc.dram_tensor("u32_s", [2, EB, 128, Tc], F32, kind="Internal").ap()
    sz_s = nc.dram_tensor("sz_s", [2, EB, 128, Tc], F32, kind="Internal").ap()
    yg_s = nc.dram_tensor("yg_s", [2, EB, 128, Tc], BF16, kind="Internal").ap()
    u16_s = nc.dram_tensor("u16_s", [2, EB, 128, Tc], BF16, kind="Internal").ap()
    ln1_s = nc.dram_tensor("ln1_s", [NCH, DBL, 128, Tc], F32, kind="Internal").ap()
    bc_st = nc.dram_tensor("bc_st", [2, 2, S * Tc], F32, kind="Internal").ap()
    c_il = nc.dram_tensor("c_il", [2, S * Tc], F32, kind="Internal").ap()
    pm_st = nc.dram_tensor("pm_st", [1, 128], F32, kind="Internal").ap()
    sc_st = nc.dram_tensor("sc_st", [1, 8], F32, kind="Internal").ap()
    cc_in = nc.dram_tensor("cc_in", [1, 1], F32, kind="Internal")
    cc_out = nc.dram_tensor("cc_out", [1, 1], F32, kind="Internal")

    k = KB()

    with contextlib.ExitStack() as stk:
        def sb(name, shape, dt=F32):
            return stk.enter_context(nc.sbuf_tensor(name, shape, dt)).ap()

        cb_sb = sb("cb_sb", [128, EB])
        ncb_sb = sb("ncb_sb", [128, EB])
        bdt_sb = sb("bdt_sb", [128, EB])
        dskip_sb = sb("dskip_sb", [128, EB])
        a_sb = sb("a_sb", [128, EB * S])
        fb_sb = sb("fb_sb", [128, DBL])
        ones_row = sb("ones_row", [1, 128])
        ones_col = sb("ones_col", [128, 1])
        wxt_sb = sb("wxt_sb", [128, EB * 96], BF16)
        wdt_sb = sb("wdt_sb", [64, E], BF16)
        xb = [sb(f"xb{p}", [128, DBL * (Tc + 3)], BF16) for p in range(2)]
        u32r = [sb(f"u32r{i}", [128, Tc]) for i in range(2)]
        ua32r = [sb(f"ua32r{i}", [128, Tc]) for i in range(2)]
        szar = [sb(f"szar{i}", [128, Tc]) for i in range(2)]
        u16r = [sb(f"u16r{i}", [128, Tc], BF16) for i in range(2)]
        dbc_ev = sb("dbc_ev", [96, Tc])
        cil_ev = sb("cil_ev", [128, 4 * S])
        dtlo16 = sb("dtlo16", [64, Tc], BF16)
        dt32r = [sb(f"dt32r{i}", [128, Tc]) for i in range(2)]
        tmpr = [sb(f"tmpr{i}", [128, Tc]) for i in range(2)]
        wtile = [sb(f"wt{i}", [128, Tc]) for i in range(2)]
        dar = [sb(f"dar{i}", [128, Tc]) for i in range(3 - 1)]
        betar = [sb(f"betar{i}", [128, 2 * Tc]) for i in range(2)]
        h_so = sb("h_so", [128, S * Tc])
        bb = sb("bb", [128, S * Tc])
        cbb = sb("cbb", [128, S * Tc])
        carry = sb("carry", [128, EB * S])
        szr = [sb(f"szr{i}", [128, Tc]) for i in range(2)]
        yg16r = [sb(f"yg16r{i}", [128, Tc], BF16) for i in range(2)]
        wkt = [sb(f"wkt{i}", [128, (DBL // 2) * 4 * 128], BF16) for i in range(2)]
        wzt_r = [sb(f"wztr{i}", [128, DBL * 128], BF16) for i in range(2)]
        wot = [sb(f"wot{i}", [128, 8 * 128], BF16) for i in range(2)]
        x32r = [sb(f"x32r{i}", [128, Tc]) for i in range(2)]
        r1 = sb("r1", [128, DBL * Tc])
        r1sqr = tmpr
        ln1r = dt32r
        amax_acc = sb("amax_acc", [128, 1])
        redt = sb("redt", [128, 1])
        pmrow = sb("pmrow", [1, 128])
        scal = sb("scal", [1, 8])
        scalb = sb("scalb", [128, 8])
        f8t = dar[0].bitcast(FP8)[:, 0:Tc]
        SR = [betar[0][0:1, 0:Tc], betar[0][0:1, Tc:2 * Tc],
              betar[1][0:1, 0:Tc]]

        w1f = h_so[:, 0:E]
        cwb = h_so[:, E:2 * E]
        wtmp16 = bb.bitcast(BF16)[:, 0:E]
        fw8 = bb.bitcast(BF16)[:, 0:DBL * D]
        fw83 = fw8.rearrange("p (b n) -> p b n", n=D)
        fxq3 = cbb.bitcast(BF16)[:, 0:DBL * Tc].rearrange("p (b n) -> p b n", n=Tc)
        ff3 = h_so[:, 0:DBL * Tc].rearrange("p (b n) -> p b n", n=Tc)

        psum = [stk.enter_context(nc.psum_tensor(f"pb{i}", [128, Tc], F32)).ap()
                for i in range(8)]

        sems = [stk.enter_context(nc.semaphore(f"sm{i}")) for i in range(48)]
        _it = iter(sems)

        def mkch():
            return Ch(next(_it))

        cPE, cACT, cDVE, cCC = mkch(), mkch(), mkch(), mkch()
        chP0L = mkch()
        chP0S = mkch()
        chXB = [mkch(), mkch()]
        chU16 = [mkch(), mkch()]
        chU16L = [mkch(), mkch()]
        chWK = [mkch(), mkch()]
        chWZ = [mkch(), mkch()]
        chWO = [mkch(), mkch()]
        chUS = [mkch(), mkch()]
        chZS = [mkch(), mkch()]
        chYS = [mkch(), mkch()]
        chUR = [mkch(), mkch()]
        chZR = [mkch(), mkch()]
        chYR = [mkch(), mkch()]
        chX32 = [mkch(), mkch()]
        chLN = [mkch(), mkch()]
        chBS = [mkch(), mkch()]
        chCS = [mkch(), mkch()]
        chBB = mkch()
        chCB = mkch()
        chSC = mkch()
        chFW = mkch()
        chST = [mkch(), mkch()]

        bank_w = [NONE_W] * 8
        bank_free = [NONE_W] * 8

        p0 = NONE_W
        for (dst, src) in ((cb_sb, cb), (bdt_sb, bdt), (dskip_sb, dskip),
                           (a_sb, alog), (fb_sb, fb)):
            p0 = k.dma("sync", dst, src, chP0L, w=[p0])
        t_wres = k.dma("gpsimd", wxt_sb.rearrange("p (b n) -> p b n", n=96),
                       wxt.rearrange("(b p) n -> p b n", p=128), chP0S)
        t_wres = k.dma("gpsimd", wdt_sb, wdtt, chP0S, w=[t_wres])
        t_aexp = k.op("scalar", lambda e: e.activation(out=a_sb, in_=a_sb,
                                                       func=AF.Exp, scale=1.0),
                      ch=cACT, w=[p0])
        k.op("vector", lambda e: e.tensor_scalar(out=a_sb, in0=a_sb, scalar1=-1.0,
                                                 scalar2=None, op0=ALU.mult),
             ch=cDVE, w=[t_aexp])
        k.op("vector", lambda e: e.tensor_scalar(out=ncb_sb, in0=cb_sb,
                                                 scalar1=-1.0, scalar2=None,
                                                 op0=ALU.mult), ch=cDVE, w=[p0])
        k.op("vector", lambda e: e.memset(ones_row, 1.0), ch=cDVE)
        k.op("vector", lambda e: e.memset(ones_col, 1.0), ch=cDVE)
        k.op("vector", lambda e: e.memset(carry, 0.0), ch=cDVE)
        k.op("vector", lambda e: e.memset(amax_acc, 0.0), ch=cDVE)
        k.op("vector", lambda e: e.memset(redt, 0.0), ch=cDVE)

        st_p0 = t_wres
        for db in range(DBL):
            p0 = k.dma("sync", w1f, w1t[db * 128:(db + 1) * 128, :], chP0L,
                       w=[p0, st_p0])
            for tap in range(4):
                p0 = k.dma("sync", cwb, cwrows[tap:tap + 1, :].broadcast_to([128, E]),
                           chP0L, w=[p0, (cDVE, cDVE.n)])
                tt = k.op("vector", lambda e: e.tensor_tensor(
                    out=wtmp16, in0=w1f, in1=cwb, op=ALU.mult),
                    ch=cDVE, w=[p0, st_p0])
                half, dbh = db // (DBL // 2), db % (DBL // 2)
                st_p0 = k.dma("gpsimd",
                              wk_s[:, half, :, dbh, tap, :].transpose([1, 0, 2]),
                              wtmp16.rearrange("p (b n) -> p b n", n=128), chP0S,
                              w=[tt, st_p0])
            p0 = k.dma("sync", w1f, wzt[db * 128:(db + 1) * 128, :], chP0L,
                       w=[p0, st_p0])
            tt2 = k.op("vector", lambda e: e.tensor_copy(out=wtmp16, in_=w1f),
                       ch=cDVE, w=[p0])
            st_p0 = k.dma("gpsimd", wz_s[:, :, db, :].transpose([1, 0, 2]),
                          wtmp16.rearrange("p (b n) -> p b n", n=128), chP0S,
                          w=[tt2, st_p0])
        for eb in range(EB):
            p0 = k.dma("sync", w1f[:, 0:D], woutt[eb * 128:(eb + 1) * 128, :],
                       chP0L, w=[p0, st_p0])
            tt = k.op("vector", lambda e: e.tensor_copy(out=wtmp16[:, 0:D],
                                                        in_=w1f[:, 0:D]),
                      ch=cDVE, w=[p0])
            st_p0 = k.dma("gpsimd", wo_s[eb], wtmp16[:, 0:D],
                          chP0S, w=[tt, st_p0])
        for db in range(DBL):
            p0 = k.dma("sync", w1f[:, 0:D], fwt[db * 128:(db + 1) * 128, :],
                       chP0L, w=[p0, st_p0, (cDVE, cDVE.n)])
            tr = k.op("vector", lambda e: e.tensor_reduce(
                out=redt, in_=w1f[:, 0:D], axis=AX.X, op=ALU.max,
                apply_absolute_value=True), ch=cDVE, w=[p0])
            if db == 0:
                k.op("vector", lambda e: e.tensor_copy(out=amax_acc, in_=redt),
                     ch=cDVE, w=[tr])
            else:
                k.op("vector", lambda e: e.tensor_tensor(
                    out=amax_acc, in0=amax_acc, in1=redt, op=ALU.max),
                    ch=cDVE, w=[tr])
        s1 = k.dma("sync", pm_st, amax_acc, chSC, w=[(cDVE, cDVE.n)])
        s2 = k.dma("sync", pmrow, pm_st, chSC, w=[s1])
        k.op("vector", lambda e: e.tensor_reduce(out=scal[0:1, 1:2], in_=pmrow,
                                                 axis=AX.X, op=ALU.max),
             ch=cDVE, w=[s2])
        k.op("vector", lambda e: e.memset(amax_acc, 0.0), ch=cDVE,
             w=[(cDVE, cDVE.n)])
        p0_all = (chP0S, chP0S.n)

        xb_guard = [NONE_W, NONE_W]
        u32_free = [NONE_W, NONE_W]
        u16_free = [NONE_W, NONE_W]
        dt_free = [NONE_W, NONE_W]
        tmp_free = [NONE_W, NONE_W]
        w_free = [NONE_W, NONE_W]
        dar_free = [NONE_W] * 2
        beta_free = [NONE_W, NONE_W]
        sz_free = [NONE_W, NONE_W]
        yg_free = [NONE_W, NONE_W]
        wkt_free = [p0_all, p0_all]
        wzt_free = [p0_all, p0_all]
        wot_free = [p0_all, p0_all]
        x32_free = [NONE_W, NONE_W]
        sq_free = [NONE_W, NONE_W]
        ln1_free = [NONE_W, NONE_W]
        sp_done_u = [[NONE_W] * EB, [NONE_W] * EB]
        sp16_done = [[NONE_W] * EB, [NONE_W] * EB]
        sp_done_z = [[NONE_W] * EB, [NONE_W] * EB]
        sp_done_y = [[NONE_W] * EB, [NONE_W] * EB]
        ln1_st = [[NONE_W] * DBL for _ in range(NCH)]
        bc_guard = [NONE_W, NONE_W]
        bc_rdB = [NONE_W, NONE_W]
        cil_guard = [NONE_W]
        bc_rdC = [NONE_W, NONE_W]
        dbc_guard = NONE_W
        h_guard = NONE_W
        r13 = r1.rearrange("p (b n) -> p b n", n=Tc)

        a_t_x = [NONE_W, NONE_W]

        def emit_A_begin(c):
            par = c % 2
            seq, j = c // CH, c % CH
            t0 = j * Tc
            xw3 = xb[par].rearrange("p (b n) -> p b n", n=Tc + 3)
            if j == 0:
                tz = k.op("gpsimd", lambda e, xw3=xw3: e.memset(xw3[:, :, 0:3], 0.0),
                          ch=cCC, w=[xb_guard[par]])
                a_t_x[par] = k.dma("gpsimd", xw3[:, :, 3:Tc + 3],
                                   x_fm[seq].rearrange("(b p) t -> p b t", p=128)[:, :, t0:t0 + Tc],
                                   chXB[par], w=[tz])
            else:
                a_t_x[par] = k.dma("gpsimd", xw3[:, :, 0:Tc + 3],
                                   x_fm[seq].rearrange("(b p) t -> p b t", p=128)[:, :, t0 - 3:t0 + Tc],
                                   chXB[par], w=[xb_guard[par]])

        def emit_A_iter(c, eb):
            par = c % 2
            ep = eb % 2
            bk = ep
            bz = 2 + ep
            t_x = a_t_x[par]
            xw3 = xb[par].rearrange("p (b n) -> p b n", n=Tc + 3)
            t_wk = [None, None]
            for half in range(2):
                t_wk[half] = k.dma("sync", wkt[half],
                                   wk_s[eb, half].rearrange("p a b n -> p (a b n)"),
                                   chWK[half], w=[wkt_free[half]])
            t_wz = k.dma("sync", wzt_r[ep], wz_s[eb].rearrange("p b n -> p (b n)"),
                         chWZ[ep], w=[wzt_free[ep]])

            def mm_xc(e, bk=bk, t_wk=tuple(t_wk), t_x=t_x,
                      bf=bank_free[bk], xw3=xw3):
                e.wait_ge(t_x[0].sem, t_x[1])
                if bf[0]:
                    e.wait_ge(bf[0].sem, bf[1])
                last = None
                for half in range(2):
                    e.wait_ge(t_wk[half][0].sem, t_wk[half][1])
                    w4 = wkt[half].rearrange("p (a b n) -> p a b n",
                                             a=DBL // 2, b=4)
                    for dbh in range(DBL // 2):
                        db = half * (DBL // 2) + dbh
                        for tap in range(4):
                            last = e.matmul(
                                psum[bk], w4[:, dbh, tap, :],
                                xw3[:, db, tap:tap + Tc],
                                start=(half == 0 and dbh == 0 and tap == 0),
                                stop=(half == 1 and dbh == DBL // 2 - 1 and tap == 3))
                return last
            mw = k.op("tensor", mm_xc, ch=cPE)
            bank_w[bk] = mw
            wkt_free[0] = wkt_free[1] = mw
            t_s1 = k.op("scalar", lambda e, bk=bk, eb=eb, ur=ua32r[ep]: e.activation(
                out=ur, in_=psum[bk], func=AF.Exp, bias=ncb_sb[:, eb:eb + 1],
                scale=-1.0), ch=cACT,
                w=[mw, ua_free[ep], ua_us_prev[ep], sp_done_u[par][eb]])
            k.op("scalar", lambda e, ur=ua32r[ep]: e.activation(
                out=ur, in_=ur, func=AF.Ln, bias=1.0, scale=1.0), ch=cACT)
            t_s4 = k.op("scalar", lambda e, ur=ua32r[ep]: e.activation(
                out=ur, in_=ur, func=AF.Exp, scale=-1.0), ch=cACT)
            t_u = k.op("vector", lambda e, bk=bk, eb=eb, ur=ua32r[ep]:
                       e.scalar_tensor_tensor(
                out=ur, in0=psum[bk], scalar=cb_sb[:, eb:eb + 1], in1=ur,
                op0=ALU.add, op1=ALU.mult), ch=cDVE, w=[t_s4])
            bank_free[bk] = t_u
            sp_done_u[par][eb] = k.dma("gpsimd", u32_s[par, eb], ua32r[ep],
                                       chUS[ep], w=[t_u])
            sp16_done[par][eb] = k.dma("gpsimd", u16_s[par, eb], ua32r[ep],
                                       chU16[ep], w=[t_u, sp16_done[par][eb]])
            ua_us_prev[ep] = sp_done_u[par][eb]
            ua_free[ep] = sp16_done[par][eb]

            def mm_z(e, bz=bz, ep=ep, t_wz=t_wz, t_x=t_x,
                     bf=bank_free[bz], xw3=xw3):
                e.wait_ge(t_x[0].sem, t_x[1])
                e.wait_ge(t_wz[0].sem, t_wz[1])
                if bf[0]:
                    e.wait_ge(bf[0].sem, bf[1])
                last = None
                wz3 = wzt_r[ep].rearrange("p (a n) -> p a n", a=DBL)
                for db in range(DBL):
                    last = e.matmul(psum[bz], wz3[:, db, :], xw3[:, db, 3:Tc + 3],
                                    start=(db == 0), stop=(db == DBL - 1))
                return last
            mz = k.op("tensor", mm_z, ch=cPE)
            bank_w[bz] = mz
            wzt_free[ep] = mz
            k.op("scalar", lambda e, bz=bz, sr=szar[ep]: e.activation(
                out=sr, in_=psum[bz], func=AF.Exp, scale=-1.0), ch=cACT,
                w=[mz, sza_free[ep], sp_done_z[par][eb]])
            k.op("scalar", lambda e, sr=szar[ep]: e.activation(
                out=sr, in_=sr, func=AF.Ln, bias=1.0, scale=1.0), ch=cACT)
            t_z4 = k.op("scalar", lambda e, sr=szar[ep]: e.activation(
                out=sr, in_=sr, func=AF.Exp, scale=-1.0), ch=cACT)
            t_z = k.op("vector", lambda e, bz=bz, sr=szar[ep]: e.tensor_tensor(
                out=sr, in0=psum[bz], in1=sr, op=ALU.mult), ch=cDVE, w=[t_z4])
            bank_free[bz] = t_z
            sp_done_z[par][eb] = k.dma("gpsimd", sz_s[par, eb], szar[ep],
                                       chZS[ep], w=[t_z])
            sza_free[ep] = sp_done_z[par][eb]
            if eb == EB - 1:
                xb_guard[par] = (cPE, cPE.n)

        def emit_B_mm(c, eb):
            par = c % 2
            ep = eb % 2
            tl = k.dma("sync", u16r[ep], u16_s[par, eb], chU16L[ep],
                       w=[u16_free[ep], sp16_done[par][eb]])

            def mm_one(e, eb=eb, ep=ep, tl=tl, bf=bank_free[6],
                       bf7=bank_free[7]):
                e.wait_ge(tl[0].sem, tl[1])
                if eb == 0:
                    if bf[0]:
                        e.wait_ge(bf[0].sem, bf[1])
                    if bf7[0]:
                        e.wait_ge(bf7[0].sem, bf7[1])
                e.matmul(psum[6][0:96, :],
                         wxt_sb.rearrange("p (b n) -> p b n", n=96)[:, eb, :],
                         u16r[ep], start=(eb == 0), stop=(eb == EB - 1))
                last = None
                for tb in range(4):
                    last = e.matmul(
                        psum[7][:, tb * 96:(tb + 1) * 96],
                        u16r[ep][:, tb * 128:(tb + 1) * 128],
                        wxt_sb.rearrange("p (b n) -> p b n", n=96)[:, eb, :],
                        start=(eb == 0), stop=(eb == EB - 1))
                return last
            mo = k.op("tensor", mm_one, ch=cPE)
            bank_w[6] = bank_w[7] = mo
            u16_free[ep] = mo

        def emit_B_tail(c):
            nonlocal dbc_guard
            par = c % 2
            dve_now = (cDVE, cDVE.n)
            t_lo = k.op("scalar", lambda e: e.copy(out=dtlo16,
                                                   in_=psum[6][0:64, :]),
                        ch=cACT, w=[bank_w[6]])
            t_bc = k.op("scalar", lambda e: e.copy(out=dbc_ev[64:80, :],
                                                   in_=psum[6][64:80, :]),
                        ch=cACT, w=[t_lo, dbc_guard])
            bank_free[6] = t_bc
            t_ce = k.op("scalar", lambda e: e.copy(
                out=cil_ev.rearrange("p (a b) -> p a b", b=S),
                in_=psum[7][:, 0:384].rearrange("p (a b) -> p a b", b=96)[:, :, 80:96]),
                ch=cACT, w=[cil_guard[0]])
            bank_free[7] = t_ce
            t_sb = k.dma("gpsimd", bc_st[par, 0].rearrange("(a n) -> a n", n=Tc),
                         dbc_ev[64:80, :], chBS[par], w=[t_bc, bc_rdB[par]])
            t_sc = k.dma("gpsimd",
                         c_il[par].rearrange("(tb p s) -> p tb s", p=128, s=S),
                         cil_ev.rearrange("p (a b) -> p a b", b=S),
                         chCS[par], w=[t_ce, bc_rdC[par]])
            t_bb = k.dma("sync", bb, bc_st[par, 0].unsqueeze(0)
                         .broadcast_to([128, S * Tc]), chBB,
                         w=[t_sb, dve_now])
            t_cb = k.dma("sync", cbb, c_il[par].unsqueeze(0)
                         .broadcast_to([128, S * Tc]), chCB,
                         w=[t_sc, dve_now])
            bc_rdB[par] = t_bb
            bc_rdC[par] = t_cb
            cil_guard[0] = t_sc
            pend["t_lo"] = t_lo
            pend["t_bb"] = t_bb
            pend["t_cb"] = t_cb
            dbc_guard = t_sb

        c_e3 = [NONE_W, NONE_W]

        def emit_C_chain(c, eb):
            par = c % 2
            ep = eb % 2
            bd = 5 - ep
            t_lo = pend["t_lo"]

            def mm_dt(e, bd=bd, eb=eb, t_lo=t_lo, bf=bank_free[bd]):
                e.wait_ge(t_lo[0].sem, t_lo[1])
                if bf[0]:
                    e.wait_ge(bf[0].sem, bf[1])
                return e.matmul(psum[bd], wdt_sb[:, eb * 128:(eb + 1) * 128],
                                dtlo16, start=True, stop=True)
            md = k.op("tensor", mm_dt, ch=cPE)
            bank_w[bd] = md
            t_e1 = k.op("scalar", lambda e, bd=bd, eb=eb, tm=tmpr[ep]: e.activation(
                out=tm, in_=psum[bd], func=AF.Exp, bias=bdt_sb[:, eb:eb + 1],
                scale=1.0), ch=cACT, w=[md, tmp_free[ep]])
            bank_free[bd] = t_e1
            t_e3 = k.op("scalar", lambda e, tm=tmpr[ep], dt=dt32r[ep]: e.activation(
                out=dt, in_=tm, func=AF.Ln, bias=1.0, scale=1.0), ch=cACT,
                w=[dt_free[ep]])
            tmp_free[ep] = t_e3
            c_e3[ep] = t_e3

        c_w = [NONE_W, NONE_W]

        def emit_w(c, eb):
            par = c % 2
            ep = eb % 2
            t_u32 = k.dma("sync", u32r[ep], u32_s[par, eb], chUR[ep],
                          w=[u32_free[ep], sp_done_u[par][eb]])
            c_w[ep] = k.op("vector", lambda e, dt=dt32r[ep], ur=u32r[ep],
                           wt=wtile[ep], tu=t_u32: (
                e.wait_ge(tu[0].sem, tu[1]),
                e.tensor_tensor(out=wt, in0=dt, in1=ur, op=ALU.mult))[1],
                ch=cDVE, w=[c_e3[ep], w_free[ep]])

        def emit_beta(c, eb, pr):
            ep = eb % 2
            bb3 = bb.rearrange("p (s n) -> p s n", n=Tc)
            t_bb = pend["t_bb"]
            k.op("vector", lambda e, pr=pr, wt=wtile[ep],
                 slot=pr % 2, t_bb=t_bb, bb3=bb3: (
                e.wait_ge(t_bb[0].sem, t_bb[1]),
                e.tensor_tensor(
                    out=betar[slot].rearrange("p (a n) -> p a n", n=Tc),
                    in0=wt.unsqueeze(1).broadcast_to([128, 2, Tc]),
                    in1=bb3[:, 2 * pr:2 * pr + 2, :],
                    op=ALU.mult))[1],
                ch=cDVE, w=[c_w[ep], beta_free[pr % 2]])

        def emit_C_iter(c, eb):
            par = c % 2
            ep = eb % 2
            t_bb, t_cb = pend["t_bb"], pend["t_cb"]
            bb3 = bb.rearrange("p (s n) -> p s n", n=Tc)
            t_e3 = c_e3[ep]
            if pend.get("wpre") != (c, eb):
                emit_w(c, eb)
                emit_beta(c, eb, 0)
                emit_beta(c, eb, 1)
            t_scan = NONE_W
            for s_i in range(S):
                sp_ = s_i % 2
                pr = s_i // 2
                if sp_ == 0 and pr >= 2:
                    emit_beta(c, eb, pr)
                t_da = k.op("scalar", lambda e, s_i=s_i, eb=eb, dt=dt32r[ep],
                            d=dar[s_i % 2]: e.activation(
                    out=d, in_=dt, func=AF.Exp,
                    scale=a_sb[:, eb * S + s_i:eb * S + s_i + 1]),
                    ch=cACT, w=[t_e3, dar_free[s_i % 2]])
                t_scan = k.op("vector", lambda e, s_i=s_i, eb=eb, sp_=sp_,
                              pr=pr, d=dar[s_i % 2]: e.tensor_tensor_scan(
                    out=h_so.rearrange("p (n s) -> p n s", s=S)[:, :, s_i],
                    data0=d,
                    data1=betar[pr % 2][:, sp_ * Tc:(sp_ + 1) * Tc],
                    initial=carry[:, eb * S + s_i:eb * S + s_i + 1],
                    op0=ALU.mult, op1=ALU.add),
                    ch=cDVE, w=[t_da] + ([h_guard[0]] if s_i == 0 else []))
                dar_free[s_i % 2] = t_scan
                if sp_ == 1:
                    beta_free[pr % 2] = t_scan
            dt_free[ep] = t_scan
            w_free[ep] = t_scan
            t_cu = k.op("vector", lambda e, eb=eb: e.tensor_copy(
                out=carry[:, eb * S:(eb + 1) * S],
                in_=h_so[:, (Tc - 1) * S:Tc * S]),
                ch=cDVE, w=[t_scan])
            nxt = eb + 1 < EB
            t_p = k.op("vector", lambda e, t_cb=t_cb: (
                e.wait_ge(t_cb[0].sem, t_cb[1]),
                e.tensor_tensor(out=h_so, in0=h_so, in1=cbb, op=ALU.mult))[1],
                ch=cDVE, w=[t_cu])
            if nxt:
                emit_w(c, eb + 1)
                emit_beta(c, eb + 1, 0)
            t_p = k.op("vector", lambda e: e.tensor_reduce(
                out=h_so[:, 0:Tc], in_=h_so.rearrange("p (n s) -> p n s", s=S),
                axis=AX.X, op=ALU.add), ch=cDVE, w=[t_p])
            if nxt:
                emit_beta(c, eb + 1, 1)
            if nxt:
                pend["wpre"] = (c, eb + 1)
            t_yt = k.op("vector", lambda e, eb=eb, ur=u32r[ep]:
                        e.scalar_tensor_tensor(
                out=ur, in0=ur, scalar=dskip_sb[:, eb:eb + 1],
                in1=h_so[:, 0:Tc], op0=ALU.mult, op1=ALU.add),
                ch=cDVE, w=[t_p])
            h_guard[0] = t_yt
            t_sz = k.dma("sync", szr[ep], sz_s[par, eb], chZR[ep],
                         w=[sz_free[ep], sp_done_z[par][eb]])
            t_yg = k.op("vector", lambda e, yt=u32r[ep], sr=szr[ep],
                        yg=yg16r[ep], ts=t_sz: (
                e.wait_ge(ts[0].sem, ts[1]),
                e.tensor_tensor(out=yg, in0=yt, in1=sr, op=ALU.mult))[1],
                ch=cDVE, w=[t_yt, yg_free[ep], sp_done_y[par][eb]])
            u32_free[ep] = t_yg
            sz_free[ep] = t_yg
            sp_done_y[par][eb] = k.dma("gpsimd", yg_s[par, eb], yg16r[ep],
                                       chYS[ep], w=[t_yg])
            yg_free[ep] = sp_done_y[par][eb]

        def emit_D(c):
            par = c % 2
            seq, j = c // CH, c % CH
            t0 = j * Tc
            for eb in range(EB):
                ep = eb % 2
                t_yl = k.dma("sync", yg16r[ep], yg_s[par, eb], chYR[ep],
                             w=[yg_free[ep], sp_done_y[par][eb]])
                t_wo = k.dma("sync", wot[ep], wo_s[eb], chWO[ep],
                             w=[wot_free[ep]])

                def mm_out(e, eb=eb, ep=ep, t_yl=t_yl, t_wo=t_wo,
                           bfs=tuple(bank_free[0:8])):
                    e.wait_ge(t_yl[0].sem, t_yl[1])
                    e.wait_ge(t_wo[0].sem, t_wo[1])
                    if eb == 0:
                        for bf in bfs:
                            if bf[0]:
                                e.wait_ge(bf[0].sem, bf[1])
                    last = None
                    wo3 = wot[ep].rearrange("p (a n) -> p a n", a=8)
                    for dq in range(8):
                        last = e.matmul(psum[dq], wo3[:, dq, :], yg16r[ep],
                                        start=(eb == 0), stop=(eb == EB - 1))
                    return last
                t_mo = k.op("tensor", mm_out, ch=cPE)
                wot_free[ep] = t_mo
                yg_free[ep] = t_mo
            t_proj = (cPE, cPE.n)
            for db in range(DBL):
                dp = db % 2
                t_x32 = k.dma("sync", x32r[dp],
                              x_fm[seq].rearrange("(b p) t -> p b t", p=128)[:, db, t0:t0 + Tc],
                              chX32[dp], w=[x32_free[dp]])
                t_r1 = k.op("vector", lambda e, db=db, dp=dp, tx=t_x32: (
                    e.wait_ge(tx[0].sem, tx[1]),
                    e.tensor_tensor(out=r13[:, db, :], in0=psum[db],
                                    in1=x32r[dp], op=ALU.add))[1],
                    ch=cDVE, w=[t_proj])
                bank_free[db] = t_r1
                x32_free[dp] = t_r1
            t_r1_all = (cDVE, cDVE.n)
            for db in range(DBL):
                dp = db % 2
                t_sq = k.op("scalar", lambda e, db=db, dp=dp: e.activation(
                    out=r1sqr[dp], in_=r13[:, db, :], func=AF.Square, scale=1.0),
                    ch=cACT, w=[t_r1_all, sq_free[dp]])

                def mm_stat(e, db=db, dp=dp, t_sq=t_sq,
                            bf0=bank_free[4], bf1=bank_free[5]):
                    e.wait_ge(t_sq[0].sem, t_sq[1])
                    if db == 0:
                        e.wait_ge(bf0[0].sem, bf0[1])
                        e.wait_ge(bf1[0].sem, bf1[1])
                    e.matmul(psum[4][0:1, :], ones_col, r13[:, db, :],
                             start=(db == 0), stop=(db == DBL - 1))
                    return e.matmul(psum[5][0:1, :], ones_col, r1sqr[dp],
                                    start=(db == 0), stop=(db == DBL - 1))
                ms = k.op("tensor", mm_stat, ch=cPE)
                bank_w[4] = bank_w[5] = ms
                sq_free[dp] = ms
            t_m = k.op("vector", lambda e: e.tensor_scalar(
                out=SR[0], in0=psum[4][0:1, :], scalar1=1.0 / D,
                scalar2=None, op0=ALU.mult), ch=cDVE, w=[bank_w[4], beta_free[0]])
            t_mq = k.op("vector", lambda e: e.tensor_scalar(
                out=SR[1], in0=psum[5][0:1, :], scalar1=1.0 / D,
                scalar2=None, op0=ALU.mult), ch=cDVE)
            bank_free[4] = t_m
            bank_free[5] = t_mq
            t_v = k.op("vector", lambda e: e.tensor_tensor(
                out=SR[2], in0=SR[0], in1=SR[0], op=ALU.mult),
                ch=cDVE, w=[t_mq, beta_free[1]])
            t_v2 = k.op("vector", lambda e: e.scalar_tensor_tensor(
                out=SR[1], in0=SR[2], scalar=-1.0, in1=SR[1],
                op0=ALU.mult, op1=ALU.add), ch=cDVE, w=[t_v])
            t_v3 = k.op("vector", lambda e: e.tensor_scalar(
                out=SR[1], in0=SR[1], scalar1=LN_EPS, scalar2=None,
                op0=ALU.add), ch=cDVE, w=[t_v2])
            t_l = k.op("scalar", lambda e: e.activation(
                out=SR[2], in_=SR[1], func=AF.Ln, scale=1.0),
                ch=cACT, w=[t_v3])
            t_rs = k.op("scalar", lambda e: e.activation(
                out=SR[2], in_=SR[2], func=AF.Exp, scale=-0.5),
                ch=cACT, w=[t_l])

            def mm_bc(e, t_m=t_m, t_rs=t_rs, bf2=bank_free[6], bf3=bank_free[7]):
                e.wait_ge(t_m[0].sem, t_m[1])
                e.wait_ge(t_rs[0].sem, t_rs[1])
                if bf2[0]:
                    e.wait_ge(bf2[0].sem, bf2[1])
                if bf3[0]:
                    e.wait_ge(bf3[0].sem, bf3[1])
                e.matmul(psum[6], ones_row, SR[0], start=True, stop=True)
                return e.matmul(psum[7], ones_row, SR[2], start=True, stop=True)
            mb = k.op("tensor", mm_bc, ch=cPE)
            bank_w[6] = bank_w[7] = mb
            for db in range(DBL):
                dp = db % 2
                t_a = k.op("vector", lambda e, db=db, dp=dp: e.tensor_tensor(
                    out=ln1r[dp], in0=r13[:, db, :], in1=psum[6], op=ALU.subtract),
                    ch=cDVE, w=[mb, ln1_free[dp]])
                t_b = k.op("vector", lambda e, dp=dp: e.tensor_tensor(
                    out=ln1r[dp], in0=ln1r[dp], in1=psum[7], op=ALU.mult),
                    ch=cDVE, w=[t_a])
                t_rd = k.op("vector", lambda e, dp=dp: e.tensor_reduce(
                    out=redt, in_=ln1r[dp], axis=AX.X, op=ALU.max,
                    apply_absolute_value=True), ch=cDVE, w=[t_b])
                k.op("vector", lambda e: e.tensor_tensor(
                    out=amax_acc, in0=amax_acc, in1=redt, op=ALU.max),
                    ch=cDVE, w=[t_rd])
                ln1_st[c][db] = k.dma("gpsimd", ln1_s[c, db], ln1r[dp], chLN[dp],
                                      w=[t_b])
                ln1_free[dp] = ln1_st[c][db]
            t_ln_last = (cDVE, cDVE.n)
            bank_free[6] = t_ln_last
            bank_free[7] = t_ln_last
            beta_free[0] = mb
            beta_free[1] = mb
            dt_free[0] = ln1_free[0]
            dt_free[1] = ln1_free[1]
            tmp_free[0] = sq_free[0]
            tmp_free[1] = sq_free[1]

        pend = {}
        ua_free = [NONE_W, NONE_W]
        ua_us_prev = [NONE_W, NONE_W]
        sza_free = [NONE_W, NONE_W]
        h_guard = [NONE_W]
        emit_A_begin(0)
        for eb in range(EB):
            emit_A_iter(0, eb)
        emit_A_begin(1)
        for eb in range(EB):
            emit_B_mm(0, eb)
        emit_B_tail(0)
        for c in range(NCH):
            if c % CH == 0:
                k.op("vector", lambda e: e.memset(carry, 0.0), ch=cDVE,
                     w=[(cDVE, cDVE.n)])
            emit_C_chain(c, 0)
            for eb in range(EB):
                if eb + 1 < EB:
                    emit_C_chain(c, eb + 1)
                emit_C_iter(c, eb)
                if c + 1 < NCH:
                    emit_A_iter(c + 1, eb)
                    emit_B_mm(c + 1, eb)
            if c + 1 < NCH:
                emit_B_tail(c + 1)
                if c + 2 < NCH:
                    emit_A_begin(c + 2)
            emit_D(c)
        h_guard = h_guard[0]

        s1 = k.dma("sync", pm_st, amax_acc, chSC, w=[(cDVE, cDVE.n), s2])
        s2 = k.dma("sync", pmrow, pm_st, chSC, w=[s1])
        t3 = k.op("vector", lambda e: e.tensor_reduce(
            out=scal[0:1, 0:1], in_=pmrow, axis=AX.X, op=ALU.max),
            ch=cDVE, w=[s2])
        s3 = k.dma("sync", cc_in.ap(), scal[0:1, 0:1], chSC, w=[t3, s2])
        t5 = k.op("gpsimd", lambda e: e.collective_compute(
            "AllReduce", ALU.max, ins=[cc_in.ap()], outs=[cc_out.ap()],
            replica_groups=[list(range(8))]), ch=cCC, w=[s3])
        s4 = k.dma("sync", scal[0:1, 0:1], cc_out.ap(), chSC, w=[t5, s3])
        t7 = k.op("vector", lambda e: e.reciprocal(
            out=scal[0:1, 5:7], in_=scal[0:1, 0:2]), ch=cDVE, w=[s4])
        t8 = k.op("vector", lambda e: e.tensor_scalar(
            out=scal[0:1, 2:4], in0=scal[0:1, 5:7], scalar1=FP8_MAX,
            scalar2=None, op0=ALU.mult), ch=cDVE, w=[t7])
        t9 = k.op("vector", lambda e: e.tensor_scalar(
            out=scal[0:1, 2:4], in0=scal[0:1, 2:4], scalar1=0.5, scalar2=None,
            op0=ALU.mult), ch=cDVE, w=[t8])
        t10 = k.op("vector", lambda e: e.tensor_tensor(
            out=scal[0:1, 6:7], in0=scal[0:1, 2:3], in1=scal[0:1, 3:4],
            op=ALU.mult), ch=cDVE, w=[t9])
        t12 = k.op("vector", lambda e: e.reciprocal(
            out=scal[0:1, 4:5], in_=scal[0:1, 6:7]), ch=cDVE, w=[t10])
        s5 = k.dma("sync", sc_st, scal, chSC, w=[t12, s4])
        s6 = k.dma("sync", scalb, sc_st.broadcast_to([128, 8]), chSC, w=[s5])

        fwp = NONE_W
        for db in range(DBL):
            fwp = k.dma("sync", w1f[:, 0:D], fwt[db * 128:(db + 1) * 128, :],
                        chFW, w=[fwp, s6, h_guard, (cDVE, cDVE.n)])
            for hf in range(2):
                tb = k.op("vector", lambda e, hf=hf, ta=fwp: (
                    e.wait_ge(ta[0].sem, ta[1]),
                    e.tensor_scalar(out=f8t, in0=w1f[:, hf * Tc:(hf + 1) * Tc],
                                    scalar1=scalb[:, 3:4], scalar2=None,
                                    op0=ALU.mult))[1], ch=cDVE, w=[s6])
                k.op("vector", lambda e, db=db, hf=hf: e.tensor_copy(
                    out=fw83[:, db, hf * Tc:(hf + 1) * Tc], in_=f8t),
                    ch=cDVE, w=[tb])
        t_fw = (cDVE, cDVE.n)
        f8t_free = NONE_W

        for c in range(NCH):
            seq, j = c // CH, c % CH
            t0 = j * Tc
            for db in range(DBL):
                dp = db % 2
                ta = k.dma("sync", x32r[dp], ln1_s[c, db], chX32[dp],
                           w=[x32_free[dp], ln1_st[c][db],
                              t_fw if c == 0 and db < 2 else NONE_W])
                tb = k.op("vector", lambda e, dp=dp, ta=ta: (
                    e.wait_ge(ta[0].sem, ta[1]),
                    e.tensor_scalar(out=f8t, in0=x32r[dp], scalar1=scalb[:, 2:3],
                                    scalar2=None, op0=ALU.mult))[1], ch=cDVE,
                    w=[t_fw, f8t_free])
                tc2 = k.op("scalar", lambda e, db=db: e.copy(
                    out=fxq3[:, db, :], in_=f8t), ch=cACT, w=[tb])
                td = k.op("scalar", lambda e, db=db, dp=dp: e.copy(
                    out=r13[:, db, :], in_=x32r[dp]), ch=cACT, w=[tb])
                f8t_free = tc2
                x32_free[dp] = td
            t_q = (cACT, cACT.n)
            t_qd = (cDVE, cDVE.n)
            for db in range(DBL):
                dp = db % 2
                bo = 6 + dp

                def mm_ffn(e, db=db, bo=bo, t_q=t_q, t_qd=t_qd, bf=bank_free[bo]):
                    e.wait_ge(t_q[0].sem, t_q[1])
                    e.wait_ge(t_qd[0].sem, t_qd[1])
                    if bf[0]:
                        e.wait_ge(bf[0].sem, bf[1])
                    last = None
                    for dk in range(DBL):
                        last = e.matmul(psum[bo], fw83[:, dk, db * 128:(db + 1) * 128],
                                        fxq3[:, dk, :], start=(dk == 0),
                                        stop=(dk == DBL - 1))
                    return last
                mf = k.op("tensor", mm_ffn, ch=cPE)
                bank_w[bo] = mf
                t_e = k.op("vector", lambda e, bo=bo, db=db, dp=dp: e.tensor_scalar(
                    out=r1sqr[dp], in0=psum[bo], scalar1=scalb[:, 4:5],
                    scalar2=fb_sb[:, db:db + 1], op0=ALU.mult, op1=ALU.add),
                    ch=cDVE, w=[mf, sq_free[dp]])
                bank_free[bo] = t_e
                t_r2 = k.op("vector", lambda e, db=db, dp=dp: e.tensor_tensor(
                    out=ff3[:, db, :], in0=r13[:, db, :], in1=r1sqr[dp],
                    op=ALU.add), ch=cDVE, w=[t_e])
                sq_free[dp] = t_r2
            t_ff_all = (cDVE, cDVE.n)
            for db in range(DBL):
                dp = db % 2
                t_sq = k.op("scalar", lambda e, db=db, dp=dp: e.activation(
                    out=r1sqr[dp], in_=ff3[:, db, :], func=AF.Square, scale=1.0),
                    ch=cACT, w=[t_ff_all, sq_free[dp]])

                def mm_stat2(e, db=db, dp=dp, t_sq=t_sq,
                             bf0=bank_free[0], bf1=bank_free[1]):
                    e.wait_ge(t_sq[0].sem, t_sq[1])
                    if db == 0:
                        e.wait_ge(bf0[0].sem, bf0[1])
                        e.wait_ge(bf1[0].sem, bf1[1])
                    e.matmul(psum[0][0:1, :], ones_col, ff3[:, db, :],
                             start=(db == 0), stop=(db == DBL - 1))
                    return e.matmul(psum[1][0:1, :], ones_col, r1sqr[dp],
                                    start=(db == 0), stop=(db == DBL - 1))
                ms = k.op("tensor", mm_stat2, ch=cPE)
                bank_w[0] = bank_w[1] = ms
                sq_free[dp] = ms
            t_m = k.op("vector", lambda e: e.tensor_scalar(
                out=SR[0], in0=psum[0][0:1, :], scalar1=1.0 / D,
                scalar2=None, op0=ALU.mult), ch=cDVE, w=[bank_w[0]])
            t_mq = k.op("vector", lambda e: e.tensor_scalar(
                out=SR[1], in0=psum[1][0:1, :], scalar1=1.0 / D,
                scalar2=None, op0=ALU.mult), ch=cDVE)
            bank_free[0] = t_m
            bank_free[1] = t_mq
            t_v = k.op("vector", lambda e: e.tensor_tensor(
                out=SR[2], in0=SR[0], in1=SR[0], op=ALU.mult),
                ch=cDVE, w=[t_mq])
            t_v2 = k.op("vector", lambda e: e.scalar_tensor_tensor(
                out=SR[1], in0=SR[2], scalar=-1.0, in1=SR[1],
                op0=ALU.mult, op1=ALU.add), ch=cDVE, w=[t_v])
            t_v3 = k.op("vector", lambda e: e.tensor_scalar(
                out=SR[1], in0=SR[1], scalar1=LN_EPS, scalar2=None,
                op0=ALU.add), ch=cDVE, w=[t_v2])
            t_l = k.op("scalar", lambda e: e.activation(
                out=SR[2], in_=SR[1], func=AF.Ln, scale=1.0),
                ch=cACT, w=[t_v3])
            t_rs = k.op("scalar", lambda e: e.activation(
                out=SR[2], in_=SR[2], func=AF.Exp, scale=-0.5),
                ch=cACT, w=[t_l])

            def mm_bc2(e, t_m=t_m, t_rs=t_rs, bf2=bank_free[2], bf3=bank_free[3]):
                e.wait_ge(t_m[0].sem, t_m[1])
                e.wait_ge(t_rs[0].sem, t_rs[1])
                e.wait_ge(bf2[0].sem, bf2[1])
                e.wait_ge(bf3[0].sem, bf3[1])
                e.matmul(psum[2], ones_row, SR[0], start=True, stop=True)
                return e.matmul(psum[3], ones_row, SR[2], start=True, stop=True)
            mb = k.op("tensor", mm_bc2, ch=cPE)
            bank_w[2] = bank_w[3] = mb
            for db in range(DBL):
                dp = db % 2
                t_a = k.op("vector", lambda e, db=db, dp=dp: e.tensor_tensor(
                    out=ln1r[dp], in0=ff3[:, db, :], in1=psum[2], op=ALU.subtract),
                    ch=cDVE, w=[mb, ln1_free[dp]])
                t_b = k.op("vector", lambda e, dp=dp: e.tensor_tensor(
                    out=ln1r[dp], in0=ln1r[dp], in1=psum[3], op=ALU.mult),
                    ch=cDVE, w=[t_a])
                ln1_free[dp] = k.dma(
                    "gpsimd",
                    out_fm[seq].rearrange("(b p) t -> p b t", p=128)[:, db, t0:t0 + Tc],
                    ln1r[dp], chST[dp], w=[t_b])
            t_last = (cDVE, cDVE.n)
            bank_free[2] = t_last
            bank_free[3] = t_last

        k.ops["sync"].append(lambda e, a=chST[0].n, b=chST[1].n: (
            e.wait_ge(chST[0].sem, a), e.wait_ge(chST[1].sem, b)))

        with nc.Block() as block:
            @block.sync
            def _(e):
                for f in k.ops["sync"]:
                    f(e)

            @block.scalar
            def _(e):
                for f in k.ops["scalar"]:
                    f(e)

            @block.vector
            def _(e):
                for f in k.ops["vector"]:
                    f(e)

            @block.tensor
            def _(e):
                for f in k.ops["tensor"]:
                    f(e)

            @block.gpsimd
            def _(e):
                for f in k.ops["gpsimd"]:
                    f(e)

    return nc


def kernel(L0_attn_x, L0_attn_W_in, L0_attn_conv_w, L0_attn_conv_b,
           L0_attn_W_xproj, L0_attn_W_dt, L0_attn_b_dt, L0_attn_A_log,
           L0_attn_D_skip, L0_attn_W_out, L0_ffn_weight, L0_ffn_bias):
    x = np.asarray(L0_attn_x, dtype=np.float32)
    w_in = np.asarray(L0_attn_W_in, dtype=np.float32)
    conv_w = np.asarray(L0_attn_conv_w, dtype=np.float32)
    conv_b = np.asarray(L0_attn_conv_b, dtype=np.float32)
    w_xproj = np.asarray(L0_attn_W_xproj, dtype=np.float32)
    w_dt = np.asarray(L0_attn_W_dt, dtype=np.float32)
    b_dt = np.asarray(L0_attn_b_dt, dtype=np.float32)
    a_log = np.asarray(L0_attn_A_log, dtype=np.float32)
    d_skip = np.asarray(L0_attn_D_skip, dtype=np.float32)
    w_out = np.asarray(L0_attn_W_out, dtype=np.float32)
    fw = np.asarray(L0_ffn_weight, dtype=np.float32)
    fbias = np.asarray(L0_ffn_bias, dtype=np.float32)

    w_inT = np.ascontiguousarray(w_in.T)
    shared = {
        "w1t": np.ascontiguousarray(w_inT[:, :E]),
        "wzt": np.ascontiguousarray(w_inT[:, E:]),
        "cwrows": np.ascontiguousarray(conv_w.T),
        "cb": np.ascontiguousarray(conv_b.reshape(EB, 128).T),
        "wxt": np.ascontiguousarray(w_xproj.T),
        "wdtt": np.ascontiguousarray(w_dt.T),
        "bdt": np.ascontiguousarray(b_dt.reshape(EB, 128).T),
        "alog": np.ascontiguousarray(
            a_log.reshape(EB, 128, S).transpose(1, 0, 2).reshape(128, EB * S)),
        "dskip": np.ascontiguousarray(d_skip.reshape(EB, 128).T),
        "woutt": np.ascontiguousarray(w_out.T),
        "fwt": np.ascontiguousarray(fw.T),
        "fb": np.ascontiguousarray(fbias.reshape(DBL, 128).T),
    }
    in_maps = []
    for c in range(8):
        xc = np.ascontiguousarray(x[2 * c:2 * c + 2].transpose(0, 2, 1))
        m = {"x_fm": xc}
        m.update(shared)
        in_maps.append(m)

    nc = build_nc()
    res = run_bass_kernel_spmd(nc, in_maps, core_ids=list(range(8)))
    out = np.empty((B_TOT, T, D), dtype=np.float32)
    for c in range(8):
        o = np.asarray(res.results[c]["out_fm"])
        out[2 * c:2 * c + 2] = o.transpose(0, 2, 1)
    return out


# version 12
# speedup vs baseline: 1.0724x; 1.0724x over previous
import contextlib
import numpy as np
import concourse.bass as bass
import concourse.mybir as mybir
from concourse.bass_utils import run_bass_kernel_spmd

F32 = mybir.dt.float32
BF16 = mybir.dt.bfloat16
FP8 = mybir.dt.float8e4
AF = mybir.ActivationFunctionType
ALU = mybir.AluOpType
AX = mybir.AxisListType

B_TOT, T, D = 16, 4096, 1024
E = 2048
S = 16
R = 64
Tc = 512
EB = E // 128
DBL = D // 128
SEQ_PER_CORE = 2
CH = T // Tc
NCH = SEQ_PER_CORE * CH
FP8_MAX = 448.0
LN_EPS = 1e-5


class Ch:
    def __init__(self, sem):
        self.sem = sem
        self.n = 0


class KB:

    ENGS = ("sync", "scalar", "vector", "tensor", "gpsimd")

    def __init__(self):
        self.ops = {e: [] for e in self.ENGS}

    def dma(self, eng, out, in_, ch, w=()):
        ch.n += 16
        t = ch.n
        waits = tuple((c.sem, v) for (c, v) in w if c is not None)
        sem = ch.sem

        def f(e, out=out, in_=in_, sem=sem, waits=waits):
            for (s, v) in waits:
                e.wait_ge(s, v)
            e.dma_start(out=out, in_=in_).then_inc(sem, 16)
        self.ops[eng].append(f)
        return (ch, t)

    def op(self, eng, fn, ch=None, w=()):
        t = None
        sem = None
        if ch is not None:
            ch.n += 1
            t = ch.n
            sem = ch.sem
        waits = tuple((c.sem, v) for (c, v) in w if c is not None)

        def f(e, fn=fn, sem=sem, waits=waits):
            for (s, v) in waits:
                e.wait_ge(s, v)
            i = fn(e)
            if sem is not None:
                i.then_inc(sem, 1)
        self.ops[eng].append(f)
        return (ch, t)


NONE_W = (None, 0)


def build_nc():
    nc = bass.Bass("TRN2", target_bir_lowering=False)

    x_fm = nc.dram_tensor("x_fm", [SEQ_PER_CORE, D, T], F32, kind="ExternalInput").ap()
    w1t = nc.dram_tensor("w1t", [D, E], F32, kind="ExternalInput").ap()
    wzt = nc.dram_tensor("wzt", [D, E], F32, kind="ExternalInput").ap()
    cwrows = nc.dram_tensor("cwrows", [4, E], F32, kind="ExternalInput").ap()
    cb = nc.dram_tensor("cb", [128, EB], F32, kind="ExternalInput").ap()
    wxt = nc.dram_tensor("wxt", [E, 96], F32, kind="ExternalInput").ap()
    wdtt = nc.dram_tensor("wdtt", [R, E], F32, kind="ExternalInput").ap()
    bdt = nc.dram_tensor("bdt", [128, EB], F32, kind="ExternalInput").ap()
    alog = nc.dram_tensor("alog", [128, EB * S], F32, kind="ExternalInput").ap()
    dskip = nc.dram_tensor("dskip", [128, EB], F32, kind="ExternalInput").ap()
    woutt = nc.dram_tensor("woutt", [E, D], F32, kind="ExternalInput").ap()
    fwt = nc.dram_tensor("fwt", [D, D], F32, kind="ExternalInput").ap()
    fb = nc.dram_tensor("fb", [128, DBL], F32, kind="ExternalInput").ap()
    out_fm = nc.dram_tensor("out_fm", [SEQ_PER_CORE, D, T], F32, kind="ExternalOutput").ap()

    wk_s = nc.dram_tensor("wk_s", [EB, 2, 128, DBL // 2, 4, 128], BF16, kind="Internal").ap()
    wz_s = nc.dram_tensor("wz_s", [EB, 128, DBL, 128], BF16, kind="Internal").ap()
    wo_s = nc.dram_tensor("wo_s", [EB, 128, D], BF16, kind="Internal").ap()
    u32_s = nc.dram_tensor("u32_s", [2, EB, 128, Tc], F32, kind="Internal").ap()
    sz_s = nc.dram_tensor("sz_s", [2, EB, 128, Tc], F32, kind="Internal").ap()
    yg_s = nc.dram_tensor("yg_s", [2, EB, 128, Tc], BF16, kind="Internal").ap()
    ln1_s = nc.dram_tensor("ln1_s", [NCH, DBL, 128, Tc], F32, kind="Internal").ap()
    bc_st = nc.dram_tensor("bc_st", [2, 2, S * Tc], F32, kind="Internal").ap()
    c_il = nc.dram_tensor("c_il", [2, S * Tc], F32, kind="Internal").ap()
    pm_st = nc.dram_tensor("pm_st", [1, 128], F32, kind="Internal").ap()
    sc_st = nc.dram_tensor("sc_st", [1, 8], F32, kind="Internal").ap()
    cc_in = nc.dram_tensor("cc_in", [1, 1], F32, kind="Internal")
    cc_out = nc.dram_tensor("cc_out", [1, 1], F32, kind="Internal")

    k = KB()

    with contextlib.ExitStack() as stk:
        def sb(name, shape, dt=F32):
            return stk.enter_context(nc.sbuf_tensor(name, shape, dt)).ap()

        cb_sb = sb("cb_sb", [128, EB])
        ncb_sb = sb("ncb_sb", [128, EB])
        bdt_sb = sb("bdt_sb", [128, EB])
        dskip_sb = sb("dskip_sb", [128, EB])
        a_sb = sb("a_sb", [128, EB * S])
        fb_sb = sb("fb_sb", [128, DBL])
        ones_row = sb("ones_row", [1, 128])
        ones_col = sb("ones_col", [128, 1])
        wxt_sb = sb("wxt_sb", [128, EB * 96], BF16)
        wdt_sb = sb("wdt_sb", [64, E], BF16)
        xb = [sb(f"xb{p}", [128, DBL * (Tc + 3)], BF16) for p in range(2)]
        u32r = [sb(f"u32r{i}", [128, Tc]) for i in range(2)]
        ua32r = [sb(f"ua32r{i}", [128, Tc]) for i in range(2)]
        szar = [sb(f"szar{i}", [128, Tc]) for i in range(2)]
        u16r = [sb(f"u16r{i}", [128, Tc], BF16) for i in range(2)]
        dbc_ev = sb("dbc_ev", [96, Tc])
        cil_ev = sb("cil_ev", [128, 4 * S])
        dtlo16 = sb("dtlo16", [64, Tc], BF16)
        dt32r = [sb(f"dt32r{i}", [128, Tc]) for i in range(2)]
        tmpr = [sb(f"tmpr{i}", [128, Tc]) for i in range(2)]
        wtile = [sb(f"wt{i}", [128, Tc]) for i in range(2)]
        dar = [sb(f"dar{i}", [128, Tc]) for i in range(3 - 1)]
        betar = [sb(f"betar{i}", [128, 2 * Tc]) for i in range(2)]
        h_so = sb("h_so", [128, S * Tc])
        bb = sb("bb", [128, S * Tc])
        cbb = sb("cbb", [128, S * Tc])
        carry = sb("carry", [128, EB * S])
        szr = [sb(f"szr{i}", [128, Tc]) for i in range(2)]
        yg16r = [sb(f"yg16r{i}", [128, Tc], BF16) for i in range(2)]
        wkt = [sb(f"wkt{i}", [128, (DBL // 2) * 4 * 128], BF16) for i in range(2)]
        wzt_r = [sb(f"wztr{i}", [128, DBL * 128], BF16) for i in range(2)]
        wot = [sb(f"wot{i}", [128, 8 * 128], BF16) for i in range(2)]
        x32r = [sb(f"x32r{i}", [128, Tc]) for i in range(2)]
        r1 = sb("r1", [128, DBL * Tc])
        r1sqr = tmpr
        ln1r = dt32r
        amax_acc = sb("amax_acc", [128, 1])
        redt = sb("redt", [128, 1])
        pmrow = sb("pmrow", [1, 128])
        scal = sb("scal", [1, 8])
        scalb = sb("scalb", [128, 8])
        f8t = dar[0].bitcast(FP8)[:, 0:Tc]
        SR = [betar[0][0:1, 0:Tc], betar[0][0:1, Tc:2 * Tc],
              betar[1][0:1, 0:Tc]]

        w1f = h_so[:, 0:E]
        cwb = h_so[:, E:2 * E]
        wtmp16 = bb.bitcast(BF16)[:, 0:E]
        fw8 = bb.bitcast(BF16)[:, 0:DBL * D]
        fw83 = fw8.rearrange("p (b n) -> p b n", n=D)
        fxq3 = cbb.bitcast(BF16)[:, 0:DBL * Tc].rearrange("p (b n) -> p b n", n=Tc)
        ff3 = h_so[:, 0:DBL * Tc].rearrange("p (b n) -> p b n", n=Tc)

        psum = [stk.enter_context(nc.psum_tensor(f"pb{i}", [128, Tc], F32)).ap()
                for i in range(8)]

        sems = [stk.enter_context(nc.semaphore(f"sm{i}")) for i in range(48)]
        _it = iter(sems)

        def mkch():
            return Ch(next(_it))

        cPE, cACT, cDVE, cCC = mkch(), mkch(), mkch(), mkch()
        chP0L = mkch()
        chP0S = mkch()
        chXB = [mkch(), mkch()]
        chWK = [mkch(), mkch()]
        chWZ = [mkch(), mkch()]
        chWO = [mkch(), mkch()]
        chUS = [mkch(), mkch()]
        chZS = [mkch(), mkch()]
        chYS = [mkch(), mkch()]
        chUR = [mkch(), mkch()]
        chZR = [mkch(), mkch()]
        chYR = [mkch(), mkch()]
        chX32 = [mkch(), mkch()]
        chLN = [mkch(), mkch()]
        chBS = [mkch(), mkch()]
        chCS = [mkch(), mkch()]
        chBB = mkch()
        chCB = mkch()
        chSC = mkch()
        chFW = mkch()
        chST = [mkch(), mkch()]

        bank_w = [NONE_W] * 8
        bank_free = [NONE_W] * 8

        p0 = NONE_W
        for (dst, src) in ((cb_sb, cb), (bdt_sb, bdt), (dskip_sb, dskip),
                           (a_sb, alog), (fb_sb, fb)):
            p0 = k.dma("sync", dst, src, chP0L, w=[p0])
        t_wres = k.dma("gpsimd", wxt_sb.rearrange("p (b n) -> p b n", n=96),
                       wxt.rearrange("(b p) n -> p b n", p=128), chP0S)
        t_wres = k.dma("gpsimd", wdt_sb, wdtt, chP0S, w=[t_wres])
        t_aexp = k.op("scalar", lambda e: e.activation(out=a_sb, in_=a_sb,
                                                       func=AF.Exp, scale=1.0),
                      ch=cACT, w=[p0])
        k.op("vector", lambda e: e.tensor_scalar(out=a_sb, in0=a_sb, scalar1=-1.0,
                                                 scalar2=None, op0=ALU.mult),
             ch=cDVE, w=[t_aexp])
        k.op("vector", lambda e: e.tensor_scalar(out=ncb_sb, in0=cb_sb,
                                                 scalar1=-1.0, scalar2=None,
                                                 op0=ALU.mult), ch=cDVE, w=[p0])
        k.op("vector", lambda e: e.memset(ones_row, 1.0), ch=cDVE)
        k.op("vector", lambda e: e.memset(ones_col, 1.0), ch=cDVE)
        k.op("vector", lambda e: e.memset(carry, 0.0), ch=cDVE)
        k.op("vector", lambda e: e.memset(amax_acc, 0.0), ch=cDVE)
        k.op("vector", lambda e: e.memset(redt, 0.0), ch=cDVE)

        st_p0 = t_wres
        for db in range(DBL):
            p0 = k.dma("sync", w1f, w1t[db * 128:(db + 1) * 128, :], chP0L,
                       w=[p0, st_p0])
            for tap in range(4):
                p0 = k.dma("sync", cwb, cwrows[tap:tap + 1, :].broadcast_to([128, E]),
                           chP0L, w=[p0, (cDVE, cDVE.n)])
                tt = k.op("vector", lambda e: e.tensor_tensor(
                    out=wtmp16, in0=w1f, in1=cwb, op=ALU.mult),
                    ch=cDVE, w=[p0, st_p0])
                half, dbh = db // (DBL // 2), db % (DBL // 2)
                st_p0 = k.dma("gpsimd",
                              wk_s[:, half, :, dbh, tap, :].transpose([1, 0, 2]),
                              wtmp16.rearrange("p (b n) -> p b n", n=128), chP0S,
                              w=[tt, st_p0])
            p0 = k.dma("sync", w1f, wzt[db * 128:(db + 1) * 128, :], chP0L,
                       w=[p0, st_p0])
            tt2 = k.op("vector", lambda e: e.tensor_copy(out=wtmp16, in_=w1f),
                       ch=cDVE, w=[p0])
            st_p0 = k.dma("gpsimd", wz_s[:, :, db, :].transpose([1, 0, 2]),
                          wtmp16.rearrange("p (b n) -> p b n", n=128), chP0S,
                          w=[tt2, st_p0])
        for eb in range(EB):
            p0 = k.dma("sync", w1f[:, 0:D], woutt[eb * 128:(eb + 1) * 128, :],
                       chP0L, w=[p0, st_p0])
            tt = k.op("vector", lambda e: e.tensor_copy(out=wtmp16[:, 0:D],
                                                        in_=w1f[:, 0:D]),
                      ch=cDVE, w=[p0])
            st_p0 = k.dma("gpsimd", wo_s[eb], wtmp16[:, 0:D],
                          chP0S, w=[tt, st_p0])
        for db in range(DBL):
            p0 = k.dma("sync", w1f[:, 0:D], fwt[db * 128:(db + 1) * 128, :],
                       chP0L, w=[p0, st_p0, (cDVE, cDVE.n)])
            tr = k.op("vector", lambda e: e.tensor_reduce(
                out=redt, in_=w1f[:, 0:D], axis=AX.X, op=ALU.max,
                apply_absolute_value=True), ch=cDVE, w=[p0])
            if db == 0:
                k.op("vector", lambda e: e.tensor_copy(out=amax_acc, in_=redt),
                     ch=cDVE, w=[tr])
            else:
                k.op("vector", lambda e: e.tensor_tensor(
                    out=amax_acc, in0=amax_acc, in1=redt, op=ALU.max),
                    ch=cDVE, w=[tr])
        s1 = k.dma("sync", pm_st, amax_acc, chSC, w=[(cDVE, cDVE.n)])
        s2 = k.dma("sync", pmrow, pm_st, chSC, w=[s1])
        k.op("vector", lambda e: e.tensor_reduce(out=scal[0:1, 1:2], in_=pmrow,
                                                 axis=AX.X, op=ALU.max),
             ch=cDVE, w=[s2])
        k.op("vector", lambda e: e.memset(amax_acc, 0.0), ch=cDVE,
             w=[(cDVE, cDVE.n)])
        p0_all = (chP0S, chP0S.n)

        xb_guard = [NONE_W, NONE_W]
        u32_free = [NONE_W, NONE_W]
        u16_free = [NONE_W, NONE_W]
        dt_free = [NONE_W, NONE_W]
        tmp_free = [NONE_W, NONE_W]
        w_free = [NONE_W, NONE_W]
        dar_free = [NONE_W] * 2
        beta_free = [NONE_W, NONE_W]
        sz_free = [NONE_W, NONE_W]
        yg_free = [NONE_W, NONE_W]
        wkt_free = [p0_all, p0_all]
        wzt_free = [p0_all, p0_all]
        wot_free = [p0_all, p0_all]
        x32_free = [NONE_W, NONE_W]
        sq_free = [NONE_W, NONE_W]
        ln1_free = [NONE_W, NONE_W]
        sp_done_u = [[NONE_W] * EB, [NONE_W] * EB]
        u16cast = [NONE_W, NONE_W]
        sp_done_z = [[NONE_W] * EB, [NONE_W] * EB]
        sp_done_y = [[NONE_W] * EB, [NONE_W] * EB]
        ln1_st = [[NONE_W] * DBL for _ in range(NCH)]
        bc_guard = [NONE_W, NONE_W]
        bc_rdB = [NONE_W, NONE_W]
        cil_guard = [NONE_W]
        bc_rdC = [NONE_W, NONE_W]
        dbc_guard = NONE_W
        h_guard = NONE_W
        r13 = r1.rearrange("p (b n) -> p b n", n=Tc)

        a_t_x = [NONE_W, NONE_W]

        def emit_A_begin(c):
            par = c % 2
            seq, j = c // CH, c % CH
            t0 = j * Tc
            xw3 = xb[par].rearrange("p (b n) -> p b n", n=Tc + 3)
            if j == 0:
                tz = k.op("gpsimd", lambda e, xw3=xw3: e.memset(xw3[:, :, 0:3], 0.0),
                          ch=cCC, w=[xb_guard[par]])
                a_t_x[par] = k.dma("gpsimd", xw3[:, :, 3:Tc + 3],
                                   x_fm[seq].rearrange("(b p) t -> p b t", p=128)[:, :, t0:t0 + Tc],
                                   chXB[par], w=[tz])
            else:
                a_t_x[par] = k.dma("gpsimd", xw3[:, :, 0:Tc + 3],
                                   x_fm[seq].rearrange("(b p) t -> p b t", p=128)[:, :, t0 - 3:t0 + Tc],
                                   chXB[par], w=[xb_guard[par]])

        def emit_A_iter(c, eb):
            par = c % 2
            ep = eb % 2
            bk = ep
            bz = 2 + ep
            t_x = a_t_x[par]
            xw3 = xb[par].rearrange("p (b n) -> p b n", n=Tc + 3)
            t_wk = [None, None]
            for half in range(2):
                t_wk[half] = k.dma("sync", wkt[half],
                                   wk_s[eb, half].rearrange("p a b n -> p (a b n)"),
                                   chWK[half], w=[wkt_free[half]])
            t_wz = k.dma("sync", wzt_r[ep], wz_s[eb].rearrange("p b n -> p (b n)"),
                         chWZ[ep], w=[wzt_free[ep]])

            def mm_xc(e, bk=bk, t_wk=tuple(t_wk), t_x=t_x,
                      bf=bank_free[bk], xw3=xw3):
                e.wait_ge(t_x[0].sem, t_x[1])
                if bf[0]:
                    e.wait_ge(bf[0].sem, bf[1])
                last = None
                for half in range(2):
                    e.wait_ge(t_wk[half][0].sem, t_wk[half][1])
                    w4 = wkt[half].rearrange("p (a b n) -> p a b n",
                                             a=DBL // 2, b=4)
                    for dbh in range(DBL // 2):
                        db = half * (DBL // 2) + dbh
                        for tap in range(4):
                            last = e.matmul(
                                psum[bk], w4[:, dbh, tap, :],
                                xw3[:, db, tap:tap + Tc],
                                start=(half == 0 and dbh == 0 and tap == 0),
                                stop=(half == 1 and dbh == DBL // 2 - 1 and tap == 3))
                return last
            mw = k.op("tensor", mm_xc, ch=cPE)
            bank_w[bk] = mw
            wkt_free[0] = wkt_free[1] = mw
            t_s1 = k.op("scalar", lambda e, bk=bk, eb=eb, ur=ua32r[ep]: e.activation(
                out=ur, in_=psum[bk], func=AF.Exp, bias=ncb_sb[:, eb:eb + 1],
                scale=-1.0), ch=cACT,
                w=[mw, ua_free[ep], ua_us_prev[ep], sp_done_u[par][eb]])
            k.op("scalar", lambda e, ur=ua32r[ep]: e.activation(
                out=ur, in_=ur, func=AF.Ln, bias=1.0, scale=1.0), ch=cACT)
            t_s4 = k.op("scalar", lambda e, ur=ua32r[ep]: e.activation(
                out=ur, in_=ur, func=AF.Exp, scale=-1.0), ch=cACT)
            t_u = k.op("vector", lambda e, bk=bk, eb=eb, ur=ua32r[ep]:
                       e.scalar_tensor_tensor(
                out=ur, in0=psum[bk], scalar=cb_sb[:, eb:eb + 1], in1=ur,
                op0=ALU.add, op1=ALU.mult), ch=cDVE, w=[t_s4])
            bank_free[bk] = t_u
            sp_done_u[par][eb] = k.dma("gpsimd", u32_s[par, eb], ua32r[ep],
                                       chUS[ep], w=[t_u])
            u16cast[ep] = k.op("scalar", lambda e, ep=ep: e.copy(
                out=u16r[ep], in_=ua32r[ep]), ch=cACT,
                w=[t_u, u16_free[ep]])
            ua_us_prev[ep] = sp_done_u[par][eb]
            ua_free[ep] = u16cast[ep]

            def mm_z(e, bz=bz, ep=ep, t_wz=t_wz, t_x=t_x,
                     bf=bank_free[bz], xw3=xw3):
                e.wait_ge(t_x[0].sem, t_x[1])
                e.wait_ge(t_wz[0].sem, t_wz[1])
                if bf[0]:
                    e.wait_ge(bf[0].sem, bf[1])
                last = None
                wz3 = wzt_r[ep].rearrange("p (a n) -> p a n", a=DBL)
                for db in range(DBL):
                    last = e.matmul(psum[bz], wz3[:, db, :], xw3[:, db, 3:Tc + 3],
                                    start=(db == 0), stop=(db == DBL - 1))
                return last
            mz = k.op("tensor", mm_z, ch=cPE)
            bank_w[bz] = mz
            wzt_free[ep] = mz
            k.op("scalar", lambda e, bz=bz, sr=szar[ep]: e.activation(
                out=sr, in_=psum[bz], func=AF.Exp, scale=-1.0), ch=cACT,
                w=[mz, sza_free[ep], sp_done_z[par][eb]])
            k.op("scalar", lambda e, sr=szar[ep]: e.activation(
                out=sr, in_=sr, func=AF.Ln, bias=1.0, scale=1.0), ch=cACT)
            t_z4 = k.op("scalar", lambda e, sr=szar[ep]: e.activation(
                out=sr, in_=sr, func=AF.Exp, scale=-1.0), ch=cACT)
            t_z = k.op("vector", lambda e, bz=bz, sr=szar[ep]: e.tensor_tensor(
                out=sr, in0=psum[bz], in1=sr, op=ALU.mult), ch=cDVE, w=[t_z4])
            bank_free[bz] = t_z
            sp_done_z[par][eb] = k.dma("gpsimd", sz_s[par, eb], szar[ep],
                                       chZS[ep], w=[t_z])
            sza_free[ep] = sp_done_z[par][eb]
            if eb == EB - 1:
                xb_guard[par] = (cPE, cPE.n)

        def emit_B_mm(c, eb):
            par = c % 2
            ep = eb % 2
            tl = u16cast[ep]

            def mm_one(e, eb=eb, ep=ep, tl=tl, bf=bank_free[6],
                       bf7=bank_free[7]):
                e.wait_ge(tl[0].sem, tl[1])
                if eb == 0:
                    if bf[0]:
                        e.wait_ge(bf[0].sem, bf[1])
                    if bf7[0]:
                        e.wait_ge(bf7[0].sem, bf7[1])
                e.matmul(psum[6][0:96, :],
                         wxt_sb.rearrange("p (b n) -> p b n", n=96)[:, eb, :],
                         u16r[ep], start=(eb == 0), stop=(eb == EB - 1))
                last = None
                for tb in range(4):
                    last = e.matmul(
                        psum[7][:, tb * 96:(tb + 1) * 96],
                        u16r[ep][:, tb * 128:(tb + 1) * 128],
                        wxt_sb.rearrange("p (b n) -> p b n", n=96)[:, eb, :],
                        start=(eb == 0), stop=(eb == EB - 1))
                return last
            mo = k.op("tensor", mm_one, ch=cPE)
            bank_w[6] = bank_w[7] = mo
            u16_free[ep] = mo

        def emit_B_tail(c):
            nonlocal dbc_guard
            par = c % 2
            dve_now = (cDVE, cDVE.n)
            t_lo = k.op("scalar", lambda e: e.copy(out=dtlo16,
                                                   in_=psum[6][0:64, :]),
                        ch=cACT, w=[bank_w[6]])
            t_bc = k.op("scalar", lambda e: e.copy(out=dbc_ev[64:80, :],
                                                   in_=psum[6][64:80, :]),
                        ch=cACT, w=[t_lo, dbc_guard])
            bank_free[6] = t_bc
            t_ce = k.op("scalar", lambda e: e.copy(
                out=cil_ev.rearrange("p (a b) -> p a b", b=S),
                in_=psum[7][:, 0:384].rearrange("p (a b) -> p a b", b=96)[:, :, 80:96]),
                ch=cACT, w=[cil_guard[0]])
            bank_free[7] = t_ce
            t_sb = k.dma("gpsimd", bc_st[par, 0].rearrange("(a n) -> a n", n=Tc),
                         dbc_ev[64:80, :], chBS[par], w=[t_bc, bc_rdB[par]])
            t_sc = k.dma("gpsimd",
                         c_il[par].rearrange("(tb p s) -> p tb s", p=128, s=S),
                         cil_ev.rearrange("p (a b) -> p a b", b=S),
                         chCS[par], w=[t_ce, bc_rdC[par]])
            t_bb = k.dma("sync", bb, bc_st[par, 0].unsqueeze(0)
                         .broadcast_to([128, S * Tc]), chBB,
                         w=[t_sb, dve_now])
            t_cb = k.dma("sync", cbb, c_il[par].unsqueeze(0)
                         .broadcast_to([128, S * Tc]), chCB,
                         w=[t_sc, dve_now])
            bc_rdB[par] = t_bb
            bc_rdC[par] = t_cb
            cil_guard[0] = t_sc
            pend["t_lo"] = t_lo
            pend["t_bb"] = t_bb
            pend["t_cb"] = t_cb
            dbc_guard = t_sb

        c_e3 = [NONE_W, NONE_W]

        def emit_C_chain(c, eb):
            par = c % 2
            ep = eb % 2
            bd = 5 - ep
            t_lo = pend["t_lo"]

            def mm_dt(e, bd=bd, eb=eb, t_lo=t_lo, bf=bank_free[bd]):
                e.wait_ge(t_lo[0].sem, t_lo[1])
                if bf[0]:
                    e.wait_ge(bf[0].sem, bf[1])
                return e.matmul(psum[bd], wdt_sb[:, eb * 128:(eb + 1) * 128],
                                dtlo16, start=True, stop=True)
            md = k.op("tensor", mm_dt, ch=cPE)
            bank_w[bd] = md
            t_e1 = k.op("scalar", lambda e, bd=bd, eb=eb, tm=tmpr[ep]: e.activation(
                out=tm, in_=psum[bd], func=AF.Exp, bias=bdt_sb[:, eb:eb + 1],
                scale=1.0), ch=cACT, w=[md, tmp_free[ep]])
            bank_free[bd] = t_e1
            t_e3 = k.op("scalar", lambda e, tm=tmpr[ep], dt=dt32r[ep]: e.activation(
                out=dt, in_=tm, func=AF.Ln, bias=1.0, scale=1.0), ch=cACT,
                w=[dt_free[ep]])
            tmp_free[ep] = t_e3
            c_e3[ep] = t_e3

        c_w = [NONE_W, NONE_W]

        def emit_w(c, eb):
            par = c % 2
            ep = eb % 2
            t_u32 = k.dma("sync", u32r[ep], u32_s[par, eb], chUR[ep],
                          w=[u32_free[ep], sp_done_u[par][eb]])
            c_w[ep] = k.op("vector", lambda e, dt=dt32r[ep], ur=u32r[ep],
                           wt=wtile[ep], tu=t_u32: (
                e.wait_ge(tu[0].sem, tu[1]),
                e.tensor_tensor(out=wt, in0=dt, in1=ur, op=ALU.mult))[1],
                ch=cDVE, w=[c_e3[ep], w_free[ep]])

        def emit_beta(c, eb, pr):
            ep = eb % 2
            bb3 = bb.rearrange("p (s n) -> p s n", n=Tc)
            t_bb = pend["t_bb"]
            k.op("vector", lambda e, pr=pr, wt=wtile[ep],
                 slot=pr % 2, t_bb=t_bb, bb3=bb3: (
                e.wait_ge(t_bb[0].sem, t_bb[1]),
                e.tensor_tensor(
                    out=betar[slot].rearrange("p (a n) -> p a n", n=Tc),
                    in0=wt.unsqueeze(1).broadcast_to([128, 2, Tc]),
                    in1=bb3[:, 2 * pr:2 * pr + 2, :],
                    op=ALU.mult))[1],
                ch=cDVE, w=[c_w[ep], beta_free[pr % 2]])

        def emit_C_iter(c, eb):
            par = c % 2
            ep = eb % 2
            t_bb, t_cb = pend["t_bb"], pend["t_cb"]
            bb3 = bb.rearrange("p (s n) -> p s n", n=Tc)
            t_e3 = c_e3[ep]
            if pend.get("wpre") != (c, eb):
                emit_w(c, eb)
                emit_beta(c, eb, 0)
                emit_beta(c, eb, 1)
            t_scan = NONE_W
            for s_i in range(S):
                sp_ = s_i % 2
                pr = s_i // 2
                if sp_ == 0 and pr >= 2:
                    emit_beta(c, eb, pr)
                t_da = k.op("scalar", lambda e, s_i=s_i, eb=eb, dt=dt32r[ep],
                            d=dar[s_i % 2]: e.activation(
                    out=d, in_=dt, func=AF.Exp,
                    scale=a_sb[:, eb * S + s_i:eb * S + s_i + 1]),
                    ch=cACT, w=[t_e3, dar_free[s_i % 2]])
                t_scan = k.op("vector", lambda e, s_i=s_i, eb=eb, sp_=sp_,
                              pr=pr, d=dar[s_i % 2]: e.tensor_tensor_scan(
                    out=h_so.rearrange("p (n s) -> p n s", s=S)[:, :, s_i],
                    data0=d,
                    data1=betar[pr % 2][:, sp_ * Tc:(sp_ + 1) * Tc],
                    initial=carry[:, eb * S + s_i:eb * S + s_i + 1],
                    op0=ALU.mult, op1=ALU.add),
                    ch=cDVE, w=[t_da] + ([h_guard[0]] if s_i == 0 else []))
                dar_free[s_i % 2] = t_scan
                if sp_ == 1:
                    beta_free[pr % 2] = t_scan
            dt_free[ep] = t_scan
            w_free[ep] = t_scan
            t_cu = k.op("vector", lambda e, eb=eb: e.tensor_copy(
                out=carry[:, eb * S:(eb + 1) * S],
                in_=h_so[:, (Tc - 1) * S:Tc * S]),
                ch=cDVE, w=[t_scan])
            nxt = eb + 1 < EB
            t_p = k.op("vector", lambda e, t_cb=t_cb: (
                e.wait_ge(t_cb[0].sem, t_cb[1]),
                e.tensor_tensor(out=h_so, in0=h_so, in1=cbb, op=ALU.mult))[1],
                ch=cDVE, w=[t_cu])
            if nxt:
                emit_w(c, eb + 1)
                emit_beta(c, eb + 1, 0)
            t_p = k.op("vector", lambda e: e.tensor_reduce(
                out=h_so[:, 0:Tc], in_=h_so.rearrange("p (n s) -> p n s", s=S),
                axis=AX.X, op=ALU.add), ch=cDVE, w=[t_p])
            if nxt:
                emit_beta(c, eb + 1, 1)
            if nxt:
                pend["wpre"] = (c, eb + 1)
            t_yt = k.op("vector", lambda e, eb=eb, ur=u32r[ep]:
                        e.scalar_tensor_tensor(
                out=ur, in0=ur, scalar=dskip_sb[:, eb:eb + 1],
                in1=h_so[:, 0:Tc], op0=ALU.mult, op1=ALU.add),
                ch=cDVE, w=[t_p])
            h_guard[0] = t_yt
            t_sz = k.dma("sync", szr[ep], sz_s[par, eb], chZR[ep],
                         w=[sz_free[ep], sp_done_z[par][eb]])
            t_yg = k.op("vector", lambda e, yt=u32r[ep], sr=szr[ep],
                        yg=yg16r[ep], ts=t_sz: (
                e.wait_ge(ts[0].sem, ts[1]),
                e.tensor_tensor(out=yg, in0=yt, in1=sr, op=ALU.mult))[1],
                ch=cDVE, w=[t_yt, yg_free[ep], sp_done_y[par][eb]])
            u32_free[ep] = t_yg
            sz_free[ep] = t_yg
            sp_done_y[par][eb] = k.dma("gpsimd", yg_s[par, eb], yg16r[ep],
                                       chYS[ep], w=[t_yg])
            yg_free[ep] = sp_done_y[par][eb]

        def emit_D(c):
            par = c % 2
            seq, j = c // CH, c % CH
            t0 = j * Tc
            for eb in range(EB):
                ep = eb % 2
                t_yl = k.dma("sync", yg16r[ep], yg_s[par, eb], chYR[ep],
                             w=[yg_free[ep], sp_done_y[par][eb]])
                t_wo = k.dma("sync", wot[ep], wo_s[eb], chWO[ep],
                             w=[wot_free[ep]])

                def mm_out(e, eb=eb, ep=ep, t_yl=t_yl, t_wo=t_wo,
                           bfs=tuple(bank_free[0:8])):
                    e.wait_ge(t_yl[0].sem, t_yl[1])
                    e.wait_ge(t_wo[0].sem, t_wo[1])
                    if eb == 0:
                        for bf in bfs:
                            if bf[0]:
                                e.wait_ge(bf[0].sem, bf[1])
                    last = None
                    wo3 = wot[ep].rearrange("p (a n) -> p a n", a=8)
                    for dq in range(8):
                        last = e.matmul(psum[dq], wo3[:, dq, :], yg16r[ep],
                                        start=(eb == 0), stop=(eb == EB - 1))
                    return last
                t_mo = k.op("tensor", mm_out, ch=cPE)
                wot_free[ep] = t_mo
                yg_free[ep] = t_mo
            t_proj = (cPE, cPE.n)
            for db in range(DBL):
                dp = db % 2
                t_x32 = k.dma("sync", x32r[dp],
                              x_fm[seq].rearrange("(b p) t -> p b t", p=128)[:, db, t0:t0 + Tc],
                              chX32[dp], w=[x32_free[dp]])
                t_r1 = k.op("vector", lambda e, db=db, dp=dp, tx=t_x32: (
                    e.wait_ge(tx[0].sem, tx[1]),
                    e.tensor_tensor(out=r13[:, db, :], in0=psum[db],
                                    in1=x32r[dp], op=ALU.add))[1],
                    ch=cDVE, w=[t_proj])
                bank_free[db] = t_r1
                x32_free[dp] = t_r1
            t_r1_all = (cDVE, cDVE.n)
            for db in range(DBL):
                dp = db % 2
                t_sq = k.op("scalar", lambda e, db=db, dp=dp: e.activation(
                    out=r1sqr[dp], in_=r13[:, db, :], func=AF.Square, scale=1.0),
                    ch=cACT, w=[t_r1_all, sq_free[dp]])

                def mm_stat(e, db=db, dp=dp, t_sq=t_sq,
                            bf0=bank_free[4], bf1=bank_free[5]):
                    e.wait_ge(t_sq[0].sem, t_sq[1])
                    if db == 0:
                        e.wait_ge(bf0[0].sem, bf0[1])
                        e.wait_ge(bf1[0].sem, bf1[1])
                    e.matmul(psum[4][0:1, :], ones_col, r13[:, db, :],
                             start=(db == 0), stop=(db == DBL - 1))
                    return e.matmul(psum[5][0:1, :], ones_col, r1sqr[dp],
                                    start=(db == 0), stop=(db == DBL - 1))
                ms = k.op("tensor", mm_stat, ch=cPE)
                bank_w[4] = bank_w[5] = ms
                sq_free[dp] = ms
            t_m = k.op("vector", lambda e: e.tensor_scalar(
                out=SR[0], in0=psum[4][0:1, :], scalar1=1.0 / D,
                scalar2=None, op0=ALU.mult), ch=cDVE, w=[bank_w[4], beta_free[0]])
            t_mq = k.op("vector", lambda e: e.tensor_scalar(
                out=SR[1], in0=psum[5][0:1, :], scalar1=1.0 / D,
                scalar2=None, op0=ALU.mult), ch=cDVE)
            bank_free[4] = t_m
            bank_free[5] = t_mq
            t_v = k.op("vector", lambda e: e.tensor_tensor(
                out=SR[2], in0=SR[0], in1=SR[0], op=ALU.mult),
                ch=cDVE, w=[t_mq, beta_free[1]])
            t_v2 = k.op("vector", lambda e: e.scalar_tensor_tensor(
                out=SR[1], in0=SR[2], scalar=-1.0, in1=SR[1],
                op0=ALU.mult, op1=ALU.add), ch=cDVE, w=[t_v])
            t_v3 = k.op("vector", lambda e: e.tensor_scalar(
                out=SR[1], in0=SR[1], scalar1=LN_EPS, scalar2=None,
                op0=ALU.add), ch=cDVE, w=[t_v2])
            t_l = k.op("scalar", lambda e: e.activation(
                out=SR[2], in_=SR[1], func=AF.Ln, scale=1.0),
                ch=cACT, w=[t_v3])
            t_rs = k.op("scalar", lambda e: e.activation(
                out=SR[2], in_=SR[2], func=AF.Exp, scale=-0.5),
                ch=cACT, w=[t_l])

            def mm_bc(e, t_m=t_m, t_rs=t_rs, bf2=bank_free[6], bf3=bank_free[7]):
                e.wait_ge(t_m[0].sem, t_m[1])
                e.wait_ge(t_rs[0].sem, t_rs[1])
                if bf2[0]:
                    e.wait_ge(bf2[0].sem, bf2[1])
                if bf3[0]:
                    e.wait_ge(bf3[0].sem, bf3[1])
                e.matmul(psum[6], ones_row, SR[0], start=True, stop=True)
                return e.matmul(psum[7], ones_row, SR[2], start=True, stop=True)
            mb = k.op("tensor", mm_bc, ch=cPE)
            bank_w[6] = bank_w[7] = mb
            for db in range(DBL):
                dp = db % 2
                t_a = k.op("vector", lambda e, db=db, dp=dp: e.tensor_tensor(
                    out=ln1r[dp], in0=r13[:, db, :], in1=psum[6], op=ALU.subtract),
                    ch=cDVE, w=[mb, ln1_free[dp]])
                t_b = k.op("vector", lambda e, dp=dp: e.tensor_tensor(
                    out=ln1r[dp], in0=ln1r[dp], in1=psum[7], op=ALU.mult),
                    ch=cDVE, w=[t_a])
                t_rd = k.op("vector", lambda e, dp=dp: e.tensor_reduce(
                    out=redt, in_=ln1r[dp], axis=AX.X, op=ALU.max,
                    apply_absolute_value=True), ch=cDVE, w=[t_b])
                k.op("vector", lambda e: e.tensor_tensor(
                    out=amax_acc, in0=amax_acc, in1=redt, op=ALU.max),
                    ch=cDVE, w=[t_rd])
                ln1_st[c][db] = k.dma("gpsimd", ln1_s[c, db], ln1r[dp], chLN[dp],
                                      w=[t_b])
                ln1_free[dp] = ln1_st[c][db]
            t_ln_last = (cDVE, cDVE.n)
            bank_free[6] = t_ln_last
            bank_free[7] = t_ln_last
            beta_free[0] = mb
            beta_free[1] = mb
            dt_free[0] = ln1_free[0]
            dt_free[1] = ln1_free[1]
            tmp_free[0] = sq_free[0]
            tmp_free[1] = sq_free[1]

        pend = {}
        ua_free = [NONE_W, NONE_W]
        ua_us_prev = [NONE_W, NONE_W]
        sza_free = [NONE_W, NONE_W]
        h_guard = [NONE_W]
        emit_A_begin(0)
        for eb in range(EB):
            emit_A_iter(0, eb)
            emit_B_mm(0, eb)
        emit_A_begin(1)
        emit_B_tail(0)
        for c in range(NCH):
            if c % CH == 0:
                k.op("vector", lambda e: e.memset(carry, 0.0), ch=cDVE,
                     w=[(cDVE, cDVE.n)])
            emit_C_chain(c, 0)
            for eb in range(EB):
                if eb + 1 < EB:
                    emit_C_chain(c, eb + 1)
                emit_C_iter(c, eb)
                if c + 1 < NCH:
                    emit_A_iter(c + 1, eb)
                    emit_B_mm(c + 1, eb)
            if c + 1 < NCH:
                emit_B_tail(c + 1)
                if c + 2 < NCH:
                    emit_A_begin(c + 2)
            emit_D(c)
        h_guard = h_guard[0]

        s1 = k.dma("sync", pm_st, amax_acc, chSC, w=[(cDVE, cDVE.n), s2])
        s2 = k.dma("sync", pmrow, pm_st, chSC, w=[s1])
        t3 = k.op("vector", lambda e: e.tensor_reduce(
            out=scal[0:1, 0:1], in_=pmrow, axis=AX.X, op=ALU.max),
            ch=cDVE, w=[s2])
        s3 = k.dma("sync", cc_in.ap(), scal[0:1, 0:1], chSC, w=[t3, s2])
        t5 = k.op("gpsimd", lambda e: e.collective_compute(
            "AllReduce", ALU.max, ins=[cc_in.ap()], outs=[cc_out.ap()],
            replica_groups=[list(range(8))]), ch=cCC, w=[s3])
        s4 = k.dma("sync", scal[0:1, 0:1], cc_out.ap(), chSC, w=[t5, s3])
        t7 = k.op("vector", lambda e: e.reciprocal(
            out=scal[0:1, 5:7], in_=scal[0:1, 0:2]), ch=cDVE, w=[s4])
        t8 = k.op("vector", lambda e: e.tensor_scalar(
            out=scal[0:1, 2:4], in0=scal[0:1, 5:7], scalar1=FP8_MAX,
            scalar2=None, op0=ALU.mult), ch=cDVE, w=[t7])
        t9 = k.op("vector", lambda e: e.tensor_scalar(
            out=scal[0:1, 2:4], in0=scal[0:1, 2:4], scalar1=0.5, scalar2=None,
            op0=ALU.mult), ch=cDVE, w=[t8])
        t10 = k.op("vector", lambda e: e.tensor_tensor(
            out=scal[0:1, 6:7], in0=scal[0:1, 2:3], in1=scal[0:1, 3:4],
            op=ALU.mult), ch=cDVE, w=[t9])
        t12 = k.op("vector", lambda e: e.reciprocal(
            out=scal[0:1, 4:5], in_=scal[0:1, 6:7]), ch=cDVE, w=[t10])
        s5 = k.dma("sync", sc_st, scal, chSC, w=[t12, s4])
        s6 = k.dma("sync", scalb, sc_st.broadcast_to([128, 8]), chSC, w=[s5])

        fwp = NONE_W
        for db in range(DBL):
            fwp = k.dma("sync", w1f[:, 0:D], fwt[db * 128:(db + 1) * 128, :],
                        chFW, w=[fwp, s6, h_guard, (cDVE, cDVE.n)])
            for hf in range(2):
                tb = k.op("vector", lambda e, hf=hf, ta=fwp: (
                    e.wait_ge(ta[0].sem, ta[1]),
                    e.tensor_scalar(out=f8t, in0=w1f[:, hf * Tc:(hf + 1) * Tc],
                                    scalar1=scalb[:, 3:4], scalar2=None,
                                    op0=ALU.mult))[1], ch=cDVE, w=[s6])
                k.op("vector", lambda e, db=db, hf=hf: e.tensor_copy(
                    out=fw83[:, db, hf * Tc:(hf + 1) * Tc], in_=f8t),
                    ch=cDVE, w=[tb])
        t_fw = (cDVE, cDVE.n)
        f8t_free = NONE_W

        for c in range(NCH):
            seq, j = c // CH, c % CH
            t0 = j * Tc
            for db in range(DBL):
                dp = db % 2
                ta = k.dma("sync", x32r[dp], ln1_s[c, db], chX32[dp],
                           w=[x32_free[dp], ln1_st[c][db],
                              t_fw if c == 0 and db < 2 else NONE_W])
                tb = k.op("vector", lambda e, dp=dp, ta=ta: (
                    e.wait_ge(ta[0].sem, ta[1]),
                    e.tensor_scalar(out=f8t, in0=x32r[dp], scalar1=scalb[:, 2:3],
                                    scalar2=None, op0=ALU.mult))[1], ch=cDVE,
                    w=[t_fw, f8t_free])
                tc2 = k.op("scalar", lambda e, db=db: e.copy(
                    out=fxq3[:, db, :], in_=f8t), ch=cACT, w=[tb])
                td = k.op("scalar", lambda e, db=db, dp=dp: e.copy(
                    out=r13[:, db, :], in_=x32r[dp]), ch=cACT, w=[tb])
                f8t_free = tc2
                x32_free[dp] = td
            t_q = (cACT, cACT.n)
            t_qd = (cDVE, cDVE.n)
            for db in range(DBL):
                dp = db % 2
                bo = 6 + dp

                def mm_ffn(e, db=db, bo=bo, t_q=t_q, t_qd=t_qd, bf=bank_free[bo]):
                    e.wait_ge(t_q[0].sem, t_q[1])
                    e.wait_ge(t_qd[0].sem, t_qd[1])
                    if bf[0]:
                        e.wait_ge(bf[0].sem, bf[1])
                    last = None
                    for dk in range(DBL):
                        last = e.matmul(psum[bo], fw83[:, dk, db * 128:(db + 1) * 128],
                                        fxq3[:, dk, :], start=(dk == 0),
                                        stop=(dk == DBL - 1))
                    return last
                mf = k.op("tensor", mm_ffn, ch=cPE)
                bank_w[bo] = mf
                t_e = k.op("vector", lambda e, bo=bo, db=db, dp=dp: e.tensor_scalar(
                    out=r1sqr[dp], in0=psum[bo], scalar1=scalb[:, 4:5],
                    scalar2=fb_sb[:, db:db + 1], op0=ALU.mult, op1=ALU.add),
                    ch=cDVE, w=[mf, sq_free[dp]])
                bank_free[bo] = t_e
                t_r2 = k.op("vector", lambda e, db=db, dp=dp: e.tensor_tensor(
                    out=ff3[:, db, :], in0=r13[:, db, :], in1=r1sqr[dp],
                    op=ALU.add), ch=cDVE, w=[t_e])
                sq_free[dp] = t_r2
            t_ff_all = (cDVE, cDVE.n)
            for db in range(DBL):
                dp = db % 2
                t_sq = k.op("scalar", lambda e, db=db, dp=dp: e.activation(
                    out=r1sqr[dp], in_=ff3[:, db, :], func=AF.Square, scale=1.0),
                    ch=cACT, w=[t_ff_all, sq_free[dp]])

                def mm_stat2(e, db=db, dp=dp, t_sq=t_sq,
                             bf0=bank_free[0], bf1=bank_free[1]):
                    e.wait_ge(t_sq[0].sem, t_sq[1])
                    if db == 0:
                        e.wait_ge(bf0[0].sem, bf0[1])
                        e.wait_ge(bf1[0].sem, bf1[1])
                    e.matmul(psum[0][0:1, :], ones_col, ff3[:, db, :],
                             start=(db == 0), stop=(db == DBL - 1))
                    return e.matmul(psum[1][0:1, :], ones_col, r1sqr[dp],
                                    start=(db == 0), stop=(db == DBL - 1))
                ms = k.op("tensor", mm_stat2, ch=cPE)
                bank_w[0] = bank_w[1] = ms
                sq_free[dp] = ms
            t_m = k.op("vector", lambda e: e.tensor_scalar(
                out=SR[0], in0=psum[0][0:1, :], scalar1=1.0 / D,
                scalar2=None, op0=ALU.mult), ch=cDVE, w=[bank_w[0]])
            t_mq = k.op("vector", lambda e: e.tensor_scalar(
                out=SR[1], in0=psum[1][0:1, :], scalar1=1.0 / D,
                scalar2=None, op0=ALU.mult), ch=cDVE)
            bank_free[0] = t_m
            bank_free[1] = t_mq
            t_v = k.op("vector", lambda e: e.tensor_tensor(
                out=SR[2], in0=SR[0], in1=SR[0], op=ALU.mult),
                ch=cDVE, w=[t_mq])
            t_v2 = k.op("vector", lambda e: e.scalar_tensor_tensor(
                out=SR[1], in0=SR[2], scalar=-1.0, in1=SR[1],
                op0=ALU.mult, op1=ALU.add), ch=cDVE, w=[t_v])
            t_v3 = k.op("vector", lambda e: e.tensor_scalar(
                out=SR[1], in0=SR[1], scalar1=LN_EPS, scalar2=None,
                op0=ALU.add), ch=cDVE, w=[t_v2])
            t_l = k.op("scalar", lambda e: e.activation(
                out=SR[2], in_=SR[1], func=AF.Ln, scale=1.0),
                ch=cACT, w=[t_v3])
            t_rs = k.op("scalar", lambda e: e.activation(
                out=SR[2], in_=SR[2], func=AF.Exp, scale=-0.5),
                ch=cACT, w=[t_l])

            def mm_bc2(e, t_m=t_m, t_rs=t_rs, bf2=bank_free[2], bf3=bank_free[3]):
                e.wait_ge(t_m[0].sem, t_m[1])
                e.wait_ge(t_rs[0].sem, t_rs[1])
                e.wait_ge(bf2[0].sem, bf2[1])
                e.wait_ge(bf3[0].sem, bf3[1])
                e.matmul(psum[2], ones_row, SR[0], start=True, stop=True)
                return e.matmul(psum[3], ones_row, SR[2], start=True, stop=True)
            mb = k.op("tensor", mm_bc2, ch=cPE)
            bank_w[2] = bank_w[3] = mb
            for db in range(DBL):
                dp = db % 2
                t_a = k.op("vector", lambda e, db=db, dp=dp: e.tensor_tensor(
                    out=ln1r[dp], in0=ff3[:, db, :], in1=psum[2], op=ALU.subtract),
                    ch=cDVE, w=[mb, ln1_free[dp]])
                t_b = k.op("vector", lambda e, dp=dp: e.tensor_tensor(
                    out=ln1r[dp], in0=ln1r[dp], in1=psum[3], op=ALU.mult),
                    ch=cDVE, w=[t_a])
                ln1_free[dp] = k.dma(
                    "gpsimd",
                    out_fm[seq].rearrange("(b p) t -> p b t", p=128)[:, db, t0:t0 + Tc],
                    ln1r[dp], chST[dp], w=[t_b])
            t_last = (cDVE, cDVE.n)
            bank_free[2] = t_last
            bank_free[3] = t_last

        k.ops["sync"].append(lambda e, a=chST[0].n, b=chST[1].n: (
            e.wait_ge(chST[0].sem, a), e.wait_ge(chST[1].sem, b)))

        with nc.Block() as block:
            @block.sync
            def _(e):
                for f in k.ops["sync"]:
                    f(e)

            @block.scalar
            def _(e):
                for f in k.ops["scalar"]:
                    f(e)

            @block.vector
            def _(e):
                for f in k.ops["vector"]:
                    f(e)

            @block.tensor
            def _(e):
                for f in k.ops["tensor"]:
                    f(e)

            @block.gpsimd
            def _(e):
                for f in k.ops["gpsimd"]:
                    f(e)

    return nc


def kernel(L0_attn_x, L0_attn_W_in, L0_attn_conv_w, L0_attn_conv_b,
           L0_attn_W_xproj, L0_attn_W_dt, L0_attn_b_dt, L0_attn_A_log,
           L0_attn_D_skip, L0_attn_W_out, L0_ffn_weight, L0_ffn_bias):
    x = np.asarray(L0_attn_x, dtype=np.float32)
    w_in = np.asarray(L0_attn_W_in, dtype=np.float32)
    conv_w = np.asarray(L0_attn_conv_w, dtype=np.float32)
    conv_b = np.asarray(L0_attn_conv_b, dtype=np.float32)
    w_xproj = np.asarray(L0_attn_W_xproj, dtype=np.float32)
    w_dt = np.asarray(L0_attn_W_dt, dtype=np.float32)
    b_dt = np.asarray(L0_attn_b_dt, dtype=np.float32)
    a_log = np.asarray(L0_attn_A_log, dtype=np.float32)
    d_skip = np.asarray(L0_attn_D_skip, dtype=np.float32)
    w_out = np.asarray(L0_attn_W_out, dtype=np.float32)
    fw = np.asarray(L0_ffn_weight, dtype=np.float32)
    fbias = np.asarray(L0_ffn_bias, dtype=np.float32)

    w_inT = np.ascontiguousarray(w_in.T)
    shared = {
        "w1t": np.ascontiguousarray(w_inT[:, :E]),
        "wzt": np.ascontiguousarray(w_inT[:, E:]),
        "cwrows": np.ascontiguousarray(conv_w.T),
        "cb": np.ascontiguousarray(conv_b.reshape(EB, 128).T),
        "wxt": np.ascontiguousarray(w_xproj.T),
        "wdtt": np.ascontiguousarray(w_dt.T),
        "bdt": np.ascontiguousarray(b_dt.reshape(EB, 128).T),
        "alog": np.ascontiguousarray(
            a_log.reshape(EB, 128, S).transpose(1, 0, 2).reshape(128, EB * S)),
        "dskip": np.ascontiguousarray(d_skip.reshape(EB, 128).T),
        "woutt": np.ascontiguousarray(w_out.T),
        "fwt": np.ascontiguousarray(fw.T),
        "fb": np.ascontiguousarray(fbias.reshape(DBL, 128).T),
    }
    in_maps = []
    for c in range(8):
        xc = np.ascontiguousarray(x[2 * c:2 * c + 2].transpose(0, 2, 1))
        m = {"x_fm": xc}
        m.update(shared)
        in_maps.append(m)

    nc = build_nc()
    res = run_bass_kernel_spmd(nc, in_maps, core_ids=list(range(8)))
    out = np.empty((B_TOT, T, D), dtype=np.float32)
    for c in range(8):
        o = np.asarray(res.results[c]["out_fm"])
        out[2 * c:2 * c + 2] = o.transpose(0, 2, 1)
    return out


# version 13
# speedup vs baseline: 1.1197x; 1.0441x over previous
import contextlib
import numpy as np
import concourse.bass as bass
import concourse.mybir as mybir
from concourse.bass_utils import run_bass_kernel_spmd

F32 = mybir.dt.float32
BF16 = mybir.dt.bfloat16
FP8 = mybir.dt.float8e4
AF = mybir.ActivationFunctionType
ALU = mybir.AluOpType
AX = mybir.AxisListType

B_TOT, T, D = 16, 4096, 1024
E = 2048
S = 16
R = 64
Tc = 512
EB = E // 128
DBL = D // 128
SEQ_PER_CORE = 2
CH = T // Tc
NCH = SEQ_PER_CORE * CH
FP8_MAX = 448.0
LN_EPS = 1e-5


class Ch:
    def __init__(self, sem):
        self.sem = sem
        self.n = 0


class KB:

    ENGS = ("sync", "scalar", "vector", "tensor", "gpsimd")

    def __init__(self):
        self.ops = {e: [] for e in self.ENGS}

    def dma(self, eng, out, in_, ch, w=()):
        ch.n += 16
        t = ch.n
        waits = tuple((c.sem, v) for (c, v) in w if c is not None)
        sem = ch.sem

        def f(e, out=out, in_=in_, sem=sem, waits=waits):
            for (s, v) in waits:
                e.wait_ge(s, v)
            e.dma_start(out=out, in_=in_).then_inc(sem, 16)
        self.ops[eng].append(f)
        return (ch, t)

    def op(self, eng, fn, ch=None, w=()):
        t = None
        sem = None
        if ch is not None:
            ch.n += 1
            t = ch.n
            sem = ch.sem
        waits = tuple((c.sem, v) for (c, v) in w if c is not None)

        def f(e, fn=fn, sem=sem, waits=waits):
            for (s, v) in waits:
                e.wait_ge(s, v)
            i = fn(e)
            if sem is not None:
                i.then_inc(sem, 1)
        self.ops[eng].append(f)
        return (ch, t)


NONE_W = (None, 0)


def build_nc():
    nc = bass.Bass("TRN2", target_bir_lowering=False)

    x_fm = nc.dram_tensor("x_fm", [SEQ_PER_CORE, D, T], F32, kind="ExternalInput").ap()
    w1t = nc.dram_tensor("w1t", [D, E], F32, kind="ExternalInput").ap()
    wzt = nc.dram_tensor("wzt", [D, E], F32, kind="ExternalInput").ap()
    cwrows = nc.dram_tensor("cwrows", [4, E], F32, kind="ExternalInput").ap()
    cb = nc.dram_tensor("cb", [128, EB], F32, kind="ExternalInput").ap()
    wxt = nc.dram_tensor("wxt", [E, 96], F32, kind="ExternalInput").ap()
    wdtt = nc.dram_tensor("wdtt", [R, E], F32, kind="ExternalInput").ap()
    bdt = nc.dram_tensor("bdt", [128, EB], F32, kind="ExternalInput").ap()
    alog = nc.dram_tensor("alog", [128, EB * S], F32, kind="ExternalInput").ap()
    dskip = nc.dram_tensor("dskip", [128, EB], F32, kind="ExternalInput").ap()
    woutt = nc.dram_tensor("woutt", [E, D], F32, kind="ExternalInput").ap()
    fwt = nc.dram_tensor("fwt", [D, D], F32, kind="ExternalInput").ap()
    fb = nc.dram_tensor("fb", [128, DBL], F32, kind="ExternalInput").ap()
    out_fm = nc.dram_tensor("out_fm", [SEQ_PER_CORE, D, T], F32, kind="ExternalOutput").ap()

    wk_s = nc.dram_tensor("wk_s", [EB, 2, 128, DBL // 2, 4, 128], BF16, kind="Internal").ap()
    wz_s = nc.dram_tensor("wz_s", [EB, 128, DBL, 128], BF16, kind="Internal").ap()
    wo_s = nc.dram_tensor("wo_s", [EB, 128, D], BF16, kind="Internal").ap()
    u32_s = nc.dram_tensor("u32_s", [2, EB, 128, Tc], F32, kind="Internal").ap()
    sz_s = nc.dram_tensor("sz_s", [2, EB, 128, Tc], F32, kind="Internal").ap()
    yg_s = nc.dram_tensor("yg_s", [2, EB, 128, Tc], BF16, kind="Internal").ap()
    u16_s = nc.dram_tensor("u16_s", [2, EB, 128, Tc], BF16, kind="Internal").ap()
    ln1_s = nc.dram_tensor("ln1_s", [NCH, DBL, 128, Tc], F32, kind="Internal").ap()
    bc_st = nc.dram_tensor("bc_st", [2, 2, S * Tc], F32, kind="Internal").ap()
    c_il = nc.dram_tensor("c_il", [2, S * Tc], F32, kind="Internal").ap()
    pm_st = nc.dram_tensor("pm_st", [1, 128], F32, kind="Internal").ap()
    sc_st = nc.dram_tensor("sc_st", [1, 8], F32, kind="Internal").ap()
    cc_in = nc.dram_tensor("cc_in", [1, 1], F32, kind="Internal")
    cc_out = nc.dram_tensor("cc_out", [1, 1], F32, kind="Internal")

    k = KB()

    with contextlib.ExitStack() as stk:
        def sb(name, shape, dt=F32):
            return stk.enter_context(nc.sbuf_tensor(name, shape, dt)).ap()

        cb_sb = sb("cb_sb", [128, EB])
        ncb_sb = sb("ncb_sb", [128, EB])
        bdt_sb = sb("bdt_sb", [128, EB])
        dskip_sb = sb("dskip_sb", [128, EB])
        a_sb = sb("a_sb", [128, EB * S])
        fb_sb = sb("fb_sb", [128, DBL])
        ones_row = sb("ones_row", [1, 128])
        ones_col = sb("ones_col", [128, 1])
        wxt_sb = sb("wxt_sb", [128, EB * 96], BF16)
        wdt_sb = sb("wdt_sb", [64, E], BF16)
        xb = [sb(f"xb{p}", [128, DBL * (Tc + 3)], BF16) for p in range(2)]
        u32r = [sb(f"u32r{i}", [128, Tc]) for i in range(2)]
        ua32r = [sb(f"ua32r{i}", [128, Tc]) for i in range(2)]
        szar = [sb(f"szar{i}", [128, Tc]) for i in range(2)]
        u16r = [sb(f"u16r{i}", [128, Tc], BF16) for i in range(2)]
        dbc_ev = sb("dbc_ev", [96, Tc])
        cil_ev = sb("cil_ev", [128, 4 * S])
        dtlo16 = sb("dtlo16", [64, Tc], BF16)
        dt32r = [sb(f"dt32r{i}", [128, Tc]) for i in range(2)]
        tmpr = [sb(f"tmpr{i}", [128, Tc]) for i in range(2)]
        wtile = [sb(f"wt{i}", [128, Tc]) for i in range(2)]
        dar = [sb(f"dar{i}", [128, Tc]) for i in range(3 - 1)]
        betar = [sb(f"betar{i}", [128, 2 * Tc]) for i in range(2)]
        h_so = sb("h_so", [128, S * Tc])
        bb = sb("bb", [128, S * Tc])
        cbb = sb("cbb", [128, S * Tc])
        carry = sb("carry", [128, EB * S])
        szr = [sb(f"szr{i}", [128, Tc]) for i in range(2)]
        yg16r = [sb(f"yg16r{i}", [128, Tc], BF16) for i in range(2)]
        wkt = [sb(f"wkt{i}", [128, (DBL // 2) * 4 * 128], BF16) for i in range(2)]
        wzt_r = [sb(f"wztr{i}", [128, DBL * 128], BF16) for i in range(2)]
        wot = [sb(f"wot{i}", [128, 8 * 128], BF16) for i in range(2)]
        x32r = [sb(f"x32r{i}", [128, Tc]) for i in range(2)]
        r1 = sb("r1", [128, DBL * Tc])
        r1sqr = tmpr
        ln1r = dt32r
        amax_acc = sb("amax_acc", [128, 1])
        redt = sb("redt", [128, 1])
        pmrow = sb("pmrow", [1, 128])
        scal = sb("scal", [1, 8])
        scalb = sb("scalb", [128, 8])
        f8t = dar[0].bitcast(FP8)[:, 0:Tc]
        SR = [betar[0][0:1, 0:Tc], betar[0][0:1, Tc:2 * Tc],
              betar[1][0:1, 0:Tc]]

        w1f = h_so[:, 0:E]
        cwb = h_so[:, E:2 * E]
        wtmp16 = bb.bitcast(BF16)[:, 0:E]
        fw8 = bb.bitcast(BF16)[:, 0:DBL * D]
        fw83 = fw8.rearrange("p (b n) -> p b n", n=D)
        fxq3 = cbb.bitcast(BF16)[:, 0:DBL * Tc].rearrange("p (b n) -> p b n", n=Tc)
        ff3 = h_so[:, 0:DBL * Tc].rearrange("p (b n) -> p b n", n=Tc)

        psum = [stk.enter_context(nc.psum_tensor(f"pb{i}", [128, Tc], F32)).ap()
                for i in range(8)]

        sems = [stk.enter_context(nc.semaphore(f"sm{i}")) for i in range(48)]
        _it = iter(sems)

        def mkch():
            return Ch(next(_it))

        cPE, cACT, cDVE, cCC = mkch(), mkch(), mkch(), mkch()
        chP0L = mkch()
        chP0S = mkch()
        chXB = [mkch(), mkch()]
        chU16 = [mkch(), mkch()]
        chU16L = [mkch(), mkch()]
        chWK = [mkch(), mkch()]
        chWZ = [mkch(), mkch()]
        chWO = [mkch(), mkch()]
        chUS = [mkch(), mkch()]
        chZS = [mkch(), mkch()]
        chYS = [mkch(), mkch()]
        chUR = [mkch(), mkch()]
        chZR = [mkch(), mkch()]
        chYR = [mkch(), mkch()]
        chX32 = [mkch(), mkch()]
        chLN = [mkch(), mkch()]
        chBS = [mkch(), mkch()]
        chCS = [mkch(), mkch()]
        chBB = mkch()
        chBB2 = mkch()
        chCB = mkch()
        chSC = mkch()
        chFW = mkch()
        chST = [mkch(), mkch()]

        bank_w = [NONE_W] * 8
        bank_free = [NONE_W] * 8

        p0 = NONE_W
        for (dst, src) in ((cb_sb, cb), (bdt_sb, bdt), (dskip_sb, dskip),
                           (a_sb, alog), (fb_sb, fb)):
            p0 = k.dma("sync", dst, src, chP0L, w=[p0])
        t_wres = k.dma("gpsimd", wxt_sb.rearrange("p (b n) -> p b n", n=96),
                       wxt.rearrange("(b p) n -> p b n", p=128), chP0S)
        t_wres = k.dma("gpsimd", wdt_sb, wdtt, chP0S, w=[t_wres])
        t_aexp = k.op("scalar", lambda e: e.activation(out=a_sb, in_=a_sb,
                                                       func=AF.Exp, scale=1.0),
                      ch=cACT, w=[p0])
        k.op("vector", lambda e: e.tensor_scalar(out=a_sb, in0=a_sb, scalar1=-1.0,
                                                 scalar2=None, op0=ALU.mult),
             ch=cDVE, w=[t_aexp])
        k.op("vector", lambda e: e.tensor_scalar(out=ncb_sb, in0=cb_sb,
                                                 scalar1=-1.0, scalar2=None,
                                                 op0=ALU.mult), ch=cDVE, w=[p0])
        k.op("vector", lambda e: e.memset(ones_row, 1.0), ch=cDVE)
        k.op("vector", lambda e: e.memset(ones_col, 1.0), ch=cDVE)
        k.op("vector", lambda e: e.memset(carry, 0.0), ch=cDVE)
        k.op("vector", lambda e: e.memset(amax_acc, 0.0), ch=cDVE)
        k.op("vector", lambda e: e.memset(redt, 0.0), ch=cDVE)

        st_p0 = t_wres
        for db in range(DBL):
            p0 = k.dma("sync", w1f, w1t[db * 128:(db + 1) * 128, :], chP0L,
                       w=[p0, st_p0])
            for tap in range(4):
                p0 = k.dma("sync", cwb, cwrows[tap:tap + 1, :].broadcast_to([128, E]),
                           chP0L, w=[p0, (cDVE, cDVE.n)])
                tt = k.op("vector", lambda e: e.tensor_tensor(
                    out=wtmp16, in0=w1f, in1=cwb, op=ALU.mult),
                    ch=cDVE, w=[p0, st_p0])
                half, dbh = db // (DBL // 2), db % (DBL // 2)
                st_p0 = k.dma("gpsimd",
                              wk_s[:, half, :, dbh, tap, :].transpose([1, 0, 2]),
                              wtmp16.rearrange("p (b n) -> p b n", n=128), chP0S,
                              w=[tt, st_p0])
            p0 = k.dma("sync", w1f, wzt[db * 128:(db + 1) * 128, :], chP0L,
                       w=[p0, st_p0])
            tt2 = k.op("vector", lambda e: e.tensor_copy(out=wtmp16, in_=w1f),
                       ch=cDVE, w=[p0])
            st_p0 = k.dma("gpsimd", wz_s[:, :, db, :].transpose([1, 0, 2]),
                          wtmp16.rearrange("p (b n) -> p b n", n=128), chP0S,
                          w=[tt2, st_p0])
        for eb in range(EB):
            p0 = k.dma("sync", w1f[:, 0:D], woutt[eb * 128:(eb + 1) * 128, :],
                       chP0L, w=[p0, st_p0])
            tt = k.op("vector", lambda e: e.tensor_copy(out=wtmp16[:, 0:D],
                                                        in_=w1f[:, 0:D]),
                      ch=cDVE, w=[p0])
            st_p0 = k.dma("gpsimd", wo_s[eb], wtmp16[:, 0:D],
                          chP0S, w=[tt, st_p0])
        for db in range(DBL):
            p0 = k.dma("sync", w1f[:, 0:D], fwt[db * 128:(db + 1) * 128, :],
                       chP0L, w=[p0, st_p0, (cDVE, cDVE.n)])
            tr = k.op("vector", lambda e: e.tensor_reduce(
                out=redt, in_=w1f[:, 0:D], axis=AX.X, op=ALU.max,
                apply_absolute_value=True), ch=cDVE, w=[p0])
            if db == 0:
                k.op("vector", lambda e: e.tensor_copy(out=amax_acc, in_=redt),
                     ch=cDVE, w=[tr])
            else:
                k.op("vector", lambda e: e.tensor_tensor(
                    out=amax_acc, in0=amax_acc, in1=redt, op=ALU.max),
                    ch=cDVE, w=[tr])
        s1 = k.dma("sync", pm_st, amax_acc, chSC, w=[(cDVE, cDVE.n)])
        s2 = k.dma("sync", pmrow, pm_st, chSC, w=[s1])
        k.op("vector", lambda e: e.tensor_reduce(out=scal[0:1, 1:2], in_=pmrow,
                                                 axis=AX.X, op=ALU.max),
             ch=cDVE, w=[s2])
        k.op("vector", lambda e: e.memset(amax_acc, 0.0), ch=cDVE,
             w=[(cDVE, cDVE.n)])
        p0_all = (chP0S, chP0S.n)

        xb_guard = [NONE_W, NONE_W]
        u32_free = [NONE_W, NONE_W]
        u16_free = [NONE_W, NONE_W]
        dt_free = [NONE_W, NONE_W]
        tmp_free = [NONE_W, NONE_W]
        w_free = [NONE_W, NONE_W]
        dar_free = [NONE_W] * 2
        beta_free = [NONE_W, NONE_W]
        sz_free = [NONE_W, NONE_W]
        yg_free = [NONE_W, NONE_W]
        wkt_free = [p0_all, p0_all]
        wzt_free = [p0_all, p0_all]
        wot_free = [p0_all, p0_all]
        x32_free = [NONE_W, NONE_W]
        sq_free = [NONE_W, NONE_W]
        ln1_free = [NONE_W, NONE_W]
        sp_done_u = [[NONE_W] * EB, [NONE_W] * EB]
        sp16_done = [[NONE_W] * EB, [NONE_W] * EB]
        sp_done_z = [[NONE_W] * EB, [NONE_W] * EB]
        sp_done_y = [[NONE_W] * EB, [NONE_W] * EB]
        ln1_st = [[NONE_W] * DBL for _ in range(NCH)]
        bc_guard = [NONE_W, NONE_W]
        bc_rdB = [NONE_W, NONE_W]
        bc_rdB2 = [NONE_W, NONE_W]
        cil_guard = [NONE_W]
        bc_rdC = [NONE_W, NONE_W]
        dbc_guard = NONE_W
        h_guard = NONE_W
        r13 = r1.rearrange("p (b n) -> p b n", n=Tc)

        a_t_x = [NONE_W, NONE_W]

        def emit_A_begin(c):
            par = c % 2
            seq, j = c // CH, c % CH
            t0 = j * Tc
            xw3 = xb[par].rearrange("p (b n) -> p b n", n=Tc + 3)
            if j == 0:
                tz = k.op("gpsimd", lambda e, xw3=xw3: e.memset(xw3[:, :, 0:3], 0.0),
                          ch=cCC, w=[xb_guard[par]])
                a_t_x[par] = k.dma("gpsimd", xw3[:, :, 3:Tc + 3],
                                   x_fm[seq].rearrange("(b p) t -> p b t", p=128)[:, :, t0:t0 + Tc],
                                   chXB[par], w=[tz])
            else:
                a_t_x[par] = k.dma("gpsimd", xw3[:, :, 0:Tc + 3],
                                   x_fm[seq].rearrange("(b p) t -> p b t", p=128)[:, :, t0 - 3:t0 + Tc],
                                   chXB[par], w=[xb_guard[par]])

        def emit_A_iter(c, eb):
            par = c % 2
            ep = eb % 2
            bk = ep
            bz = 2 + ep
            t_x = a_t_x[par]
            xw3 = xb[par].rearrange("p (b n) -> p b n", n=Tc + 3)
            t_wk = [None, None]
            for half in range(2):
                t_wk[half] = k.dma("sync", wkt[half],
                                   wk_s[eb, half].rearrange("p a b n -> p (a b n)"),
                                   chWK[half], w=[wkt_free[half]])
            t_wz = k.dma("sync", wzt_r[ep], wz_s[eb].rearrange("p b n -> p (b n)"),
                         chWZ[ep], w=[wzt_free[ep]])

            def mm_xc(e, bk=bk, t_wk=tuple(t_wk), t_x=t_x,
                      bf=bank_free[bk], xw3=xw3):
                e.wait_ge(t_x[0].sem, t_x[1])
                if bf[0]:
                    e.wait_ge(bf[0].sem, bf[1])
                last = None
                for half in range(2):
                    e.wait_ge(t_wk[half][0].sem, t_wk[half][1])
                    w4 = wkt[half].rearrange("p (a b n) -> p a b n",
                                             a=DBL // 2, b=4)
                    for dbh in range(DBL // 2):
                        db = half * (DBL // 2) + dbh
                        for tap in range(4):
                            last = e.matmul(
                                psum[bk], w4[:, dbh, tap, :],
                                xw3[:, db, tap:tap + Tc],
                                start=(half == 0 and dbh == 0 and tap == 0),
                                stop=(half == 1 and dbh == DBL // 2 - 1 and tap == 3))
                return last
            mw = k.op("tensor", mm_xc, ch=cPE)
            bank_w[bk] = mw
            wkt_free[0] = wkt_free[1] = mw
            t_s1 = k.op("scalar", lambda e, bk=bk, eb=eb, ur=ua32r[ep]: e.activation(
                out=ur, in_=psum[bk], func=AF.Exp, bias=ncb_sb[:, eb:eb + 1],
                scale=-1.0), ch=cACT,
                w=[mw, ua_free[ep], ua_us_prev[ep], sp_done_u[par][eb]])
            k.op("scalar", lambda e, ur=ua32r[ep]: e.activation(
                out=ur, in_=ur, func=AF.Ln, bias=1.0, scale=1.0), ch=cACT)
            t_s4 = k.op("scalar", lambda e, ur=ua32r[ep]: e.activation(
                out=ur, in_=ur, func=AF.Exp, scale=-1.0), ch=cACT)
            t_u = k.op("vector", lambda e, bk=bk, eb=eb, ur=ua32r[ep]:
                       e.scalar_tensor_tensor(
                out=ur, in0=psum[bk], scalar=cb_sb[:, eb:eb + 1], in1=ur,
                op0=ALU.add, op1=ALU.mult), ch=cDVE, w=[t_s4])
            bank_free[bk] = t_u
            sp_done_u[par][eb] = k.dma("gpsimd", u32_s[par, eb], ua32r[ep],
                                       chUS[ep], w=[t_u])
            sp16_done[par][eb] = k.dma("gpsimd", u16_s[par, eb], ua32r[ep],
                                       chU16[ep], w=[t_u, sp16_done[par][eb]])
            ua_us_prev[ep] = sp_done_u[par][eb]
            ua_free[ep] = sp16_done[par][eb]

            def mm_z(e, bz=bz, ep=ep, t_wz=t_wz, t_x=t_x,
                     bf=bank_free[bz], xw3=xw3):
                e.wait_ge(t_x[0].sem, t_x[1])
                e.wait_ge(t_wz[0].sem, t_wz[1])
                if bf[0]:
                    e.wait_ge(bf[0].sem, bf[1])
                last = None
                wz3 = wzt_r[ep].rearrange("p (a n) -> p a n", a=DBL)
                for db in range(DBL):
                    last = e.matmul(psum[bz], wz3[:, db, :], xw3[:, db, 3:Tc + 3],
                                    start=(db == 0), stop=(db == DBL - 1))
                return last
            mz = k.op("tensor", mm_z, ch=cPE)
            bank_w[bz] = mz
            wzt_free[ep] = mz
            k.op("scalar", lambda e, bz=bz, sr=szar[ep]: e.activation(
                out=sr, in_=psum[bz], func=AF.Exp, scale=-1.0), ch=cACT,
                w=[mz, sza_free[ep], sp_done_z[par][eb]])
            k.op("scalar", lambda e, sr=szar[ep]: e.activation(
                out=sr, in_=sr, func=AF.Ln, bias=1.0, scale=1.0), ch=cACT)
            t_z4 = k.op("scalar", lambda e, sr=szar[ep]: e.activation(
                out=sr, in_=sr, func=AF.Exp, scale=-1.0), ch=cACT)
            t_z = k.op("vector", lambda e, bz=bz, sr=szar[ep]: e.tensor_tensor(
                out=sr, in0=psum[bz], in1=sr, op=ALU.mult), ch=cDVE, w=[t_z4])
            bank_free[bz] = t_z
            sp_done_z[par][eb] = k.dma("gpsimd", sz_s[par, eb], szar[ep],
                                       chZS[ep], w=[t_z])
            sza_free[ep] = sp_done_z[par][eb]
            if eb == EB - 1:
                xb_guard[par] = (cPE, cPE.n)

        def emit_B(c):
            par = c % 2
            for eb in range(EB):
                ep = eb % 2
                tl = k.dma("sync", u16r[ep], u16_s[par, eb], chU16L[ep],
                           w=[u16_free[ep], sp16_done[par][eb]])

                def mm_one(e, eb=eb, ep=ep, tl=tl, bf=bank_free[4],
                           bf5=bank_free[5]):
                    e.wait_ge(tl[0].sem, tl[1])
                    if eb == 0:
                        if bf[0]:
                            e.wait_ge(bf[0].sem, bf[1])
                        if bf5[0]:
                            e.wait_ge(bf5[0].sem, bf5[1])
                    e.matmul(psum[4][0:96, :],
                             wxt_sb.rearrange("p (b n) -> p b n", n=96)[:, eb, :],
                             u16r[ep], start=(eb == 0), stop=(eb == EB - 1))
                    last = None
                    for tb in range(4):
                        last = e.matmul(
                            psum[5][:, tb * 96:(tb + 1) * 96],
                            u16r[ep][:, tb * 128:(tb + 1) * 128],
                            wxt_sb.rearrange("p (b n) -> p b n", n=96)[:, eb, :],
                            start=(eb == 0), stop=(eb == EB - 1))
                    return last
                mo = k.op("tensor", mm_one, ch=cPE)
                bank_w[4] = bank_w[5] = mo
                u16_free[ep] = mo
            t_lo = k.op("scalar", lambda e: e.copy(out=dtlo16,
                                                   in_=psum[4][0:64, :]),
                        ch=cACT, w=[bank_w[4]])
            t_bc = k.op("scalar", lambda e: e.copy(out=dbc_ev[64:80, :],
                                                   in_=psum[4][64:80, :]),
                        ch=cACT, w=[t_lo, dbc_guard])
            bank_free[4] = t_bc
            t_ce = k.op("scalar", lambda e: e.copy(
                out=cil_ev.rearrange("p (a b) -> p a b", b=S),
                in_=psum[5][:, 0:384].rearrange("p (a b) -> p a b", b=96)[:, :, 80:96]),
                ch=cACT, w=[cil_guard[0]])
            bank_free[5] = t_ce
            t_sb = k.dma("gpsimd", bc_st[par, 0].rearrange("(a n) -> a n", n=Tc),
                         dbc_ev[64:80, :], chBS[par],
                         w=[t_bc, bc_rdB[par], bc_rdB2[par]])
            t_sc = k.dma("gpsimd",
                         c_il[par].rearrange("(tb p s) -> p tb s", p=128, s=S),
                         cil_ev.rearrange("p (a b) -> p a b", b=S),
                         chCS[par], w=[t_ce, bc_rdC[par]])
            t_bb0 = k.dma("sync", bb[:, 0:4 * Tc],
                          bc_st[par, 0][0:4 * Tc].unsqueeze(0)
                          .broadcast_to([128, 4 * Tc]), chBB,
                          w=[t_sb, bc_guard[0]])
            t_bb1 = k.dma("sync", bb[:, 4 * Tc:S * Tc],
                          bc_st[par, 0][4 * Tc:S * Tc].unsqueeze(0)
                          .broadcast_to([128, 12 * Tc]), chBB2,
                          w=[t_sb, bc_guard[0]])
            t_cb = k.dma("gpsimd", cbb, c_il[par].unsqueeze(0)
                         .broadcast_to([128, S * Tc]), chCB,
                         w=[t_sc, bc_guard[1]])
            bc_rdB[par] = t_bb0
            bc_rdB2[par] = t_bb1
            bc_rdC[par] = t_cb
            cil_guard[0] = t_sc
            pend["t_lo"] = t_lo
            pend["t_bb0"] = t_bb0
            pend["t_bb"] = t_bb1
            pend["t_cb"] = t_cb
            return t_sc

        c_e3 = [NONE_W, NONE_W]

        def emit_C_chain(c, eb):
            par = c % 2
            ep = eb % 2
            bd = 5 - ep
            t_lo = pend["t_lo"]

            def mm_dt(e, bd=bd, eb=eb, t_lo=t_lo, bf=bank_free[bd]):
                e.wait_ge(t_lo[0].sem, t_lo[1])
                if bf[0]:
                    e.wait_ge(bf[0].sem, bf[1])
                return e.matmul(psum[bd], wdt_sb[:, eb * 128:(eb + 1) * 128],
                                dtlo16, start=True, stop=True)
            md = k.op("tensor", mm_dt, ch=cPE)
            bank_w[bd] = md
            t_e1 = k.op("scalar", lambda e, bd=bd, eb=eb, tm=tmpr[ep]: e.activation(
                out=tm, in_=psum[bd], func=AF.Exp, bias=bdt_sb[:, eb:eb + 1],
                scale=1.0), ch=cACT, w=[md, tmp_free[ep]])
            bank_free[bd] = t_e1
            t_e3 = k.op("scalar", lambda e, tm=tmpr[ep], dt=dt32r[ep]: e.activation(
                out=dt, in_=tm, func=AF.Ln, bias=1.0, scale=1.0), ch=cACT,
                w=[dt_free[ep]])
            tmp_free[ep] = t_e3
            c_e3[ep] = t_e3

        c_w = [NONE_W, NONE_W]

        def emit_w(c, eb):
            par = c % 2
            ep = eb % 2
            t_u32 = k.dma("sync", u32r[ep], u32_s[par, eb], chUR[ep],
                          w=[u32_free[ep], sp_done_u[par][eb]])
            c_w[ep] = k.op("vector", lambda e, dt=dt32r[ep], ur=u32r[ep],
                           wt=wtile[ep], tu=t_u32: (
                e.wait_ge(tu[0].sem, tu[1]),
                e.tensor_tensor(out=wt, in0=dt, in1=ur, op=ALU.mult))[1],
                ch=cDVE, w=[c_e3[ep], w_free[ep]])

        def emit_beta(c, eb, pr):
            ep = eb % 2
            bb3 = bb.rearrange("p (s n) -> p s n", n=Tc)
            t_bb = pend["t_bb0"] if pr < 2 else pend["t_bb"]
            k.op("vector", lambda e, pr=pr, wt=wtile[ep],
                 slot=pr % 2, t_bb=t_bb, bb3=bb3: (
                e.wait_ge(t_bb[0].sem, t_bb[1]),
                e.tensor_tensor(
                    out=betar[slot].rearrange("p (a n) -> p a n", n=Tc),
                    in0=wt.unsqueeze(1).broadcast_to([128, 2, Tc]),
                    in1=bb3[:, 2 * pr:2 * pr + 2, :],
                    op=ALU.mult))[1],
                ch=cDVE, w=[c_w[ep], beta_free[pr % 2]])

        def emit_C_iter(c, eb):
            par = c % 2
            ep = eb % 2
            t_bb, t_cb = pend["t_bb"], pend["t_cb"]
            bb3 = bb.rearrange("p (s n) -> p s n", n=Tc)
            t_e3 = c_e3[ep]
            if pend.get("wpre") != (c, eb):
                emit_w(c, eb)
                emit_beta(c, eb, 0)
                emit_beta(c, eb, 1)
            t_scan = NONE_W
            for s_i in range(S):
                sp_ = s_i % 2
                pr = s_i // 2
                if sp_ == 0 and pr >= 2:
                    emit_beta(c, eb, pr)
                t_da = k.op("scalar", lambda e, s_i=s_i, eb=eb, dt=dt32r[ep],
                            d=dar[s_i % 2]: e.activation(
                    out=d, in_=dt, func=AF.Exp,
                    scale=a_sb[:, eb * S + s_i:eb * S + s_i + 1]),
                    ch=cACT, w=[t_e3, dar_free[s_i % 2]])
                t_scan = k.op("vector", lambda e, s_i=s_i, eb=eb, sp_=sp_,
                              pr=pr, d=dar[s_i % 2]: e.tensor_tensor_scan(
                    out=h_so.rearrange("p (n s) -> p n s", s=S)[:, :, s_i],
                    data0=d,
                    data1=betar[pr % 2][:, sp_ * Tc:(sp_ + 1) * Tc],
                    initial=carry[:, eb * S + s_i:eb * S + s_i + 1],
                    op0=ALU.mult, op1=ALU.add),
                    ch=cDVE, w=[t_da] + ([h_guard[0]] if s_i == 0 else []))
                dar_free[s_i % 2] = t_scan
                if sp_ == 1:
                    beta_free[pr % 2] = t_scan
            dt_free[ep] = t_scan
            w_free[ep] = t_scan
            t_cu = k.op("vector", lambda e, eb=eb: e.tensor_copy(
                out=carry[:, eb * S:(eb + 1) * S],
                in_=h_so[:, (Tc - 1) * S:Tc * S]),
                ch=cDVE, w=[t_scan])
            nxt = eb + 1 < EB
            t_p = k.op("vector", lambda e, t_cb=t_cb: (
                e.wait_ge(t_cb[0].sem, t_cb[1]),
                e.tensor_tensor(out=h_so, in0=h_so, in1=cbb, op=ALU.mult))[1],
                ch=cDVE, w=[t_cu])
            if nxt:
                emit_w(c, eb + 1)
                emit_beta(c, eb + 1, 0)
            t_p = k.op("vector", lambda e: e.tensor_reduce(
                out=h_so[:, 0:Tc], in_=h_so.rearrange("p (n s) -> p n s", s=S),
                axis=AX.X, op=ALU.add), ch=cDVE, w=[t_p])
            if nxt:
                emit_beta(c, eb + 1, 1)
            if nxt:
                pend["wpre"] = (c, eb + 1)
            t_yt = k.op("vector", lambda e, eb=eb, ur=u32r[ep]:
                        e.scalar_tensor_tensor(
                out=ur, in0=ur, scalar=dskip_sb[:, eb:eb + 1],
                in1=h_so[:, 0:Tc], op0=ALU.mult, op1=ALU.add),
                ch=cDVE, w=[t_p])
            h_guard[0] = t_yt
            t_sz = k.dma("sync", szr[ep], sz_s[par, eb], chZR[ep],
                         w=[sz_free[ep], sp_done_z[par][eb]])
            t_yg = k.op("vector", lambda e, yt=u32r[ep], sr=szr[ep],
                        yg=yg16r[ep], ts=t_sz: (
                e.wait_ge(ts[0].sem, ts[1]),
                e.tensor_tensor(out=yg, in0=yt, in1=sr, op=ALU.mult))[1],
                ch=cDVE, w=[t_yt, yg_free[ep], sp_done_y[par][eb]])
            u32_free[ep] = t_yg
            sz_free[ep] = t_yg
            sp_done_y[par][eb] = k.dma("gpsimd", yg_s[par, eb], yg16r[ep],
                                       chYS[ep], w=[t_yg])
            yg_free[ep] = sp_done_y[par][eb]

        def emit_D(c):
            par = c % 2
            seq, j = c // CH, c % CH
            t0 = j * Tc
            for eb in range(EB):
                ep = eb % 2
                t_yl = k.dma("sync", yg16r[ep], yg_s[par, eb], chYR[ep],
                             w=[yg_free[ep], sp_done_y[par][eb]])
                t_wo = k.dma("sync", wot[ep], wo_s[eb], chWO[ep],
                             w=[wot_free[ep]])

                def mm_out(e, eb=eb, ep=ep, t_yl=t_yl, t_wo=t_wo,
                           bfs=tuple(bank_free[0:8])):
                    e.wait_ge(t_yl[0].sem, t_yl[1])
                    e.wait_ge(t_wo[0].sem, t_wo[1])
                    if eb == 0:
                        for bf in bfs:
                            if bf[0]:
                                e.wait_ge(bf[0].sem, bf[1])
                    last = None
                    wo3 = wot[ep].rearrange("p (a n) -> p a n", a=8)
                    for dq in range(8):
                        last = e.matmul(psum[dq], wo3[:, dq, :], yg16r[ep],
                                        start=(eb == 0), stop=(eb == EB - 1))
                    return last
                t_mo = k.op("tensor", mm_out, ch=cPE)
                wot_free[ep] = t_mo
                yg_free[ep] = t_mo
            t_proj = (cPE, cPE.n)
            for db in range(DBL):
                dp = db % 2
                t_x32 = k.dma("sync", x32r[dp],
                              x_fm[seq].rearrange("(b p) t -> p b t", p=128)[:, db, t0:t0 + Tc],
                              chX32[dp], w=[x32_free[dp]])
                t_r1 = k.op("vector", lambda e, db=db, dp=dp, tx=t_x32: (
                    e.wait_ge(tx[0].sem, tx[1]),
                    e.tensor_tensor(out=r13[:, db, :], in0=psum[db],
                                    in1=x32r[dp], op=ALU.add))[1],
                    ch=cDVE, w=[t_proj])
                bank_free[db] = t_r1
                x32_free[dp] = t_r1
            t_r1_all = (cDVE, cDVE.n)
            for db in range(DBL):
                dp = db % 2
                t_sq = k.op("scalar", lambda e, db=db, dp=dp: e.activation(
                    out=r1sqr[dp], in_=r13[:, db, :], func=AF.Square, scale=1.0),
                    ch=cACT, w=[t_r1_all, sq_free[dp]])

                def mm_stat(e, db=db, dp=dp, t_sq=t_sq,
                            bf0=bank_free[4], bf1=bank_free[5]):
                    e.wait_ge(t_sq[0].sem, t_sq[1])
                    if db == 0:
                        e.wait_ge(bf0[0].sem, bf0[1])
                        e.wait_ge(bf1[0].sem, bf1[1])
                    e.matmul(psum[4][0:1, :], ones_col, r13[:, db, :],
                             start=(db == 0), stop=(db == DBL - 1))
                    return e.matmul(psum[5][0:1, :], ones_col, r1sqr[dp],
                                    start=(db == 0), stop=(db == DBL - 1))
                ms = k.op("tensor", mm_stat, ch=cPE)
                bank_w[4] = bank_w[5] = ms
                sq_free[dp] = ms
            t_m = k.op("vector", lambda e: e.tensor_scalar(
                out=SR[0], in0=psum[4][0:1, :], scalar1=1.0 / D,
                scalar2=None, op0=ALU.mult), ch=cDVE, w=[bank_w[4], beta_free[0]])
            t_mq = k.op("vector", lambda e: e.tensor_scalar(
                out=SR[1], in0=psum[5][0:1, :], scalar1=1.0 / D,
                scalar2=None, op0=ALU.mult), ch=cDVE)
            bank_free[4] = t_m
            bank_free[5] = t_mq
            t_v = k.op("vector", lambda e: e.tensor_tensor(
                out=SR[2], in0=SR[0], in1=SR[0], op=ALU.mult),
                ch=cDVE, w=[t_mq, beta_free[1]])
            t_v2 = k.op("vector", lambda e: e.scalar_tensor_tensor(
                out=SR[1], in0=SR[2], scalar=-1.0, in1=SR[1],
                op0=ALU.mult, op1=ALU.add), ch=cDVE, w=[t_v])
            t_v3 = k.op("vector", lambda e: e.tensor_scalar(
                out=SR[1], in0=SR[1], scalar1=LN_EPS, scalar2=None,
                op0=ALU.add), ch=cDVE, w=[t_v2])
            t_l = k.op("scalar", lambda e: e.activation(
                out=SR[2], in_=SR[1], func=AF.Ln, scale=1.0),
                ch=cACT, w=[t_v3])
            t_rs = k.op("scalar", lambda e: e.activation(
                out=SR[2], in_=SR[2], func=AF.Exp, scale=-0.5),
                ch=cACT, w=[t_l])

            def mm_bc(e, t_m=t_m, t_rs=t_rs, bf2=bank_free[6], bf3=bank_free[7]):
                e.wait_ge(t_m[0].sem, t_m[1])
                e.wait_ge(t_rs[0].sem, t_rs[1])
                if bf2[0]:
                    e.wait_ge(bf2[0].sem, bf2[1])
                if bf3[0]:
                    e.wait_ge(bf3[0].sem, bf3[1])
                e.matmul(psum[6], ones_row, SR[0], start=True, stop=True)
                return e.matmul(psum[7], ones_row, SR[2], start=True, stop=True)
            mb = k.op("tensor", mm_bc, ch=cPE)
            bank_w[6] = bank_w[7] = mb
            for db in range(DBL):
                dp = db % 2
                t_a = k.op("vector", lambda e, db=db, dp=dp: e.tensor_tensor(
                    out=ln1r[dp], in0=r13[:, db, :], in1=psum[6], op=ALU.subtract),
                    ch=cDVE, w=[mb, ln1_free[dp]])
                t_b = k.op("vector", lambda e, dp=dp: e.tensor_tensor(
                    out=ln1r[dp], in0=ln1r[dp], in1=psum[7], op=ALU.mult),
                    ch=cDVE, w=[t_a])
                t_rd = k.op("vector", lambda e, dp=dp: e.tensor_reduce(
                    out=redt, in_=ln1r[dp], axis=AX.X, op=ALU.max,
                    apply_absolute_value=True), ch=cDVE, w=[t_b])
                k.op("vector", lambda e: e.tensor_tensor(
                    out=amax_acc, in0=amax_acc, in1=redt, op=ALU.max),
                    ch=cDVE, w=[t_rd])
                ln1_st[c][db] = k.dma("gpsimd", ln1_s[c, db], ln1r[dp], chLN[dp],
                                      w=[t_b])
                ln1_free[dp] = ln1_st[c][db]
            t_ln_last = (cDVE, cDVE.n)
            bank_free[6] = t_ln_last
            bank_free[7] = t_ln_last
            bc_guard[0] = t_ln_last
            bc_guard[1] = t_ln_last
            beta_free[0] = mb
            beta_free[1] = mb
            dt_free[0] = ln1_free[0]
            dt_free[1] = ln1_free[1]
            tmp_free[0] = sq_free[0]
            tmp_free[1] = sq_free[1]

        pend = {}
        ua_free = [NONE_W, NONE_W]
        ua_us_prev = [NONE_W, NONE_W]
        sza_free = [NONE_W, NONE_W]
        h_guard = [NONE_W]
        emit_A_begin(0)
        for eb in range(EB):
            emit_A_iter(0, eb)
        for c in range(NCH):
            if c % CH == 0:
                k.op("vector", lambda e: e.memset(carry, 0.0), ch=cDVE,
                     w=[(cDVE, cDVE.n)])
            dbc_guard_new = emit_B(c)
            if c + 1 < NCH:
                emit_A_begin(c + 1)
            emit_C_chain(c, 0)
            for eb in range(EB):
                if eb + 1 < EB:
                    emit_C_chain(c, eb + 1)
                emit_C_iter(c, eb)
                if c + 1 < NCH:
                    emit_A_iter(c + 1, eb)
            emit_D(c)
            dbc_guard = dbc_guard_new
        h_guard = h_guard[0]

        s1 = k.dma("sync", pm_st, amax_acc, chSC, w=[(cDVE, cDVE.n), s2])
        s2 = k.dma("sync", pmrow, pm_st, chSC, w=[s1])
        t3 = k.op("vector", lambda e: e.tensor_reduce(
            out=scal[0:1, 0:1], in_=pmrow, axis=AX.X, op=ALU.max),
            ch=cDVE, w=[s2])
        s3 = k.dma("sync", cc_in.ap(), scal[0:1, 0:1], chSC, w=[t3, s2])
        t5 = k.op("gpsimd", lambda e: e.collective_compute(
            "AllReduce", ALU.max, ins=[cc_in.ap()], outs=[cc_out.ap()],
            replica_groups=[list(range(8))]), ch=cCC, w=[s3])
        s4 = k.dma("sync", scal[0:1, 0:1], cc_out.ap(), chSC, w=[t5, s3])
        t7 = k.op("vector", lambda e: e.reciprocal(
            out=scal[0:1, 5:7], in_=scal[0:1, 0:2]), ch=cDVE, w=[s4])
        t8 = k.op("vector", lambda e: e.tensor_scalar(
            out=scal[0:1, 2:4], in0=scal[0:1, 5:7], scalar1=FP8_MAX,
            scalar2=None, op0=ALU.mult), ch=cDVE, w=[t7])
        t9 = k.op("vector", lambda e: e.tensor_scalar(
            out=scal[0:1, 2:4], in0=scal[0:1, 2:4], scalar1=0.5, scalar2=None,
            op0=ALU.mult), ch=cDVE, w=[t8])
        t10 = k.op("vector", lambda e: e.tensor_tensor(
            out=scal[0:1, 6:7], in0=scal[0:1, 2:3], in1=scal[0:1, 3:4],
            op=ALU.mult), ch=cDVE, w=[t9])
        t12 = k.op("vector", lambda e: e.reciprocal(
            out=scal[0:1, 4:5], in_=scal[0:1, 6:7]), ch=cDVE, w=[t10])
        s5 = k.dma("sync", sc_st, scal, chSC, w=[t12, s4])
        s6 = k.dma("sync", scalb, sc_st.broadcast_to([128, 8]), chSC, w=[s5])

        fwp = NONE_W
        for db in range(DBL):
            fwp = k.dma("sync", w1f[:, 0:D], fwt[db * 128:(db + 1) * 128, :],
                        chFW, w=[fwp, s6, h_guard, (cDVE, cDVE.n)])
            for hf in range(2):
                tb = k.op("vector", lambda e, hf=hf, ta=fwp: (
                    e.wait_ge(ta[0].sem, ta[1]),
                    e.tensor_scalar(out=f8t, in0=w1f[:, hf * Tc:(hf + 1) * Tc],
                                    scalar1=scalb[:, 3:4], scalar2=None,
                                    op0=ALU.mult))[1], ch=cDVE, w=[s6])
                k.op("vector", lambda e, db=db, hf=hf: e.tensor_copy(
                    out=fw83[:, db, hf * Tc:(hf + 1) * Tc], in_=f8t),
                    ch=cDVE, w=[tb])
        t_fw = (cDVE, cDVE.n)
        f8t_free = NONE_W

        for c in range(NCH):
            seq, j = c // CH, c % CH
            t0 = j * Tc
            for db in range(DBL):
                dp = db % 2
                ta = k.dma("sync", x32r[dp], ln1_s[c, db], chX32[dp],
                           w=[x32_free[dp], ln1_st[c][db],
                              t_fw if c == 0 and db < 2 else NONE_W])
                tb = k.op("vector", lambda e, dp=dp, ta=ta: (
                    e.wait_ge(ta[0].sem, ta[1]),
                    e.tensor_scalar(out=f8t, in0=x32r[dp], scalar1=scalb[:, 2:3],
                                    scalar2=None, op0=ALU.mult))[1], ch=cDVE,
                    w=[t_fw, f8t_free])
                tc2 = k.op("scalar", lambda e, db=db: e.copy(
                    out=fxq3[:, db, :], in_=f8t), ch=cACT, w=[tb])
                td = k.op("scalar", lambda e, db=db, dp=dp: e.copy(
                    out=r13[:, db, :], in_=x32r[dp]), ch=cACT, w=[tb])
                f8t_free = tc2
                x32_free[dp] = td
            t_q = (cACT, cACT.n)
            t_qd = (cDVE, cDVE.n)
            for db in range(DBL):
                dp = db % 2
                bo = 6 + dp

                def mm_ffn(e, db=db, bo=bo, t_q=t_q, t_qd=t_qd, bf=bank_free[bo]):
                    e.wait_ge(t_q[0].sem, t_q[1])
                    e.wait_ge(t_qd[0].sem, t_qd[1])
                    if bf[0]:
                        e.wait_ge(bf[0].sem, bf[1])
                    last = None
                    for dk in range(DBL):
                        last = e.matmul(psum[bo], fw83[:, dk, db * 128:(db + 1) * 128],
                                        fxq3[:, dk, :], start=(dk == 0),
                                        stop=(dk == DBL - 1))
                    return last
                mf = k.op("tensor", mm_ffn, ch=cPE)
                bank_w[bo] = mf
                t_e = k.op("vector", lambda e, bo=bo, db=db, dp=dp: e.tensor_scalar(
                    out=r1sqr[dp], in0=psum[bo], scalar1=scalb[:, 4:5],
                    scalar2=fb_sb[:, db:db + 1], op0=ALU.mult, op1=ALU.add),
                    ch=cDVE, w=[mf, sq_free[dp]])
                bank_free[bo] = t_e
                t_r2 = k.op("vector", lambda e, db=db, dp=dp: e.tensor_tensor(
                    out=ff3[:, db, :], in0=r13[:, db, :], in1=r1sqr[dp],
                    op=ALU.add), ch=cDVE, w=[t_e])
                sq_free[dp] = t_r2
            t_ff_all = (cDVE, cDVE.n)
            for db in range(DBL):
                dp = db % 2
                t_sq = k.op("scalar", lambda e, db=db, dp=dp: e.activation(
                    out=r1sqr[dp], in_=ff3[:, db, :], func=AF.Square, scale=1.0),
                    ch=cACT, w=[t_ff_all, sq_free[dp]])

                def mm_stat2(e, db=db, dp=dp, t_sq=t_sq,
                             bf0=bank_free[0], bf1=bank_free[1]):
                    e.wait_ge(t_sq[0].sem, t_sq[1])
                    if db == 0:
                        e.wait_ge(bf0[0].sem, bf0[1])
                        e.wait_ge(bf1[0].sem, bf1[1])
                    e.matmul(psum[0][0:1, :], ones_col, ff3[:, db, :],
                             start=(db == 0), stop=(db == DBL - 1))
                    return e.matmul(psum[1][0:1, :], ones_col, r1sqr[dp],
                                    start=(db == 0), stop=(db == DBL - 1))
                ms = k.op("tensor", mm_stat2, ch=cPE)
                bank_w[0] = bank_w[1] = ms
                sq_free[dp] = ms
            t_m = k.op("vector", lambda e: e.tensor_scalar(
                out=SR[0], in0=psum[0][0:1, :], scalar1=1.0 / D,
                scalar2=None, op0=ALU.mult), ch=cDVE, w=[bank_w[0]])
            t_mq = k.op("vector", lambda e: e.tensor_scalar(
                out=SR[1], in0=psum[1][0:1, :], scalar1=1.0 / D,
                scalar2=None, op0=ALU.mult), ch=cDVE)
            bank_free[0] = t_m
            bank_free[1] = t_mq
            t_v = k.op("vector", lambda e: e.tensor_tensor(
                out=SR[2], in0=SR[0], in1=SR[0], op=ALU.mult),
                ch=cDVE, w=[t_mq])
            t_v2 = k.op("vector", lambda e: e.scalar_tensor_tensor(
                out=SR[1], in0=SR[2], scalar=-1.0, in1=SR[1],
                op0=ALU.mult, op1=ALU.add), ch=cDVE, w=[t_v])
            t_v3 = k.op("vector", lambda e: e.tensor_scalar(
                out=SR[1], in0=SR[1], scalar1=LN_EPS, scalar2=None,
                op0=ALU.add), ch=cDVE, w=[t_v2])
            t_l = k.op("scalar", lambda e: e.activation(
                out=SR[2], in_=SR[1], func=AF.Ln, scale=1.0),
                ch=cACT, w=[t_v3])
            t_rs = k.op("scalar", lambda e: e.activation(
                out=SR[2], in_=SR[2], func=AF.Exp, scale=-0.5),
                ch=cACT, w=[t_l])

            def mm_bc2(e, t_m=t_m, t_rs=t_rs, bf2=bank_free[2], bf3=bank_free[3]):
                e.wait_ge(t_m[0].sem, t_m[1])
                e.wait_ge(t_rs[0].sem, t_rs[1])
                e.wait_ge(bf2[0].sem, bf2[1])
                e.wait_ge(bf3[0].sem, bf3[1])
                e.matmul(psum[2], ones_row, SR[0], start=True, stop=True)
                return e.matmul(psum[3], ones_row, SR[2], start=True, stop=True)
            mb = k.op("tensor", mm_bc2, ch=cPE)
            bank_w[2] = bank_w[3] = mb
            for db in range(DBL):
                dp = db % 2
                t_a = k.op("vector", lambda e, db=db, dp=dp: e.tensor_tensor(
                    out=ln1r[dp], in0=ff3[:, db, :], in1=psum[2], op=ALU.subtract),
                    ch=cDVE, w=[mb, ln1_free[dp]])
                t_b = k.op("vector", lambda e, dp=dp: e.tensor_tensor(
                    out=ln1r[dp], in0=ln1r[dp], in1=psum[3], op=ALU.mult),
                    ch=cDVE, w=[t_a])
                ln1_free[dp] = k.dma(
                    "gpsimd",
                    out_fm[seq].rearrange("(b p) t -> p b t", p=128)[:, db, t0:t0 + Tc],
                    ln1r[dp], chST[dp], w=[t_b])
            t_last = (cDVE, cDVE.n)
            bank_free[2] = t_last
            bank_free[3] = t_last

        k.ops["sync"].append(lambda e, a=chST[0].n, b=chST[1].n: (
            e.wait_ge(chST[0].sem, a), e.wait_ge(chST[1].sem, b)))

        with nc.Block() as block:
            @block.sync
            def _(e):
                for f in k.ops["sync"]:
                    f(e)

            @block.scalar
            def _(e):
                for f in k.ops["scalar"]:
                    f(e)

            @block.vector
            def _(e):
                for f in k.ops["vector"]:
                    f(e)

            @block.tensor
            def _(e):
                for f in k.ops["tensor"]:
                    f(e)

            @block.gpsimd
            def _(e):
                for f in k.ops["gpsimd"]:
                    f(e)

    return nc


def kernel(L0_attn_x, L0_attn_W_in, L0_attn_conv_w, L0_attn_conv_b,
           L0_attn_W_xproj, L0_attn_W_dt, L0_attn_b_dt, L0_attn_A_log,
           L0_attn_D_skip, L0_attn_W_out, L0_ffn_weight, L0_ffn_bias):
    x = np.asarray(L0_attn_x, dtype=np.float32)
    w_in = np.asarray(L0_attn_W_in, dtype=np.float32)
    conv_w = np.asarray(L0_attn_conv_w, dtype=np.float32)
    conv_b = np.asarray(L0_attn_conv_b, dtype=np.float32)
    w_xproj = np.asarray(L0_attn_W_xproj, dtype=np.float32)
    w_dt = np.asarray(L0_attn_W_dt, dtype=np.float32)
    b_dt = np.asarray(L0_attn_b_dt, dtype=np.float32)
    a_log = np.asarray(L0_attn_A_log, dtype=np.float32)
    d_skip = np.asarray(L0_attn_D_skip, dtype=np.float32)
    w_out = np.asarray(L0_attn_W_out, dtype=np.float32)
    fw = np.asarray(L0_ffn_weight, dtype=np.float32)
    fbias = np.asarray(L0_ffn_bias, dtype=np.float32)

    w_inT = np.ascontiguousarray(w_in.T)
    shared = {
        "w1t": np.ascontiguousarray(w_inT[:, :E]),
        "wzt": np.ascontiguousarray(w_inT[:, E:]),
        "cwrows": np.ascontiguousarray(conv_w.T),
        "cb": np.ascontiguousarray(conv_b.reshape(EB, 128).T),
        "wxt": np.ascontiguousarray(w_xproj.T),
        "wdtt": np.ascontiguousarray(w_dt.T),
        "bdt": np.ascontiguousarray(b_dt.reshape(EB, 128).T),
        "alog": np.ascontiguousarray(
            a_log.reshape(EB, 128, S).transpose(1, 0, 2).reshape(128, EB * S)),
        "dskip": np.ascontiguousarray(d_skip.reshape(EB, 128).T),
        "woutt": np.ascontiguousarray(w_out.T),
        "fwt": np.ascontiguousarray(fw.T),
        "fb": np.ascontiguousarray(fbias.reshape(DBL, 128).T),
    }
    in_maps = []
    for c in range(8):
        xc = np.ascontiguousarray(x[2 * c:2 * c + 2].transpose(0, 2, 1))
        m = {"x_fm": xc}
        m.update(shared)
        in_maps.append(m)

    nc = build_nc()
    res = run_bass_kernel_spmd(nc, in_maps, core_ids=list(range(8)))
    out = np.empty((B_TOT, T, D), dtype=np.float32)
    for c in range(8):
        o = np.asarray(res.results[c]["out_fm"])
        out[2 * c:2 * c + 2] = o.transpose(0, 2, 1)
    return out


# version 14
# speedup vs baseline: 1.1343x; 1.0131x over previous
import contextlib
import numpy as np
import concourse.bass as bass
import concourse.mybir as mybir
from concourse.bass_utils import run_bass_kernel_spmd

F32 = mybir.dt.float32
BF16 = mybir.dt.bfloat16
FP8 = mybir.dt.float8e4
AF = mybir.ActivationFunctionType
ALU = mybir.AluOpType
AX = mybir.AxisListType

B_TOT, T, D = 16, 4096, 1024
E = 2048
S = 16
R = 64
Tc = 512
EB = E // 128
DBL = D // 128
SEQ_PER_CORE = 2
CH = T // Tc
NCH = SEQ_PER_CORE * CH
FP8_MAX = 448.0
LN_EPS = 1e-5


class Ch:
    def __init__(self, sem):
        self.sem = sem
        self.n = 0


class KB:

    ENGS = ("sync", "scalar", "vector", "tensor", "gpsimd")

    def __init__(self):
        self.ops = {e: [] for e in self.ENGS}

    def dma(self, eng, out, in_, ch, w=()):
        ch.n += 16
        t = ch.n
        waits = tuple((c.sem, v) for (c, v) in w if c is not None)
        sem = ch.sem

        def f(e, out=out, in_=in_, sem=sem, waits=waits):
            for (s, v) in waits:
                e.wait_ge(s, v)
            e.dma_start(out=out, in_=in_).then_inc(sem, 16)
        self.ops[eng].append(f)
        return (ch, t)

    def op(self, eng, fn, ch=None, w=()):
        t = None
        sem = None
        if ch is not None:
            ch.n += 1
            t = ch.n
            sem = ch.sem
        waits = tuple((c.sem, v) for (c, v) in w if c is not None)

        def f(e, fn=fn, sem=sem, waits=waits):
            for (s, v) in waits:
                e.wait_ge(s, v)
            i = fn(e)
            if sem is not None:
                i.then_inc(sem, 1)
        self.ops[eng].append(f)
        return (ch, t)


NONE_W = (None, 0)


def build_nc():
    nc = bass.Bass("TRN2", target_bir_lowering=False)

    x_fm = nc.dram_tensor("x_fm", [SEQ_PER_CORE, D, T], F32, kind="ExternalInput").ap()
    w1t = nc.dram_tensor("w1t", [D, E], F32, kind="ExternalInput").ap()
    wzt = nc.dram_tensor("wzt", [D, E], F32, kind="ExternalInput").ap()
    cwrows = nc.dram_tensor("cwrows", [4, E], F32, kind="ExternalInput").ap()
    cb = nc.dram_tensor("cb", [128, EB], F32, kind="ExternalInput").ap()
    wxt = nc.dram_tensor("wxt", [E, 96], F32, kind="ExternalInput").ap()
    wdtt = nc.dram_tensor("wdtt", [R, E], F32, kind="ExternalInput").ap()
    bdt = nc.dram_tensor("bdt", [128, EB], F32, kind="ExternalInput").ap()
    alog = nc.dram_tensor("alog", [128, EB * S], F32, kind="ExternalInput").ap()
    dskip = nc.dram_tensor("dskip", [128, EB], F32, kind="ExternalInput").ap()
    woutt = nc.dram_tensor("woutt", [E, D], F32, kind="ExternalInput").ap()
    fwt = nc.dram_tensor("fwt", [D, D], F32, kind="ExternalInput").ap()
    fb = nc.dram_tensor("fb", [128, DBL], F32, kind="ExternalInput").ap()
    out_fm = nc.dram_tensor("out_fm", [SEQ_PER_CORE, D, T], F32, kind="ExternalOutput").ap()

    wk_s = nc.dram_tensor("wk_s", [EB, 2, 128, DBL // 2, 4, 128], BF16, kind="Internal").ap()
    wz_s = nc.dram_tensor("wz_s", [EB, 128, DBL, 128], BF16, kind="Internal").ap()
    wo_s = nc.dram_tensor("wo_s", [EB, 128, D], BF16, kind="Internal").ap()
    u32_s = nc.dram_tensor("u32_s", [2, EB, 128, Tc], F32, kind="Internal").ap()
    sz_s = nc.dram_tensor("sz_s", [2, EB, 128, Tc], F32, kind="Internal").ap()
    yg_s = nc.dram_tensor("yg_s", [2, EB, 128, Tc], BF16, kind="Internal").ap()
    u16_s = nc.dram_tensor("u16_s", [2, EB, 128, Tc], BF16, kind="Internal").ap()
    ln1_s = nc.dram_tensor("ln1_s", [NCH, DBL, 128, Tc], F32, kind="Internal").ap()
    bc_st = nc.dram_tensor("bc_st", [2, 2, S * Tc], F32, kind="Internal").ap()
    c_il = nc.dram_tensor("c_il", [2, S * Tc], F32, kind="Internal").ap()
    pm_st = nc.dram_tensor("pm_st", [1, 128], F32, kind="Internal").ap()
    sc_st = nc.dram_tensor("sc_st", [1, 8], F32, kind="Internal").ap()
    cc_in = nc.dram_tensor("cc_in", [1, 1], F32, kind="Internal")
    cc_out = nc.dram_tensor("cc_out", [1, 1], F32, kind="Internal")

    k = KB()

    with contextlib.ExitStack() as stk:
        def sb(name, shape, dt=F32):
            return stk.enter_context(nc.sbuf_tensor(name, shape, dt)).ap()

        cb_sb = sb("cb_sb", [128, EB])
        ncb_sb = sb("ncb_sb", [128, EB])
        bdt_sb = sb("bdt_sb", [128, EB])
        dskip_sb = sb("dskip_sb", [128, EB])
        a_sb = sb("a_sb", [128, EB * S])
        fb_sb = sb("fb_sb", [128, DBL])
        ones_row = sb("ones_row", [1, 128])
        ones_col = sb("ones_col", [128, 1])
        wxt_sb = sb("wxt_sb", [128, EB * 96], BF16)
        wdt_sb = sb("wdt_sb", [64, E], BF16)
        xb = [sb(f"xb{p}", [128, DBL * (Tc + 3)], BF16) for p in range(2)]
        u32r = [sb(f"u32r{i}", [128, Tc]) for i in range(2)]
        ua32r = [sb(f"ua32r{i}", [128, Tc]) for i in range(2)]
        szar = [sb(f"szar{i}", [128, Tc]) for i in range(2)]
        u16r = [sb(f"u16r{i}", [128, Tc], BF16) for i in range(2)]
        dbc_ev = sb("dbc_ev", [96, Tc])
        cil_ev = sb("cil_ev", [128, 4 * S])
        dtlo16 = sb("dtlo16", [64, Tc], BF16)
        dt32r = [sb(f"dt32r{i}", [128, Tc]) for i in range(2)]
        tmpr = [sb(f"tmpr{i}", [128, Tc]) for i in range(2)]
        wtile = [sb(f"wt{i}", [128, Tc]) for i in range(2)]
        dar = [sb(f"dar{i}", [128, Tc]) for i in range(3 - 1)]
        betar = [sb(f"betar{i}", [128, 2 * Tc]) for i in range(2)]
        h_so = sb("h_so", [128, S * Tc])
        bb = sb("bb", [128, S * Tc])
        cbb = sb("cbb", [128, S * Tc])
        carry = sb("carry", [128, EB * S])
        szr = [sb(f"szr{i}", [128, Tc]) for i in range(2)]
        yg16r = [sb(f"yg16r{i}", [128, Tc], BF16) for i in range(2)]
        wkt = [sb(f"wkt{i}", [128, (DBL // 2) * 4 * 128], BF16) for i in range(2)]
        wzt_r = [sb(f"wztr{i}", [128, DBL * 128], BF16) for i in range(2)]
        wot = [sb(f"wot{i}", [128, 8 * 128], BF16) for i in range(2)]
        x32r = [sb(f"x32r{i}", [128, Tc]) for i in range(2)]
        r1 = sb("r1", [128, DBL * Tc])
        r1sqr = tmpr
        ln1r = dt32r
        amax_acc = sb("amax_acc", [128, 1])
        redt = sb("redt", [128, 1])
        pmrow = sb("pmrow", [1, 128])
        scal = sb("scal", [1, 8])
        scalb = sb("scalb", [128, 8])
        f8t = dar[0].bitcast(FP8)[:, 0:Tc]
        SR = [betar[0][0:1, 0:Tc], betar[0][0:1, Tc:2 * Tc],
              betar[1][0:1, 0:Tc]]

        w1f = h_so[:, 0:E]
        cwb = h_so[:, E:2 * E]
        wtmp16 = bb.bitcast(BF16)[:, 0:E]
        fw8 = bb.bitcast(BF16)[:, 0:DBL * D]
        fw83 = fw8.rearrange("p (b n) -> p b n", n=D)
        fxq3 = cbb.bitcast(BF16)[:, 0:DBL * Tc].rearrange("p (b n) -> p b n", n=Tc)
        ff3 = h_so[:, 0:DBL * Tc].rearrange("p (b n) -> p b n", n=Tc)

        psum = [stk.enter_context(nc.psum_tensor(f"pb{i}", [128, Tc], F32)).ap()
                for i in range(8)]

        sems = [stk.enter_context(nc.semaphore(f"sm{i}")) for i in range(48)]
        _it = iter(sems)

        def mkch():
            return Ch(next(_it))

        cPE, cACT, cDVE, cCC = mkch(), mkch(), mkch(), mkch()
        chP0L = mkch()
        chP0S = mkch()
        chXB = [mkch(), mkch()]
        chU16 = [mkch(), mkch()]
        chU16L = [mkch(), mkch()]
        chWK = [mkch(), mkch()]
        chWZ = [mkch(), mkch()]
        chWO = [mkch(), mkch()]
        chUS = [mkch(), mkch()]
        chZS = [mkch(), mkch()]
        chYS = [mkch(), mkch()]
        chUR = [mkch(), mkch()]
        chZR = [mkch(), mkch()]
        chYR = [mkch(), mkch()]
        chX32 = [mkch(), mkch()]
        chLN = [mkch(), mkch()]
        chBS = [mkch(), mkch()]
        chCS = [mkch(), mkch()]
        chBB = mkch()
        chCB = mkch()
        chSC = mkch()
        chFW = mkch()
        chST = [mkch(), mkch()]

        bank_w = [NONE_W] * 8
        bank_free = [NONE_W] * 8

        p0 = NONE_W
        for (dst, src) in ((cb_sb, cb), (bdt_sb, bdt), (dskip_sb, dskip),
                           (a_sb, alog), (fb_sb, fb)):
            p0 = k.dma("sync", dst, src, chP0L, w=[p0])
        t_wres = k.dma("gpsimd", wxt_sb.rearrange("p (b n) -> p b n", n=96),
                       wxt.rearrange("(b p) n -> p b n", p=128), chP0S)
        t_wres = k.dma("gpsimd", wdt_sb, wdtt, chP0S, w=[t_wres])
        t_aexp = k.op("scalar", lambda e: e.activation(out=a_sb, in_=a_sb,
                                                       func=AF.Exp, scale=1.0),
                      ch=cACT, w=[p0])
        k.op("vector", lambda e: e.tensor_scalar(out=a_sb, in0=a_sb, scalar1=-1.0,
                                                 scalar2=None, op0=ALU.mult),
             ch=cDVE, w=[t_aexp])
        k.op("vector", lambda e: e.tensor_scalar(out=ncb_sb, in0=cb_sb,
                                                 scalar1=-1.0, scalar2=None,
                                                 op0=ALU.mult), ch=cDVE, w=[p0])
        k.op("vector", lambda e: e.memset(ones_row, 1.0), ch=cDVE)
        k.op("vector", lambda e: e.memset(ones_col, 1.0), ch=cDVE)
        k.op("vector", lambda e: e.memset(carry, 0.0), ch=cDVE)
        k.op("vector", lambda e: e.memset(amax_acc, 0.0), ch=cDVE)
        k.op("vector", lambda e: e.memset(redt, 0.0), ch=cDVE)

        st_p0 = t_wres
        for db in range(DBL):
            p0 = k.dma("sync", w1f, w1t[db * 128:(db + 1) * 128, :], chP0L,
                       w=[p0, st_p0])
            for tap in range(4):
                p0 = k.dma("sync", cwb, cwrows[tap:tap + 1, :].broadcast_to([128, E]),
                           chP0L, w=[p0, (cDVE, cDVE.n)])
                tt = k.op("vector", lambda e: e.tensor_tensor(
                    out=wtmp16, in0=w1f, in1=cwb, op=ALU.mult),
                    ch=cDVE, w=[p0, st_p0])
                half, dbh = db // (DBL // 2), db % (DBL // 2)
                st_p0 = k.dma("gpsimd",
                              wk_s[:, half, :, dbh, tap, :].transpose([1, 0, 2]),
                              wtmp16.rearrange("p (b n) -> p b n", n=128), chP0S,
                              w=[tt, st_p0])
            p0 = k.dma("sync", w1f, wzt[db * 128:(db + 1) * 128, :], chP0L,
                       w=[p0, st_p0])
            tt2 = k.op("vector", lambda e: e.tensor_copy(out=wtmp16, in_=w1f),
                       ch=cDVE, w=[p0])
            st_p0 = k.dma("gpsimd", wz_s[:, :, db, :].transpose([1, 0, 2]),
                          wtmp16.rearrange("p (b n) -> p b n", n=128), chP0S,
                          w=[tt2, st_p0])
        for eb in range(EB):
            p0 = k.dma("sync", w1f[:, 0:D], woutt[eb * 128:(eb + 1) * 128, :],
                       chP0L, w=[p0, st_p0])
            tt = k.op("vector", lambda e: e.tensor_copy(out=wtmp16[:, 0:D],
                                                        in_=w1f[:, 0:D]),
                      ch=cDVE, w=[p0])
            st_p0 = k.dma("gpsimd", wo_s[eb], wtmp16[:, 0:D],
                          chP0S, w=[tt, st_p0])
        for db in range(DBL):
            p0 = k.dma("sync", w1f[:, 0:D], fwt[db * 128:(db + 1) * 128, :],
                       chP0L, w=[p0, st_p0, (cDVE, cDVE.n)])
            tr = k.op("vector", lambda e: e.tensor_reduce(
                out=redt, in_=w1f[:, 0:D], axis=AX.X, op=ALU.max,
                apply_absolute_value=True), ch=cDVE, w=[p0])
            if db == 0:
                k.op("vector", lambda e: e.tensor_copy(out=amax_acc, in_=redt),
                     ch=cDVE, w=[tr])
            else:
                k.op("vector", lambda e: e.tensor_tensor(
                    out=amax_acc, in0=amax_acc, in1=redt, op=ALU.max),
                    ch=cDVE, w=[tr])
        s1 = k.dma("sync", pm_st, amax_acc, chSC, w=[(cDVE, cDVE.n)])
        s2 = k.dma("sync", pmrow, pm_st, chSC, w=[s1])
        k.op("vector", lambda e: e.tensor_reduce(out=scal[0:1, 1:2], in_=pmrow,
                                                 axis=AX.X, op=ALU.max),
             ch=cDVE, w=[s2])
        k.op("vector", lambda e: e.memset(amax_acc, 0.0), ch=cDVE,
             w=[(cDVE, cDVE.n)])
        p0_all = (chP0S, chP0S.n)

        xb_guard = [NONE_W, NONE_W]
        u32_free = [NONE_W, NONE_W]
        u16_free = [NONE_W, NONE_W]
        dt_free = [NONE_W, NONE_W]
        tmp_free = [NONE_W, NONE_W]
        w_free = [NONE_W, NONE_W]
        dar_free = [NONE_W] * 2
        beta_free = [NONE_W, NONE_W]
        sz_free = [NONE_W, NONE_W]
        yg_free = [NONE_W, NONE_W]
        wkt_free = [p0_all, p0_all]
        wzt_free = [p0_all, p0_all]
        wot_free = [p0_all, p0_all]
        x32_free = [NONE_W, NONE_W]
        sq_free = [NONE_W, NONE_W]
        ln1_free = [NONE_W, NONE_W]
        sp_done_u = [[NONE_W] * EB, [NONE_W] * EB]
        sp16_done = [[NONE_W] * EB, [NONE_W] * EB]
        sp_done_z = [[NONE_W] * EB, [NONE_W] * EB]
        sp_done_y = [[NONE_W] * EB, [NONE_W] * EB]
        ln1_st = [[NONE_W] * DBL for _ in range(NCH)]
        bc_guard = [NONE_W, NONE_W]
        bc_rdB = [NONE_W, NONE_W]
        cil_guard = [NONE_W]
        bc_rdC = [NONE_W, NONE_W]
        dbc_guard = NONE_W
        h_guard = NONE_W
        r13 = r1.rearrange("p (b n) -> p b n", n=Tc)

        a_t_x = [NONE_W, NONE_W]

        def emit_A_begin(c):
            par = c % 2
            seq, j = c // CH, c % CH
            t0 = j * Tc
            xw3 = xb[par].rearrange("p (b n) -> p b n", n=Tc + 3)
            if j == 0:
                tz = k.op("gpsimd", lambda e, xw3=xw3: e.memset(xw3[:, :, 0:3], 0.0),
                          ch=cCC, w=[xb_guard[par]])
                a_t_x[par] = k.dma("gpsimd", xw3[:, :, 3:Tc + 3],
                                   x_fm[seq].rearrange("(b p) t -> p b t", p=128)[:, :, t0:t0 + Tc],
                                   chXB[par], w=[tz])
            else:
                a_t_x[par] = k.dma("gpsimd", xw3[:, :, 0:Tc + 3],
                                   x_fm[seq].rearrange("(b p) t -> p b t", p=128)[:, :, t0 - 3:t0 + Tc],
                                   chXB[par], w=[xb_guard[par]])

        def emit_A_iter(c, eb):
            par = c % 2
            ep = eb % 2
            bk = ep
            bz = 2 + ep
            t_x = a_t_x[par]
            xw3 = xb[par].rearrange("p (b n) -> p b n", n=Tc + 3)
            t_wk = [None, None]
            for half in range(2):
                t_wk[half] = k.dma("sync", wkt[half],
                                   wk_s[eb, half].rearrange("p a b n -> p (a b n)"),
                                   chWK[half], w=[wkt_free[half]])
            t_wz = k.dma("sync", wzt_r[ep], wz_s[eb].rearrange("p b n -> p (b n)"),
                         chWZ[ep], w=[wzt_free[ep]])

            def mm_xc(e, bk=bk, t_wk=tuple(t_wk), t_x=t_x,
                      bf=bank_free[bk], xw3=xw3):
                e.wait_ge(t_x[0].sem, t_x[1])
                if bf[0]:
                    e.wait_ge(bf[0].sem, bf[1])
                last = None
                for half in range(2):
                    e.wait_ge(t_wk[half][0].sem, t_wk[half][1])
                    w4 = wkt[half].rearrange("p (a b n) -> p a b n",
                                             a=DBL // 2, b=4)
                    for dbh in range(DBL // 2):
                        db = half * (DBL // 2) + dbh
                        for tap in range(4):
                            last = e.matmul(
                                psum[bk], w4[:, dbh, tap, :],
                                xw3[:, db, tap:tap + Tc],
                                start=(half == 0 and dbh == 0 and tap == 0),
                                stop=(half == 1 and dbh == DBL // 2 - 1 and tap == 3))
                return last
            mw = k.op("tensor", mm_xc, ch=cPE)
            bank_w[bk] = mw
            wkt_free[0] = wkt_free[1] = mw
            t_s1 = k.op("scalar", lambda e, bk=bk, eb=eb, ur=ua32r[ep]: e.activation(
                out=ur, in_=psum[bk], func=AF.Exp, bias=ncb_sb[:, eb:eb + 1],
                scale=-1.0), ch=cACT,
                w=[mw, ua_free[ep], ua_us_prev[ep], sp_done_u[par][eb]])
            k.op("scalar", lambda e, ur=ua32r[ep]: e.activation(
                out=ur, in_=ur, func=AF.Ln, bias=1.0, scale=1.0), ch=cACT)
            t_s4 = k.op("scalar", lambda e, ur=ua32r[ep]: e.activation(
                out=ur, in_=ur, func=AF.Exp, scale=-1.0), ch=cACT)
            t_u = k.op("vector", lambda e, bk=bk, eb=eb, ur=ua32r[ep]:
                       e.scalar_tensor_tensor(
                out=ur, in0=psum[bk], scalar=cb_sb[:, eb:eb + 1], in1=ur,
                op0=ALU.add, op1=ALU.mult), ch=cDVE, w=[t_s4])
            bank_free[bk] = t_u
            sp_done_u[par][eb] = k.dma("gpsimd", u32_s[par, eb], ua32r[ep],
                                       chUS[ep], w=[t_u])
            sp16_done[par][eb] = k.dma("gpsimd", u16_s[par, eb], ua32r[ep],
                                       chU16[ep], w=[t_u, sp16_done[par][eb]])
            ua_us_prev[ep] = sp_done_u[par][eb]
            ua_free[ep] = sp16_done[par][eb]

            def mm_z(e, bz=bz, ep=ep, t_wz=t_wz, t_x=t_x,
                     bf=bank_free[bz], xw3=xw3):
                e.wait_ge(t_x[0].sem, t_x[1])
                e.wait_ge(t_wz[0].sem, t_wz[1])
                if bf[0]:
                    e.wait_ge(bf[0].sem, bf[1])
                last = None
                wz3 = wzt_r[ep].rearrange("p (a n) -> p a n", a=DBL)
                for db in range(DBL):
                    last = e.matmul(psum[bz], wz3[:, db, :], xw3[:, db, 3:Tc + 3],
                                    start=(db == 0), stop=(db == DBL - 1))
                return last
            mz = k.op("tensor", mm_z, ch=cPE)
            bank_w[bz] = mz
            wzt_free[ep] = mz
            k.op("scalar", lambda e, bz=bz, sr=szar[ep]: e.activation(
                out=sr, in_=psum[bz], func=AF.Exp, scale=-1.0), ch=cACT,
                w=[mz, sza_free[ep], sp_done_z[par][eb]])
            k.op("scalar", lambda e, sr=szar[ep]: e.activation(
                out=sr, in_=sr, func=AF.Ln, bias=1.0, scale=1.0), ch=cACT)
            t_z4 = k.op("scalar", lambda e, sr=szar[ep]: e.activation(
                out=sr, in_=sr, func=AF.Exp, scale=-1.0), ch=cACT)
            t_z = k.op("vector", lambda e, bz=bz, sr=szar[ep]: e.tensor_tensor(
                out=sr, in0=psum[bz], in1=sr, op=ALU.mult), ch=cDVE, w=[t_z4])
            bank_free[bz] = t_z
            sp_done_z[par][eb] = k.dma("gpsimd", sz_s[par, eb], szar[ep],
                                       chZS[ep], w=[t_z])
            sza_free[ep] = sp_done_z[par][eb]
            if eb == EB - 1:
                xb_guard[par] = (cPE, cPE.n)

        def emit_B(c):
            par = c % 2
            for eb in range(EB):
                ep = eb % 2
                tl = k.dma("sync", u16r[ep], u16_s[par, eb], chU16L[ep],
                           w=[u16_free[ep], sp16_done[par][eb]])

                def mm_one(e, eb=eb, ep=ep, tl=tl, bf=bank_free[4],
                           bf5=bank_free[5]):
                    e.wait_ge(tl[0].sem, tl[1])
                    if eb == 0:
                        if bf[0]:
                            e.wait_ge(bf[0].sem, bf[1])
                        if bf5[0]:
                            e.wait_ge(bf5[0].sem, bf5[1])
                    e.matmul(psum[4][0:96, :],
                             wxt_sb.rearrange("p (b n) -> p b n", n=96)[:, eb, :],
                             u16r[ep], start=(eb == 0), stop=(eb == EB - 1))
                    last = None
                    for tb in range(4):
                        last = e.matmul(
                            psum[5][:, tb * 96:(tb + 1) * 96],
                            u16r[ep][:, tb * 128:(tb + 1) * 128],
                            wxt_sb.rearrange("p (b n) -> p b n", n=96)[:, eb, :],
                            start=(eb == 0), stop=(eb == EB - 1))
                    return last
                mo = k.op("tensor", mm_one, ch=cPE)
                bank_w[4] = bank_w[5] = mo
                u16_free[ep] = mo
            t_lo = k.op("scalar", lambda e: e.copy(out=dtlo16,
                                                   in_=psum[4][0:64, :]),
                        ch=cACT, w=[bank_w[4]])
            t_bc = k.op("scalar", lambda e: e.copy(out=dbc_ev[64:80, :],
                                                   in_=psum[4][64:80, :]),
                        ch=cACT, w=[t_lo, dbc_guard])
            bank_free[4] = t_bc
            t_ce = k.op("scalar", lambda e: e.copy(
                out=cil_ev.rearrange("p (a b) -> p a b", b=S),
                in_=psum[5][:, 0:384].rearrange("p (a b) -> p a b", b=96)[:, :, 80:96]),
                ch=cACT, w=[cil_guard[0]])
            bank_free[5] = t_ce
            t_sb = k.dma("gpsimd", bc_st[par, 0].rearrange("(a n) -> a n", n=Tc),
                         dbc_ev[64:80, :], chBS[par], w=[t_bc, bc_rdB[par]])
            t_sc = k.dma("gpsimd",
                         c_il[par].rearrange("(tb p s) -> p tb s", p=128, s=S),
                         cil_ev.rearrange("p (a b) -> p a b", b=S),
                         chCS[par], w=[t_ce, bc_rdC[par]])
            t_bb = k.dma("sync", bb, bc_st[par, 0].unsqueeze(0)
                         .broadcast_to([128, S * Tc]), chBB,
                         w=[t_sb, bc_guard[0]])
            t_cb = k.dma("sync", cbb, c_il[par].unsqueeze(0)
                         .broadcast_to([128, S * Tc]), chCB,
                         w=[t_sc, bc_guard[1]])
            bc_rdB[par] = t_bb
            bc_rdC[par] = t_cb
            cil_guard[0] = t_sc
            pend["t_lo"] = t_lo
            pend["t_bb"] = t_bb
            pend["t_cb"] = t_cb
            return t_sc

        c_e3 = [NONE_W, NONE_W]

        def emit_C_chain(c, eb):
            par = c % 2
            ep = eb % 2
            bd = 5 - ep
            t_lo = pend["t_lo"]

            def mm_dt(e, bd=bd, eb=eb, t_lo=t_lo, bf=bank_free[bd]):
                e.wait_ge(t_lo[0].sem, t_lo[1])
                if bf[0]:
                    e.wait_ge(bf[0].sem, bf[1])
                return e.matmul(psum[bd], wdt_sb[:, eb * 128:(eb + 1) * 128],
                                dtlo16, start=True, stop=True)
            md = k.op("tensor", mm_dt, ch=cPE)
            bank_w[bd] = md
            t_e1 = k.op("scalar", lambda e, bd=bd, eb=eb, tm=tmpr[ep]: e.activation(
                out=tm, in_=psum[bd], func=AF.Exp, bias=bdt_sb[:, eb:eb + 1],
                scale=1.0), ch=cACT, w=[md, tmp_free[ep]])
            bank_free[bd] = t_e1
            t_e3 = k.op("scalar", lambda e, tm=tmpr[ep], dt=dt32r[ep]: e.activation(
                out=dt, in_=tm, func=AF.Ln, bias=1.0, scale=1.0), ch=cACT,
                w=[dt_free[ep]])
            tmp_free[ep] = t_e3
            c_e3[ep] = t_e3

        c_w = [NONE_W, NONE_W]

        def emit_w(c, eb):
            par = c % 2
            ep = eb % 2
            t_u32 = k.dma("sync", u32r[ep], u32_s[par, eb], chUR[ep],
                          w=[u32_free[ep], sp_done_u[par][eb]])
            c_w[ep] = k.op("vector", lambda e, dt=dt32r[ep], ur=u32r[ep],
                           wt=wtile[ep], tu=t_u32: (
                e.wait_ge(tu[0].sem, tu[1]),
                e.tensor_tensor(out=wt, in0=dt, in1=ur, op=ALU.mult))[1],
                ch=cDVE, w=[c_e3[ep], w_free[ep]])

        def emit_beta(c, eb, pr):
            ep = eb % 2
            bb3 = bb.rearrange("p (s n) -> p s n", n=Tc)
            t_bb = pend["t_bb"]
            k.op("vector", lambda e, pr=pr, wt=wtile[ep],
                 slot=pr % 2, t_bb=t_bb, bb3=bb3: (
                e.wait_ge(t_bb[0].sem, t_bb[1]),
                e.tensor_tensor(
                    out=betar[slot].rearrange("p (a n) -> p a n", n=Tc),
                    in0=wt.unsqueeze(1).broadcast_to([128, 2, Tc]),
                    in1=bb3[:, 2 * pr:2 * pr + 2, :],
                    op=ALU.mult))[1],
                ch=cDVE, w=[c_w[ep], beta_free[pr % 2]])

        def emit_C_iter(c, eb):
            par = c % 2
            ep = eb % 2
            t_bb, t_cb = pend["t_bb"], pend["t_cb"]
            bb3 = bb.rearrange("p (s n) -> p s n", n=Tc)
            t_e3 = c_e3[ep]
            if pend.get("wpre") != (c, eb):
                emit_w(c, eb)
                emit_beta(c, eb, 0)
                emit_beta(c, eb, 1)
            t_scan = NONE_W
            for s_i in range(S):
                sp_ = s_i % 2
                pr = s_i // 2
                if sp_ == 0 and pr >= 2:
                    emit_beta(c, eb, pr)
                t_da = k.op("scalar", lambda e, s_i=s_i, eb=eb, dt=dt32r[ep],
                            d=dar[s_i % 2]: e.activation(
                    out=d, in_=dt, func=AF.Exp,
                    scale=a_sb[:, eb * S + s_i:eb * S + s_i + 1]),
                    ch=cACT, w=[t_e3, dar_free[s_i % 2]])
                t_scan = k.op("vector", lambda e, s_i=s_i, eb=eb, sp_=sp_,
                              pr=pr, d=dar[s_i % 2]: e.tensor_tensor_scan(
                    out=h_so.rearrange("p (n s) -> p n s", s=S)[:, :, s_i],
                    data0=d,
                    data1=betar[pr % 2][:, sp_ * Tc:(sp_ + 1) * Tc],
                    initial=carry[:, eb * S + s_i:eb * S + s_i + 1],
                    op0=ALU.mult, op1=ALU.add),
                    ch=cDVE, w=[t_da] + ([h_guard[0]] if s_i == 0 else []))
                dar_free[s_i % 2] = t_scan
                if sp_ == 1:
                    beta_free[pr % 2] = t_scan
            dt_free[ep] = t_scan
            w_free[ep] = t_scan
            t_cu = k.op("vector", lambda e, eb=eb: e.tensor_copy(
                out=carry[:, eb * S:(eb + 1) * S],
                in_=h_so[:, (Tc - 1) * S:Tc * S]),
                ch=cDVE, w=[t_scan])
            nxt = eb + 1 < EB
            t_p = k.op("vector", lambda e, t_cb=t_cb: (
                e.wait_ge(t_cb[0].sem, t_cb[1]),
                e.tensor_tensor(out=h_so, in0=h_so, in1=cbb, op=ALU.mult))[1],
                ch=cDVE, w=[t_cu])
            if nxt:
                emit_w(c, eb + 1)
                emit_beta(c, eb + 1, 0)
            t_p = k.op("vector", lambda e: e.tensor_reduce(
                out=h_so[:, 0:Tc], in_=h_so.rearrange("p (n s) -> p n s", s=S),
                axis=AX.X, op=ALU.add), ch=cDVE, w=[t_p])
            if nxt:
                emit_beta(c, eb + 1, 1)
            if nxt:
                pend["wpre"] = (c, eb + 1)
            t_yt = k.op("vector", lambda e, eb=eb, ur=u32r[ep]:
                        e.scalar_tensor_tensor(
                out=ur, in0=ur, scalar=dskip_sb[:, eb:eb + 1],
                in1=h_so[:, 0:Tc], op0=ALU.mult, op1=ALU.add),
                ch=cDVE, w=[t_p])
            h_guard[0] = t_yt
            t_sz = k.dma("sync", szr[ep], sz_s[par, eb], chZR[ep],
                         w=[sz_free[ep], sp_done_z[par][eb]])
            t_yg = k.op("vector", lambda e, yt=u32r[ep], sr=szr[ep],
                        yg=yg16r[ep], ts=t_sz: (
                e.wait_ge(ts[0].sem, ts[1]),
                e.tensor_tensor(out=yg, in0=yt, in1=sr, op=ALU.mult))[1],
                ch=cDVE, w=[t_yt, yg_free[ep], sp_done_y[par][eb]])
            u32_free[ep] = t_yg
            sz_free[ep] = t_yg
            sp_done_y[par][eb] = k.dma("gpsimd", yg_s[par, eb], yg16r[ep],
                                       chYS[ep], w=[t_yg])
            yg_free[ep] = sp_done_y[par][eb]

        def emit_D(c):
            par = c % 2
            seq, j = c // CH, c % CH
            t0 = j * Tc
            for eb in range(EB):
                ep = eb % 2
                t_yl = k.dma("sync", yg16r[ep], yg_s[par, eb], chYR[ep],
                             w=[yg_free[ep], sp_done_y[par][eb]])
                t_wo = k.dma("sync", wot[ep], wo_s[eb], chWO[ep],
                             w=[wot_free[ep]])

                def mm_out(e, eb=eb, ep=ep, t_yl=t_yl, t_wo=t_wo,
                           bfs=tuple(bank_free[0:8])):
                    e.wait_ge(t_yl[0].sem, t_yl[1])
                    e.wait_ge(t_wo[0].sem, t_wo[1])
                    if eb == 0:
                        for bf in bfs:
                            if bf[0]:
                                e.wait_ge(bf[0].sem, bf[1])
                    last = None
                    wo3 = wot[ep].rearrange("p (a n) -> p a n", a=8)
                    for dq in range(8):
                        last = e.matmul(psum[dq], wo3[:, dq, :], yg16r[ep],
                                        start=(eb == 0), stop=(eb == EB - 1))
                    return last
                t_mo = k.op("tensor", mm_out, ch=cPE)
                wot_free[ep] = t_mo
                yg_free[ep] = t_mo
            t_proj = (cPE, cPE.n)
            for db in range(DBL):
                dp = db % 2
                t_x32 = k.dma("sync", x32r[dp],
                              x_fm[seq].rearrange("(b p) t -> p b t", p=128)[:, db, t0:t0 + Tc],
                              chX32[dp], w=[x32_free[dp]])
                t_r1 = k.op("vector", lambda e, db=db, dp=dp, tx=t_x32: (
                    e.wait_ge(tx[0].sem, tx[1]),
                    e.tensor_tensor(out=r13[:, db, :], in0=psum[db],
                                    in1=x32r[dp], op=ALU.add))[1],
                    ch=cDVE, w=[t_proj])
                bank_free[db] = t_r1
                x32_free[dp] = t_r1
            t_r1_all = (cDVE, cDVE.n)
            for db in range(DBL):
                dp = db % 2
                t_sq = k.op("scalar", lambda e, db=db, dp=dp: e.activation(
                    out=r1sqr[dp], in_=r13[:, db, :], func=AF.Square, scale=1.0),
                    ch=cACT, w=[t_r1_all, sq_free[dp]])

                def mm_stat(e, db=db, dp=dp, t_sq=t_sq,
                            bf0=bank_free[4], bf1=bank_free[5]):
                    e.wait_ge(t_sq[0].sem, t_sq[1])
                    if db == 0:
                        e.wait_ge(bf0[0].sem, bf0[1])
                        e.wait_ge(bf1[0].sem, bf1[1])
                    e.matmul(psum[4][0:1, :], ones_col, r13[:, db, :],
                             start=(db == 0), stop=(db == DBL - 1))
                    return e.matmul(psum[5][0:1, :], ones_col, r1sqr[dp],
                                    start=(db == 0), stop=(db == DBL - 1))
                ms = k.op("tensor", mm_stat, ch=cPE)
                bank_w[4] = bank_w[5] = ms
                sq_free[dp] = ms
            t_m = k.op("vector", lambda e: e.tensor_scalar(
                out=SR[0], in0=psum[4][0:1, :], scalar1=1.0 / D,
                scalar2=None, op0=ALU.mult), ch=cDVE, w=[bank_w[4], beta_free[0]])
            t_mq = k.op("vector", lambda e: e.tensor_scalar(
                out=SR[1], in0=psum[5][0:1, :], scalar1=1.0 / D,
                scalar2=None, op0=ALU.mult), ch=cDVE)
            bank_free[4] = t_m
            bank_free[5] = t_mq
            t_v = k.op("vector", lambda e: e.tensor_tensor(
                out=SR[2], in0=SR[0], in1=SR[0], op=ALU.mult),
                ch=cDVE, w=[t_mq, beta_free[1]])
            t_v2 = k.op("vector", lambda e: e.scalar_tensor_tensor(
                out=SR[1], in0=SR[2], scalar=-1.0, in1=SR[1],
                op0=ALU.mult, op1=ALU.add), ch=cDVE, w=[t_v])
            t_v3 = k.op("vector", lambda e: e.tensor_scalar(
                out=SR[1], in0=SR[1], scalar1=LN_EPS, scalar2=None,
                op0=ALU.add), ch=cDVE, w=[t_v2])
            t_l = k.op("scalar", lambda e: e.activation(
                out=SR[2], in_=SR[1], func=AF.Ln, scale=1.0),
                ch=cACT, w=[t_v3])
            t_rs = k.op("scalar", lambda e: e.activation(
                out=SR[2], in_=SR[2], func=AF.Exp, scale=-0.5),
                ch=cACT, w=[t_l])

            def mm_bc(e, t_m=t_m, t_rs=t_rs, bf2=bank_free[6], bf3=bank_free[7]):
                e.wait_ge(t_m[0].sem, t_m[1])
                e.wait_ge(t_rs[0].sem, t_rs[1])
                if bf2[0]:
                    e.wait_ge(bf2[0].sem, bf2[1])
                if bf3[0]:
                    e.wait_ge(bf3[0].sem, bf3[1])
                e.matmul(psum[6], ones_row, SR[0], start=True, stop=True)
                return e.matmul(psum[7], ones_row, SR[2], start=True, stop=True)
            mb = k.op("tensor", mm_bc, ch=cPE)
            bank_w[6] = bank_w[7] = mb
            for db in range(DBL):
                dp = db % 2
                t_a = k.op("vector", lambda e, db=db, dp=dp: e.tensor_tensor(
                    out=ln1r[dp], in0=r13[:, db, :], in1=psum[6], op=ALU.subtract),
                    ch=cDVE, w=[mb, ln1_free[dp]])
                t_b = k.op("vector", lambda e, dp=dp: e.tensor_tensor(
                    out=ln1r[dp], in0=ln1r[dp], in1=psum[7], op=ALU.mult),
                    ch=cDVE, w=[t_a])
                t_rd = k.op("vector", lambda e, dp=dp: e.tensor_reduce(
                    out=redt, in_=ln1r[dp], axis=AX.X, op=ALU.max,
                    apply_absolute_value=True), ch=cDVE, w=[t_b])
                k.op("vector", lambda e: e.tensor_tensor(
                    out=amax_acc, in0=amax_acc, in1=redt, op=ALU.max),
                    ch=cDVE, w=[t_rd])
                ln1_st[c][db] = k.dma("gpsimd", ln1_s[c, db], ln1r[dp], chLN[dp],
                                      w=[t_b])
                ln1_free[dp] = ln1_st[c][db]
            t_ln_last = (cDVE, cDVE.n)
            bank_free[6] = t_ln_last
            bank_free[7] = t_ln_last
            bc_guard[0] = t_ln_last
            bc_guard[1] = t_ln_last
            beta_free[0] = mb
            beta_free[1] = mb
            dt_free[0] = ln1_free[0]
            dt_free[1] = ln1_free[1]
            tmp_free[0] = sq_free[0]
            tmp_free[1] = sq_free[1]

        pend = {}
        ua_free = [NONE_W, NONE_W]
        ua_us_prev = [NONE_W, NONE_W]
        sza_free = [NONE_W, NONE_W]
        h_guard = [NONE_W]
        emit_A_begin(0)
        for eb in range(EB):
            emit_A_iter(0, eb)
        for c in range(NCH):
            if c % CH == 0:
                k.op("vector", lambda e: e.memset(carry, 0.0), ch=cDVE,
                     w=[(cDVE, cDVE.n)])
            dbc_guard_new = emit_B(c)
            if c + 1 < NCH:
                emit_A_begin(c + 1)
            emit_C_chain(c, 0)
            for eb in range(EB):
                if eb + 1 < EB:
                    emit_C_chain(c, eb + 1)
                emit_C_iter(c, eb)
                if c + 1 < NCH:
                    emit_A_iter(c + 1, eb)
            emit_D(c)
            dbc_guard = dbc_guard_new
        h_guard = h_guard[0]

        s1 = k.dma("sync", pm_st, amax_acc, chSC, w=[(cDVE, cDVE.n), s2])
        s2 = k.dma("sync", pmrow, pm_st, chSC, w=[s1])
        t3 = k.op("vector", lambda e: e.tensor_reduce(
            out=scal[0:1, 0:1], in_=pmrow, axis=AX.X, op=ALU.max),
            ch=cDVE, w=[s2])
        s3 = k.dma("sync", cc_in.ap(), scal[0:1, 0:1], chSC, w=[t3, s2])
        t5 = k.op("gpsimd", lambda e: e.collective_compute(
            "AllReduce", ALU.max, ins=[cc_in.ap()], outs=[cc_out.ap()],
            replica_groups=[list(range(8))]), ch=cCC, w=[s3])
        tw1 = k.op("vector", lambda e: e.reciprocal(
            out=scal[0:1, 7:8], in_=scal[0:1, 1:2]), ch=cDVE)
        tw2 = k.op("vector", lambda e: e.tensor_scalar(
            out=scal[0:1, 3:4], in0=scal[0:1, 7:8], scalar1=FP8_MAX,
            scalar2=None, op0=ALU.mult), ch=cDVE, w=[tw1])
        tw3 = k.op("vector", lambda e: e.tensor_scalar(
            out=scal[0:1, 3:4], in0=scal[0:1, 3:4], scalar1=0.5, scalar2=None,
            op0=ALU.mult), ch=cDVE, w=[tw2])
        s5a = k.dma("sync", sc_st, scal, chSC, w=[tw3, s3])
        s6a = k.dma("sync", scalb, sc_st.broadcast_to([128, 8]), chSC, w=[s5a])

        fwp = NONE_W
        for db in range(DBL):
            fwp = k.dma("sync", w1f[:, 0:D], fwt[db * 128:(db + 1) * 128, :],
                        chFW, w=[fwp, s6a, h_guard, (cDVE, cDVE.n)])
            for hf in range(2):
                tb = k.op("vector", lambda e, hf=hf, ta=fwp: (
                    e.wait_ge(ta[0].sem, ta[1]),
                    e.tensor_scalar(out=f8t, in0=w1f[:, hf * Tc:(hf + 1) * Tc],
                                    scalar1=scalb[:, 3:4], scalar2=None,
                                    op0=ALU.mult))[1], ch=cDVE, w=[s6a])
                k.op("vector", lambda e, db=db, hf=hf: e.tensor_copy(
                    out=fw83[:, db, hf * Tc:(hf + 1) * Tc], in_=f8t),
                    ch=cDVE, w=[tb])
        t_fw = (cDVE, cDVE.n)
        f8t_free = NONE_W

        s4 = k.dma("sync", scal[0:1, 0:1], cc_out.ap(), chSC, w=[t5, s3])
        t7 = k.op("vector", lambda e: e.reciprocal(
            out=scal[0:1, 5:7], in_=scal[0:1, 0:2]), ch=cDVE, w=[s4])
        t8 = k.op("vector", lambda e: e.tensor_scalar(
            out=scal[0:1, 2:4], in0=scal[0:1, 5:7], scalar1=FP8_MAX,
            scalar2=None, op0=ALU.mult), ch=cDVE, w=[t7])
        t9 = k.op("vector", lambda e: e.tensor_scalar(
            out=scal[0:1, 2:4], in0=scal[0:1, 2:4], scalar1=0.5, scalar2=None,
            op0=ALU.mult), ch=cDVE, w=[t8])
        t10 = k.op("vector", lambda e: e.tensor_tensor(
            out=scal[0:1, 6:7], in0=scal[0:1, 2:3], in1=scal[0:1, 3:4],
            op=ALU.mult), ch=cDVE, w=[t9])
        t12 = k.op("vector", lambda e: e.reciprocal(
            out=scal[0:1, 4:5], in_=scal[0:1, 6:7]), ch=cDVE, w=[t10])
        s5 = k.dma("sync", sc_st, scal, chSC, w=[t12, s4, t_fw])
        s6 = k.dma("sync", scalb, sc_st.broadcast_to([128, 8]), chSC, w=[s5])

        fxq3b = cbb.bitcast(BF16)[:, DBL * Tc:2 * DBL * Tc].rearrange(
            "p (b n) -> p b n", n=Tc)
        r13b = h_so[:, DBL * Tc:2 * DBL * Tc].rearrange("p (b n) -> p b n", n=Tc)
        mm_last = {}
        r_all = {}
        q_t = {}

        def emit_q(c):
            nonlocal f8t_free
            fxq = fxq3 if c % 2 == 0 else fxq3b
            r13c = r13 if c % 2 == 0 else r13b
            w_mm = mm_last.get(c - 2, NONE_W)
            w_r = r_all.get(c - 2, NONE_W)
            for db in range(DBL):
                dp = db % 2
                ta = k.dma("sync", x32r[dp], ln1_s[c, db], chX32[dp],
                           w=[x32_free[dp], ln1_st[c][db],
                              t_fw if c == 0 and db < 2 else NONE_W])
                tb = k.op("vector", lambda e, dp=dp, ta=ta: (
                    e.wait_ge(ta[0].sem, ta[1]),
                    e.tensor_scalar(out=f8t, in0=x32r[dp], scalar1=scalb[:, 2:3],
                                    scalar2=None, op0=ALU.mult))[1], ch=cDVE,
                    w=[t_fw, f8t_free])
                tc2 = k.op("scalar", lambda e, db=db, fxq=fxq: e.copy(
                    out=fxq[:, db, :], in_=f8t), ch=cACT,
                    w=[tb] + ([w_mm, w_r] if db == 0 else []))
                td = k.op("scalar", lambda e, db=db, dp=dp, r13c=r13c: e.copy(
                    out=r13c[:, db, :], in_=x32r[dp]), ch=cACT, w=[tb])
                f8t_free = tc2
                x32_free[dp] = td
            q_t[c] = ((cACT, cACT.n), (cDVE, cDVE.n))

        emit_q(0)
        for c in range(NCH):
            seq, j = c // CH, c % CH
            t0 = j * Tc
            t_q, t_qd = q_t[c]
            fxq = fxq3 if c % 2 == 0 else fxq3b
            r13c = r13 if c % 2 == 0 else r13b
            for db in range(DBL):
                dp = db % 2
                bo = 6 + dp

                def mm_ffn(e, db=db, bo=bo, t_q=t_q, t_qd=t_qd,
                           bf=bank_free[bo], fxq=fxq):
                    e.wait_ge(t_q[0].sem, t_q[1])
                    e.wait_ge(t_qd[0].sem, t_qd[1])
                    if bf[0]:
                        e.wait_ge(bf[0].sem, bf[1])
                    last = None
                    for dk in range(DBL):
                        last = e.matmul(psum[bo], fw83[:, dk, db * 128:(db + 1) * 128],
                                        fxq[:, dk, :], start=(dk == 0),
                                        stop=(dk == DBL - 1))
                    return last
                mf = k.op("tensor", mm_ffn, ch=cPE)
                bank_w[bo] = mf
                t_e = k.op("vector", lambda e, bo=bo, db=db, dp=dp: e.tensor_scalar(
                    out=r1sqr[dp], in0=psum[bo], scalar1=scalb[:, 4:5],
                    scalar2=fb_sb[:, db:db + 1], op0=ALU.mult, op1=ALU.add),
                    ch=cDVE, w=[mf, sq_free[dp]])
                bank_free[bo] = t_e
                t_r2 = k.op("vector", lambda e, db=db, dp=dp, r13c=r13c:
                            e.tensor_tensor(
                    out=ff3[:, db, :], in0=r13c[:, db, :], in1=r1sqr[dp],
                    op=ALU.add), ch=cDVE, w=[t_e])
                sq_free[dp] = t_r2
                if db == 1 and c + 1 < NCH:
                    mm_last[c] = (cPE, cPE.n)
                    emit_q(c + 1)
            mm_last[c] = (cPE, cPE.n)
            t_ff_all = (cDVE, cDVE.n)
            r_all[c] = t_ff_all
            for db in range(DBL):
                dp = db % 2
                t_sq = k.op("scalar", lambda e, db=db, dp=dp: e.activation(
                    out=r1sqr[dp], in_=ff3[:, db, :], func=AF.Square, scale=1.0),
                    ch=cACT, w=[t_ff_all, sq_free[dp]])

                def mm_stat2(e, db=db, dp=dp, t_sq=t_sq,
                             bf0=bank_free[0], bf1=bank_free[1]):
                    e.wait_ge(t_sq[0].sem, t_sq[1])
                    if db == 0:
                        e.wait_ge(bf0[0].sem, bf0[1])
                        e.wait_ge(bf1[0].sem, bf1[1])
                    e.matmul(psum[0][0:1, :], ones_col, ff3[:, db, :],
                             start=(db == 0), stop=(db == DBL - 1))
                    return e.matmul(psum[1][0:1, :], ones_col, r1sqr[dp],
                                    start=(db == 0), stop=(db == DBL - 1))
                ms = k.op("tensor", mm_stat2, ch=cPE)
                bank_w[0] = bank_w[1] = ms
                sq_free[dp] = ms
            t_m = k.op("vector", lambda e: e.tensor_scalar(
                out=SR[0], in0=psum[0][0:1, :], scalar1=1.0 / D,
                scalar2=None, op0=ALU.mult), ch=cDVE, w=[bank_w[0]])
            t_mq = k.op("vector", lambda e: e.tensor_scalar(
                out=SR[1], in0=psum[1][0:1, :], scalar1=1.0 / D,
                scalar2=None, op0=ALU.mult), ch=cDVE)
            bank_free[0] = t_m
            bank_free[1] = t_mq
            t_v = k.op("vector", lambda e: e.tensor_tensor(
                out=SR[2], in0=SR[0], in1=SR[0], op=ALU.mult),
                ch=cDVE, w=[t_mq])
            t_v2 = k.op("vector", lambda e: e.scalar_tensor_tensor(
                out=SR[1], in0=SR[2], scalar=-1.0, in1=SR[1],
                op0=ALU.mult, op1=ALU.add), ch=cDVE, w=[t_v])
            t_v3 = k.op("vector", lambda e: e.tensor_scalar(
                out=SR[1], in0=SR[1], scalar1=LN_EPS, scalar2=None,
                op0=ALU.add), ch=cDVE, w=[t_v2])
            t_l = k.op("scalar", lambda e: e.activation(
                out=SR[2], in_=SR[1], func=AF.Ln, scale=1.0),
                ch=cACT, w=[t_v3])
            t_rs = k.op("scalar", lambda e: e.activation(
                out=SR[2], in_=SR[2], func=AF.Exp, scale=-0.5),
                ch=cACT, w=[t_l])

            def mm_bc2(e, t_m=t_m, t_rs=t_rs, bf2=bank_free[2], bf3=bank_free[3]):
                e.wait_ge(t_m[0].sem, t_m[1])
                e.wait_ge(t_rs[0].sem, t_rs[1])
                e.wait_ge(bf2[0].sem, bf2[1])
                e.wait_ge(bf3[0].sem, bf3[1])
                e.matmul(psum[2], ones_row, SR[0], start=True, stop=True)
                return e.matmul(psum[3], ones_row, SR[2], start=True, stop=True)
            mb = k.op("tensor", mm_bc2, ch=cPE)
            bank_w[2] = bank_w[3] = mb
            for db in range(DBL):
                dp = db % 2
                t_a = k.op("vector", lambda e, db=db, dp=dp: e.tensor_tensor(
                    out=ln1r[dp], in0=ff3[:, db, :], in1=psum[2], op=ALU.subtract),
                    ch=cDVE, w=[mb, ln1_free[dp]])
                t_b = k.op("vector", lambda e, dp=dp: e.tensor_tensor(
                    out=ln1r[dp], in0=ln1r[dp], in1=psum[3], op=ALU.mult),
                    ch=cDVE, w=[t_a])
                ln1_free[dp] = k.dma(
                    "gpsimd",
                    out_fm[seq].rearrange("(b p) t -> p b t", p=128)[:, db, t0:t0 + Tc],
                    ln1r[dp], chST[dp], w=[t_b])
            t_last = (cDVE, cDVE.n)
            bank_free[2] = t_last
            bank_free[3] = t_last

        k.ops["sync"].append(lambda e, a=chST[0].n, b=chST[1].n: (
            e.wait_ge(chST[0].sem, a), e.wait_ge(chST[1].sem, b)))

        with nc.Block() as block:
            @block.sync
            def _(e):
                for f in k.ops["sync"]:
                    f(e)

            @block.scalar
            def _(e):
                for f in k.ops["scalar"]:
                    f(e)

            @block.vector
            def _(e):
                for f in k.ops["vector"]:
                    f(e)

            @block.tensor
            def _(e):
                for f in k.ops["tensor"]:
                    f(e)

            @block.gpsimd
            def _(e):
                for f in k.ops["gpsimd"]:
                    f(e)

    return nc


def kernel(L0_attn_x, L0_attn_W_in, L0_attn_conv_w, L0_attn_conv_b,
           L0_attn_W_xproj, L0_attn_W_dt, L0_attn_b_dt, L0_attn_A_log,
           L0_attn_D_skip, L0_attn_W_out, L0_ffn_weight, L0_ffn_bias):
    x = np.asarray(L0_attn_x, dtype=np.float32)
    w_in = np.asarray(L0_attn_W_in, dtype=np.float32)
    conv_w = np.asarray(L0_attn_conv_w, dtype=np.float32)
    conv_b = np.asarray(L0_attn_conv_b, dtype=np.float32)
    w_xproj = np.asarray(L0_attn_W_xproj, dtype=np.float32)
    w_dt = np.asarray(L0_attn_W_dt, dtype=np.float32)
    b_dt = np.asarray(L0_attn_b_dt, dtype=np.float32)
    a_log = np.asarray(L0_attn_A_log, dtype=np.float32)
    d_skip = np.asarray(L0_attn_D_skip, dtype=np.float32)
    w_out = np.asarray(L0_attn_W_out, dtype=np.float32)
    fw = np.asarray(L0_ffn_weight, dtype=np.float32)
    fbias = np.asarray(L0_ffn_bias, dtype=np.float32)

    w_inT = np.ascontiguousarray(w_in.T)
    shared = {
        "w1t": np.ascontiguousarray(w_inT[:, :E]),
        "wzt": np.ascontiguousarray(w_inT[:, E:]),
        "cwrows": np.ascontiguousarray(conv_w.T),
        "cb": np.ascontiguousarray(conv_b.reshape(EB, 128).T),
        "wxt": np.ascontiguousarray(w_xproj.T),
        "wdtt": np.ascontiguousarray(w_dt.T),
        "bdt": np.ascontiguousarray(b_dt.reshape(EB, 128).T),
        "alog": np.ascontiguousarray(
            a_log.reshape(EB, 128, S).transpose(1, 0, 2).reshape(128, EB * S)),
        "dskip": np.ascontiguousarray(d_skip.reshape(EB, 128).T),
        "woutt": np.ascontiguousarray(w_out.T),
        "fwt": np.ascontiguousarray(fw.T),
        "fb": np.ascontiguousarray(fbias.reshape(DBL, 128).T),
    }
    in_maps = []
    for c in range(8):
        xc = np.ascontiguousarray(x[2 * c:2 * c + 2].transpose(0, 2, 1))
        m = {"x_fm": xc}
        m.update(shared)
        in_maps.append(m)

    nc = build_nc()
    res = run_bass_kernel_spmd(nc, in_maps, core_ids=list(range(8)))
    out = np.empty((B_TOT, T, D), dtype=np.float32)
    for c in range(8):
        o = np.asarray(res.results[c]["out_fm"])
        out[2 * c:2 * c + 2] = o.transpose(0, 2, 1)
    return out
